# version 08
# baseline (speedup 1.0000x reference)
import numpy as np
import concourse.bass as bass
import concourse.mybir as mybir
from concourse.bass_utils import run_bass_kernel_spmd
from contextlib import ExitStack

F32 = mybir.dt.float32
BF16 = mybir.dt.bfloat16
U8 = mybir.dt.uint8
AF = mybir.ActivationFunctionType
ALU = mybir.AluOpType
AX = mybir.AxisListType

EPS = 1e-5


import os as _os


class CFG:
    p_ln = _os.environ.get("P_LN", "1") == "1"
    t_src_idb = _os.environ.get("T_SRC", "x") == "idb"
    t_evac = _os.environ.get("T_EVAC", "1") == "1"
    k_now = _os.environ.get("K_NOW", "0") == "1"
    k_dbg = _os.environ.get("K_DBG", "")
    p_tr = _os.environ.get("P_TR", "1") == "1"
    p_proj = _os.environ.get("P_PROJ", "1") == "1"
    p_v = _os.environ.get("P_V", "1") == "1"
    p_sc = _os.environ.get("P_SC", "1") == "1"
    p_dl = _os.environ.get("P_DL", "1") == "1"
    run_pre = True
    run_own = True
    run_gate = True
    run_ffn = True
    D = 1024
    S = 128
    G = 256
    E = 8
    H = 2048
    T_own = 2048
    T_pre = 14336
    GRP = 512
    CH = 128
    TSUB = 512
    n_cores = 8


class PeTrack:

    def __init__(self, eng):
        object.__setattr__(self, "_e", eng)
        object.__setattr__(self, "last", None)

    def matmul(self, *a, **k):
        r = self._e.matmul(*a, **k)
        object.__setattr__(self, "last", r)
        return r

    def __getattr__(self, n):
        return getattr(self._e, n)


class Bld:
    def __init__(self, nc, stack):
        self.nc = nc
        self.stack = stack
        self.sem = {}
        self.cnt = {}
        self.last_wait = {}
        self.engs = {}

    def new_sem(self, name):
        self.sem[name] = self.stack.enter_context(self.nc.semaphore(name))
        self.cnt[name] = 0

    def inc(self, instr, name, k=1):
        self.cnt[name] += k
        if name in self.engs:
            instr = self.engs[name].drain()
        instr.then_inc(self.sem[name], k)
        return (name, self.cnt[name])

    def ev(self, name):
        return (name, self.cnt[name])

    DMA_SEMS = ("sx", "sw", "sa", "so", "sw2", "sw3", "sw4")

    def wait(self, eng, evt):
        if evt is None:
            return
        if isinstance(evt, (list, tuple)) and evt and isinstance(evt[0], tuple):
            for e in evt:
                self.wait(eng, e)
            return
        name, val = evt
        if name in self.DMA_SEMS:
            val = self.cnt[name]
        if val <= 0:
            return
        assert val <= self.cnt[name], f"forward wait {name} {val} > {self.cnt[name]}"
        key = (id(eng), name)
        if self.last_wait.get(key, -1) >= val:
            return
        eng.wait_ge(self.sem[name], val)
        self.last_wait[key] = val


def build_program(cfg):
    nc = bass.Bass("TRN2", target_bir_lowering=False, detect_race_conditions=False)
    D, S, G, E, H = cfg.D, cfg.S, cfg.G, cfg.E, cfg.H
    ND, NG, NH = D // 128, G // 128, H // 128
    GRP, CH, TSUB = cfg.GRP, cfg.CH, cfg.TSUB
    NCH = GRP // CH
    n_pre_grp = cfg.T_pre // GRP
    n_own_grp = cfg.T_own // GRP
    n_own_ch = cfg.T_own // CH
    n_tsub = cfg.T_own // TSUB
    NPRECH = max(cfg.T_pre // CH, 1)
    T_own = cfg.T_own

    dt_in = {}

    def din(name, shape, dtype=F32):
        dt_in[name] = nc.dram_tensor(name, shape, dtype, kind="ExternalInput").ap()
        return dt_in[name]

    if n_pre_grp:
        din("x_pre", [cfg.T_pre, D])
    din("pmask", [128, NPRECH])
    din("x_own", [T_own, D])
    for nm, shp in [("Wq", [D, S]), ("Wk", [D, S]), ("Wv", [D, S]), ("Wg", [D, S]),
                    ("Wo", [S, D]), ("wg1", [D, G]), ("wg2", [G, E]),
                    ("w_in", [E, D, H]), ("w_out", [E, H, D]),
                    ("bq", [S]), ("bk", [S]), ("bv", [S]), ("bg", [S]),
                    ("bo", [D]), ("gamma", [D]), ("beta", [D]), ("bg1", [G])]:
        din(nm, shp)
    din("gammaB", [128, D])
    din("betaB", [128, D])
    din("bg1C", [128, G // 128])
    din("onehot", [E, E * 128])
    din("ident", [128, 128])
    din("m_bd8", [128, 128])
    din("m_od8", [128, 128])
    din("m_od16", [128, 128])
    din("m_od32", [128, 128])
    din("submask", [128, 2])
    din("cconst", [128, 4])
    din("onesrow", [1, 128])
    din("zer128", [128, 128])
    y_out = nc.dram_tensor("y", [D, T_own], F32, kind="ExternalOutput").ap()
    dbg_out = nc.dram_tensor("dbg", [128, 2048], F32, kind="ExternalOutput").ap()

    with ExitStack() as stack:
        b = Bld(nc, stack)
        al = stack.enter_context

        def sb(name, shape, dtype):
            return al(nc.sbuf_tensor(name, shape, dtype))

        for s_ in ["sx", "sw", "so", "sa", "sw2", "sw3", "sw4", "pe", "dv", "sc", "gp"]:
            b.new_sem(s_)
        pe, dv, sc, gp, sy = PeTrack(nc.tensor), nc.vector, nc.scalar, nc.gpsimd, nc.sync
        b.engs = {"pe": nc.tensor, "dv": nc.vector, "sc": nc.scalar, "gp": nc.gpsimd}

        id_f = sb("id_f", [128, 128], F32)
        id_b = sb("id_b", [128, 128], BF16)
        mbd = sb("mbd", [128, 128], BF16)
        mod8 = sb("mod8", [128, 128], BF16)
        mod_ = sb("mod_", [128, 128], BF16)
        mod32 = sb("mod32", [128, 128], BF16)
        smsk = sb("smsk", [128, 2], F32)
        ones_b = sb("ones_b", [1, 128], BF16)
        ones_f = sb("ones_f", [1, 128], F32)
        zer_b = sb("zer_b", [128, 1], BF16)
        gb_bc = sb("gb_bc", [128, D], F32)
        bb_bc = sb("bb_bc", [128, D], F32)
        xn2T = sb("xn2T", [128, ND * T_own], BF16)
        gatesT = sb("gatesT", [E, T_own], BF16)
        oneh = sb("oneh", [E, E * 128], BF16)
        win = sb("win", [128, ND * H], BF16)
        wout = sb("wout", [128, NH * D], BF16)
        dbg_sb = sb("dbg_sb", [128, 2048 if cfg.k_dbg else 8], F32)
        csp = sb("csp", [128, 32], F32)

        b.inc(sy.dma_start(id_f[:, :], dt_in["ident"]), "sx", 16)
        b.inc(gp.dma_start(mbd[:, :], dt_in["m_bd8"]), "sw", 16)
        b.inc(gp.dma_start(mod8[:, :], dt_in["m_od8"]), "sw", 16)
        b.inc(gp.dma_start(mod_[:, :], dt_in["m_od16"]), "sw", 16)
        b.inc(gp.dma_start(mod32[:, :], dt_in["m_od32"]), "sw", 16)
        b.inc(sy.dma_start(smsk[:, :], dt_in["submask"]), "sx", 16)
        b.inc(sy.dma_start(gb_bc[:, :], dt_in["gammaB"]), "sx", 16)
        e_small0 = b.inc(sy.dma_start(bb_bc[:, :], dt_in["betaB"]), "sx", 16)
        b.inc(gp.dma_start(oneh[:, :], dt_in["onehot"]), "sw", 16)
        e_idb = b.inc(gp.dma_start(id_b[:, :], dt_in["ident"]), "sw", 16)
        if not cfg.k_now:
            e_win = b.inc(gp.dma_start(win[:, :].rearrange("p (c h) -> p c h", c=ND),
                                       dt_in["w_in"][0].rearrange("(c p) h -> p c h", p=128)),
                          "sw", 16)
            e_wout = b.inc(gp.dma_start(wout[:, :].rearrange("p (c d) -> p c d", c=NH),
                                        dt_in["w_out"][0].rearrange("(c p) d -> p c d", p=128)),
                           "sw", 16)
        else:
            e_win = e_wout = None
        cpool = sb("cpool", [128, 4], F32)
        b.inc(sy.dma_start(cpool[:, :], dt_in["cconst"]), "sx", 16)
        b.inc(sy.dma_start(ones_f[:, :], dt_in["onesrow"]), "sx", 16)
        b.inc(gp.dma_start(ones_b[:, :], dt_in["onesrow"]), "sw", 16)
        nc.const_aps.register(F32, EPS, cpool[:, 0:1])

        ab = ExitStack()
        al2 = ab.enter_context

        def sb2(name, shape, dtype):
            return al2(nc.sbuf_tensor(name, shape, dtype))

        def ps2(name, shape, dtype):
            return al2(nc.psum_tensor(name, shape, dtype))

        wq = sb2("wq", [128, ND * S], BF16)
        wk = sb2("wk", [128, ND * S], BF16)
        wv = sb2("wv", [128, ND * S], BF16)
        wgt = sb2("wgt", [128, ND * S], BF16)
        wo = sb2("wo", [S, D], BF16)
        bqc = sb2("bqc", [128, 1], F32)
        bkc = sb2("bkc", [128, 1], F32)
        bgc = sb2("bgc", [128, 1], F32)
        bvrb = sb2("bvrb", [1, S], BF16)
        borb = sb2("borb", [1, D], BF16)
        pmk = sb2("pmk", [128, NPRECH], F32)
        S_f = sb2("S_f", [128, 128], F32)
        S_b = sb2("S_b", [128, 128], BF16)
        NXB = 4
        xc = [sb2(f"xc{i}", [128, D], F32) for i in range(NXB)]
        xnf = sb2("xnf", [128, D], F32)
        stg = [sb2(f"stg{i}", [128, D], F32) for i in range(2)]
        xnbf = sb2("xnbf", [128, D], BF16)
        xnT = sb2("xnT", [128, ND * GRP], BF16)
        qb = sb2("qb", [128, GRP], BF16)
        kb = sb2("kb", [128, GRP], BF16)
        spb = sb2("spb", [128, GRP], F32)
        escr = sb2("escr", [128, GRP], F32)
        s_buf = sb2("s_buf", [128, GRP + 1], F32)
        Eoff = sb2("Eoff", [128, GRP], F32)
        qhat = sb2("qhat", [128, GRP], BF16)
        q8 = sb2("q8", [128, GRP], BF16)
        k8 = sb2("k8", [128, GRP], BF16)
        q16 = sb2("q16", [128, GRP], BF16)
        k16 = sb2("k16", [128, GRP], BF16)
        q32 = sb2("q32", [128, GRP], BF16)
        k32 = sb2("k32", [128, GRP], BF16)
        escr2 = sb2("escr2", [128, GRP], F32)
        khT = sb2("khT", [128, GRP], BF16)
        kh_t = sb2("kh_t", [128, GRP], BF16)
        v_b = sb2("v_b", [128, GRP], BF16)
        A1m = sb2("A1m", [128, 128], BF16)
        A2m = sb2("A2m", [128, 128], BF16)
        A3m = sb2("A3m", [128, 128], BF16)
        A4m = sb2("A4m", [128, 128], BF16)
        q64 = sb2("q64", [128, GRP], BF16)
        k64 = sb2("k64", [128, GRP], BF16)
        x1hi = sb2("x1hi", [128, D], BF16)
        x1lo = sb2("x1lo", [128, D], BF16)
        khm = sb2("khm", [128, 128], BF16)
        outb = sb2("outb", [128, 128], BF16)
        ybuf = sb2("ybuf", [128, D], F32)
        ddump = sb2("ddump", [128, D // 2], F32)
        sdump = ddump
        lnsets = []
        for li in range(2):
            lnsets.append({nm: sb2(f"{nm}_{li}", [128, 1], F32)
                           for nm in ("acc_sa0", "acc_sa1", "acc_qa0",
                                      "acc_qa1", "msqt", "ts_t", "tq_t",
                                      "m2t", "nm2t", "t6t")})
        mu_cd = [sb2(f"mu_cd{i}", [128, 1], F32) for i in range(4)]
        rs_cd = [sb2(f"rs_cd{i}", [128, 1], F32) for i in range(4)]
        mu_sd = sb2("mu_sd", [128, 1], F32)
        rs_sd = sb2("rs_sd", [128, 1], F32)
        t4c = csp[:, 18:19]
        t5c = csp[:, 19:20]
        t6c = csp[:, 20:21]
        ts_c = csp[:, 25:26]
        tq_c = csp[:, 26:27]
        m2_c = csp[:, 27:28]
        nm2_c = csp[:, 28:29]
        decc = csp[:, 21:22]
        mu_s = mu_sd[:, 0:1]
        rs_s = rs_sd[:, 0:1]
        mu_c = [mu_cd[i][:, 0:1] for i in range(NCH)]
        rs_c = [rs_cd[i][:, 0:1] for i in range(NCH)]
        ptf0 = ps2("ptf0", [128, 512], F32)
        ptf1 = ps2("ptf1", [128, 512], F32)
        pp0 = ps2("pp0", [128, 512], F32)
        pp1 = ps2("pp1", [128, 512], F32)
        pA = ps2("pA", [128, 512], F32)
        pdel = ps2("pdel", [128, 512], F32)
        pout = ps2("pout", [128, 512], F32)
        pps = [pp0, pp1]

        b.inc(sy.dma_start(bqc[:, :], dt_in["bq"].unsqueeze(-1)), "sx", 16)
        b.inc(sy.dma_start(bkc[:, :], dt_in["bk"].unsqueeze(-1)), "sx", 16)
        b.inc(sy.dma_start(bgc[:, :], dt_in["bg"].unsqueeze(-1)), "sx", 16)
        b.inc(sy.dma_start(S_f[:, :], dt_in["zer128"]), "sx", 16)
        e_small = b.inc(sy.dma_start(pmk[:, :], dt_in["pmask"]), "sx", 16)
        for w_sb, nm in [(wq, "Wq"), (wk, "Wk"), (wv, "Wv"), (wgt, "Wg")]:
            b.inc(gp.dma_start(w_sb[:, :].rearrange("p (c s) -> p c s", c=ND),
                               dt_in[nm].rearrange("(c p) s -> p c s", p=128)), "sw", 16)
        b.inc(gp.dma_start(wo[:, :], dt_in["Wo"]), "sw", 16)
        b.inc(gp.dma_start(S_b[:, :], dt_in["zer128"]), "sw", 16)
        b.inc(gp.dma_start(bvrb[:, :], dt_in["bv"].unsqueeze(0)), "sw", 16)
        e_smallw = b.inc(gp.dma_start(borb[:, :], dt_in["bo"].unsqueeze(0)), "sw", 16)

        b.wait(sc, e_small)
        b.wait(sc, e_smallw)
        sc.activation(s_buf[:, 0:1], cpool[:, 1:2], AF.Identity)
        e_dvj = b.inc(sc.activation(csp[:, 23:24], bgc[:, 0:1], AF.Identity,
                                    scale=-1.0), "sc")

        e_bc_done = e_dvj

        dbg_done = [False]
        ln_call = [0]
        ln_trigger = int(_os.environ.get("K_LNTRIG", "1"))

        def dump_dbg(name, src_ap, evt=None):
            if cfg.k_dbg != name or dbg_done[0]:
                return
            dbg_done[0] = True
            b.wait(dv, evt)
            n = min(src_ap.shape[-1], 2048)
            e_d = b.inc(dv.tensor_copy(dbg_sb[0:src_ap.shape[0], 0:n], src_ap[:, 0:n]), "dv")
            b.wait(sy, e_d)
            b.inc(sy.dma_start(dbg_out, dbg_sb[:, :]), "so", 16)

        st = {
            "pp_free": [e_bc_done, e_bc_done],
            "pp_i": 0,
            "trbf_free": e_bc_done,
            "pA_free": None, "pdel_free": None, "pout_free": None,
            "S_bf_ready": e_smallw,
            "xbuf_free": [None] * NXB,
            "stg_free": [None, None],
            "xch": 0,
            "ln_guard": [None, None],
        }

        def load_x_chunk(src_ap):
            i = st["xch"] % NXB
            b.wait(sy, st["xbuf_free"][i])
            ev = b.inc(sy.dma_start(xc[i][:, :], src_ap), "sx", 16)
            st["xch"] += 1
            return i, ev

        def group_barrier():
            epe, edv, esc = b.ev("pe"), b.ev("dv"), b.ev("sc")
            b.wait(pe, edv)
            b.wait(pe, esc)
            b.wait(dv, esc)
            b.wait(dv, epe)
            b.wait(sc, edv)
            b.wait(sc, epe)

        def full_barrier():
            group_barrier()
            for eng in (gp, sy):
                for s_ in ("pe", "dv", "sc"):
                    b.wait(eng, b.ev(s_))

        def pp_next():
            i = st["pp_i"]
            st["pp_i"] ^= 1
            return pps[i], i

        def ln_stats(xap, mu_ap, rstd_ap, e_src=None):
            par = ln_call[0] % 2
            L = lnsets[par]
            acc_sa0, acc_sa1 = L["acc_sa0"], L["acc_sa1"]
            acc_qa0, acc_qa1 = L["acc_qa0"], L["acc_qa1"]
            msqt, ts_t, tq_t = L["msqt"], L["ts_t"], L["tq_t"]
            m2t, nm2t, t6t = L["m2t"], L["nm2t"], L["t6t"]
            b.wait(sc, st["ln_guard"][par])
            b.wait(sc, e_src)
            HD = D // 2
            sc.activation(ddump[:, 0:HD], xap[:, 0:HD], AF.Identity,
                          accum_out=acc_sa0[:, 0:1])
            sc.activation(ddump[:, 0:HD], xap[:, HD:D], AF.Identity,
                          accum_out=acc_sa1[:, 0:1])
            sc.activation(sdump[:, 0:HD], xap[:, 0:HD], AF.Square,
                          accum_out=acc_qa0[:, 0:1])
            sc.activation(sdump[:, 0:HD], xap[:, HD:D], AF.Square,
                          accum_out=acc_qa1[:, 0:1])
            e_sq = b.inc(nc.scalar.drain(), "sc")
            b.wait(gp, st["ln_guard"][par])
            b.wait(gp, e_sq)
            gp.tensor_add(ts_t[:, 0:1], acc_sa0[:, 0:1], acc_sa1[:, 0:1])
            gp.tensor_scalar_mul(mu_ap, ts_t[:, 0:1], 1.0 / D)
            gp.tensor_add(tq_t[:, 0:1], acc_qa0[:, 0:1], acc_qa1[:, 0:1])
            gp.tensor_scalar_mul(msqt[:, 0:1], tq_t[:, 0:1], 1.0 / D)
            gp.tensor_mul(m2t[:, 0:1], mu_ap, mu_ap)
            e_v = b.inc(gp.tensor_sub(nm2t[:, 0:1], msqt[:, 0:1],
                                      m2t[:, 0:1]), "gp")
            b.wait(sc, e_v)
            sc.activation(ddump[:, 0:1], nm2t[:, 0:1], AF.Ln,
                          bias=EPS, accum_out=t6t[:, 0:1])
            e_r = b.inc(sc.activation(ddump[:, 0:1], t6t[:, 0:1], AF.Exp,
                                      scale=-0.5, accum_out=rstd_ap), "sc")
            st["ln_guard"][par] = e_v
            ln_call[0] += 1
            if cfg.k_dbg == "lnx" and ln_call[0] == ln_trigger and not dbg_done[0]:
                dbg_done[0] = True
                b.wait(dv, e_r)
                dv.tensor_copy(dbg_sb[:, 0:1024], xap[:, 0:1024])
                dv.tensor_copy(dbg_sb[:, 1024:1025], acc_sa0[:, 0:1])
                dv.tensor_copy(dbg_sb[:, 1025:1026], acc_sa1[:, 0:1])
                dv.tensor_copy(dbg_sb[:, 1026:1027], acc_qa0[:, 0:1])
                dv.tensor_copy(dbg_sb[:, 1027:1028], acc_qa1[:, 0:1])
                dv.tensor_copy(dbg_sb[:, 1028:1029], mu_ap)
                dv.tensor_copy(dbg_sb[:, 1029:1030], msqt[:, 0:1])
                dv.tensor_copy(dbg_sb[:, 1030:1031], nm2t[:, 0:1])
                dv.tensor_copy(dbg_sb[:, 1032:1033], ts_t[:, 0:1])
                dv.tensor_copy(dbg_sb[:, 1033:1034], tq_t[:, 0:1])
                dv.tensor_copy(dbg_sb[:, 1034:1035], m2t[:, 0:1])
                dv.tensor_copy(dbg_sb[:, 1035:1036], t6t[:, 0:1])
                e_d = b.inc(dv.tensor_copy(dbg_sb[:, 1031:1032], rstd_ap), "dv")
                b.wait(sy, e_d)
                b.inc(sy.dma_start(dbg_out, dbg_sb[:, :]), "so", 16)
            return e_r

        def proj(w_sb, dst_bf, bias_col, e_ready):
            p, pi = pp_next()
            b.wait(pe, st["pp_free"][pi])
            b.wait(pe, e_ready)
            for d in range(ND):
                pe.matmul(p[:, 0:GRP], w_sb[:, d * S:(d + 1) * S],
                          xnT[:, d * GRP:(d + 1) * GRP],
                          start=(d == 0), stop=(d == ND - 1))
            e_mm = b.inc(pe.last, "pe")
            b.wait(sc, e_mm)
            e_ev = b.inc(sc.activation(dst_bf[:, 0:GRP], p[:, 0:GRP],
                                       AF.Identity, bias=bias_col), "sc")
            st["pp_free"][pi] = e_ev
            return e_ev


        def tpsum(d):
            return (ptf0 if d % 2 == 0 else ptf1), (d // 2) * 128

        def khat_to_token_major(e_khT):
            b.wait(pe, e_khT)
            b.wait(pe, st["trbf_free"])
            for ci in range(NCH):
                pe.matmul(tpsum(ci)[0][:, tpsum(ci)[1]:tpsum(ci)[1] + 128],
                          khT[:, ci * 128:(ci + 1) * 128], id_b[:, :],
                          start=True, stop=True)
            e_tp = b.inc(pe.last, "pe")
            b.wait(dv, e_tp)
            for ci in range(NCH - 1):
                dv.tensor_copy(kh_t[:, ci * 128:(ci + 1) * 128],
                               tpsum(ci)[0][:, tpsum(ci)[1]:tpsum(ci)[1] + 128])
            e_c = b.inc(dv.tensor_copy(kh_t[:, (NCH - 1) * 128:NCH * 128],
                                       tpsum(NCH - 1)[0][:, tpsum(NCH - 1)[1]:tpsum(NCH - 1)[1] + 128]), "dv")
            st["trbf_free"] = e_c
            return e_c

        def do_group(own, src, first_tok, gidx_own=None):
            group_barrier()
            ch_bufs = [load_x_chunk(src[first_tok + ci * CH: first_tok + ci * CH + CH, :])
                       for ci in range(NCH)]
            e_tr_all = None
            for ci in range(NCH):
                bi, e_x = ch_bufs[ci]
                b.wait(dv, e_x)
                b.wait(sc, e_x)
                if cfg.p_ln:
                    e_r = ln_stats(xc[bi][:, :], mu_c[ci],
                                   rs_c[ci])
                    b.wait(dv, e_r)
                    e_ap = b.inc(dv.tensor_scalar(
                        xnbf[:, :], xc[bi][:, :], mu_c[ci],
                        rs_c[ci],
                        op0=ALU.subtract, op1=ALU.mult), "dv")
                else:
                    e_r = None
                    e_ap = b.inc(dv.tensor_copy(xnbf[:, :], xc[bi][:, :]), "dv")
                if not own:
                    st["xbuf_free"][bi] = [e_ap] + ([e_r] if e_r else [])
                if own and gidx_own == 0 and ci == 0:
                    dump_dbg("xnbf", xnbf[:, :], e_ap)
                    dump_dbg("xc", xc[bi][:, :], e_ap)
                    dump_dbg("murst", csp[:, 0:16], e_ap)
                    dump_dbg("idb", id_b[:, :], e_ap)
                if (not own) and ci == 0:
                    dump_dbg("pxnbf", xnbf[:, :], e_ap)
                    dump_dbg("pxc", xc[bi][:, :], e_ap)
                if cfg.p_tr:
                    b.wait(pe, e_ap)
                    b.wait(pe, st["trbf_free"])
                    for d in range(ND):
                        src = id_b[:, :] if cfg.t_src_idb else xnbf[:, d * 128:(d + 1) * 128]
                        pe.matmul(tpsum(d)[0][:, tpsum(d)[1]:tpsum(d)[1] + 128],
                                  src, id_b[:, :],
                                  start=True, stop=True)
                    e_tp = b.inc(pe.last, "pe")
                    b.wait(dv, e_tp)
                    if cfg.t_evac:
                        for d in range(ND - 1):
                            dv.tensor_copy(xnT[:, d * GRP + ci * 128: d * GRP + ci * 128 + 128],
                                           tpsum(d)[0][:, tpsum(d)[1]:tpsum(d)[1] + 128])
                        d = ND - 1
                        e_tc = b.inc(dv.tensor_copy(
                            xnT[:, d * GRP + ci * 128: d * GRP + ci * 128 + 128],
                            tpsum(d)[0][:, tpsum(d)[1]:tpsum(d)[1] + 128]), "dv")
                    else:
                        e_tc = b.inc(dv.nop(), "dv")
                    if own and gidx_own == 0 and ci == 0 and cfg.k_dbg == "ptf":
                        b.wait(dv, e_tp)
                        e_d2 = b.inc(dv.tensor_copy(dbg_sb[:, 0:512], ptf0[:, 0:512]), "dv")
                        e_d2 = b.inc(dv.tensor_copy(dbg_sb[:, 512:1024], ptf1[:, 0:512]), "dv")
                        dbg_done[0] = True
                        b.wait(sy, e_d2)
                        b.inc(sy.dma_start(dbg_out, dbg_sb[:, :]), "so", 16)
                        e_tc = e_d2
                    st["trbf_free"] = e_tc
                    e_tr_all = e_tc
                else:
                    e_tr_all = e_ap

            if cfg.p_proj:
                e_k = proj(wk, kb, bkc[:, 0:1], e_tr_all)
                e_q = proj(wq, qb, bqc[:, 0:1], e_tr_all) if own else None
                p, pi = pp_next()
                b.wait(pe, st["pp_free"][pi])
                b.wait(pe, e_tr_all)
                for d in range(ND):
                    pe.matmul(p[:, 0:GRP], wgt[:, d * S:(d + 1) * S],
                              xnT[:, d * GRP:(d + 1) * GRP],
                              start=(d == 0), stop=(d == ND - 1))
                e_gmm = b.inc(pe.last, "pe")
                b.wait(sc, e_gmm)
                sc.activation(escr[:, 0:GRP], p[:, 0:GRP], AF.Exp, scale=-1.0,
                              bias=csp[:, 23:24])
                e_sp = b.inc(sc.activation(spb[:, 0:GRP], escr[:, 0:GRP],
                                           AF.Ln, bias=1.0), "sc")
                st["pp_free"][pi] = e_sp
            else:
                dv.memset(kb[:, 0:GRP], 0.01)
                dv.memset(qb[:, 0:GRP], 0.01)
                e_sp = e_k = e_q = b.inc(dv.memset(spb[:, 0:GRP], 0.01), "dv")

            e_vev = None
            if cfg.p_v:
                for ci in range(NCH):
                    p, pi = pp_next()
                    b.wait(pe, st["pp_free"][pi])
                    for d in range(ND):
                        pe.matmul(p[:, 0:128],
                                  xnT[:, d * GRP + ci * 128: d * GRP + ci * 128 + 128],
                                  wv[:, d * S:(d + 1) * S], start=(d == 0), stop=False)
                    pe.matmul(p[:, 0:128], ones_b[:, :], bvrb[:, :], start=False, stop=True)
                    e_mm = b.inc(pe.last, "pe")
                    b.wait(dv, e_mm)
                    e_vev = b.inc(dv.tensor_copy(v_b[:, ci * 128:(ci + 1) * 128],
                                                 p[:, 0:128]), "dv")
                    st["pp_free"][pi] = e_vev
            else:
                e_vev = b.inc(dv.memset(v_b[:, 0:GRP], 0.01), "dv")

            if cfg.p_sc:
                b.wait(dv, e_sp)
                e_s = b.inc(dv.tensor_tensor_scan(s_buf[:, 1:GRP + 1], spb[:, 0:GRP],
                                                  spb[:, 0:GRP], 0.0,
                                                  op0=ALU.add, op1=ALU.bypass), "dv")
            else:
                e_s = b.inc(dv.memset(s_buf[:, 1:GRP + 1], 0.02), "dv")

            if not own:
                dump_dbg("pxnT", xnT[:, 0:2048], e_tr_all)
                dump_dbg("pspb", spb[:, 0:GRP], e_sp)
                dump_dbg("psbuf", s_buf[:, :], e_s)
                dump_dbg("pkb", kb[:, 0:GRP], e_k)
                dump_dbg("pvb", v_b[:, 0:GRP], e_vev)
                if cfg.p_sc:
                    b.wait(dv, e_s)
                    e_sub = b.inc(dv.tensor_scalar(escr[:, 0:GRP], s_buf[:, 1:GRP + 1],
                                                   s_buf[:, GRP:GRP + 1], None,
                                                   op0=ALU.subtract), "dv")
                    b.wait(sc, e_sub)
                    e_eh = b.inc(sc.activation(escr[:, 0:GRP], escr[:, 0:GRP], AF.Exp), "sc")
                    b.wait(dv, e_eh)
                    b.wait(dv, e_k)
                    e_kh = b.inc(dv.tensor_tensor(khT[:, 0:GRP], kb[:, 0:GRP],
                                                  escr[:, 0:GRP], op=ALU.mult), "dv")
                    e_khc = khat_to_token_major(e_kh)
                    ch0 = first_tok // CH
                    e_msk = e_khc
                    for ci in range(NCH):
                        e_msk = b.inc(dv.tensor_scalar(kh_t[:, ci * 128:(ci + 1) * 128],
                                                       kh_t[:, ci * 128:(ci + 1) * 128],
                                                       pmk[:, ch0 + ci:ch0 + ci + 1], None,
                                                       op0=ALU.mult), "dv")
                else:
                    e_msk = b.inc(dv.memset(kh_t[:, 0:GRP], 0.01), "dv")
                if cfg.p_dl:
                    b.wait(pe, e_msk)
                    b.wait(pe, e_vev)
                    b.wait(pe, st["pdel_free"])
                    for ci in range(NCH):
                        pe.matmul(pdel[:, 0:128], kh_t[:, ci * 128:(ci + 1) * 128],
                                  v_b[:, ci * 128:(ci + 1) * 128],
                                  start=(ci == 0), stop=(ci == NCH - 1))
                    e_dl = b.inc(pe.last, "pe")
                    b.wait(sc, e_s)
                    e_dc = b.inc(sc.activation(decc, s_buf[:, GRP:GRP + 1],
                                               AF.Exp, scale=-1.0), "sc")
                    b.wait(dv, e_dl)
                    b.wait(dv, e_dc)
                    dv.scalar_tensor_tensor(S_f[:, :], S_f[:, :], decc,
                                            pdel[:, 0:128], op0=ALU.mult, op1=ALU.add)
                    e_sb = b.inc(dv.tensor_copy(S_b[:, :], S_f[:, :]), "dv")
                    st["pdel_free"] = e_sb
                    st["S_bf_ready"] = e_sb
                return

            if gidx_own == 0:
                dump_dbg("S", S_f[:, :], st["S_bf_ready"])
                dump_dbg("qb", qb[:, 0:GRP], e_q)
                dump_dbg("kb", kb[:, 0:GRP], e_k)
                dump_dbg("spb", spb[:, 0:GRP], e_sp)
                dump_dbg("vb", v_b[:, 0:GRP], e_vev)
                dump_dbg("sbuf", s_buf[:, :], e_s)
            NB64, NB32, NB16, NB8 = GRP // 64, GRP // 32, GRP // 16, GRP // 8
            s1 = s_buf[:, 1:GRP + 1]
            s3_64 = s1.rearrange("p (g c) -> p g c", c=64)
            s3_32 = s1.rearrange("p (g c) -> p g c", c=32)
            s3_16 = s1.rearrange("p (g c) -> p g c", c=16)
            s3_8 = s1.rearrange("p (g c) -> p g c", c=8)
            CL = 24.0
            b.wait(dv, e_s)
            bas = s_buf[:, 0:GRP:64].unsqueeze(-1).broadcast_to([128, NB64, 64])
            e_t1 = b.inc(dv.tensor_tensor(
                Eoff[:, 0:GRP].rearrange("p (g c) -> p g c", c=64),
                bas, s3_64, op=ALU.subtract), "dv")
            b.wait(sc, e_t1)
            e_x1e = b.inc(sc.activation(Eoff[:, 0:GRP], Eoff[:, 0:GRP], AF.Exp), "sc")
            mid8 = s_buf[:, 4:4 + (NB8 - 1) * 8 + 1:8].unsqueeze(-1).broadcast_to(
                [128, NB8, 8])
            dv.tensor_tensor(escr[:, 0:GRP].rearrange("p (g c) -> p g c", c=8),
                             mid8, s3_8, op=ALU.subtract)
            e_c8 = b.inc(dv.tensor_scalar(escr[:, 0:GRP], escr[:, 0:GRP], CL, -CL,
                                          op0=ALU.min, op1=ALU.max), "dv")
            b.wait(sc, e_c8)
            sc.activation(q8[:, 0:GRP], escr[:, 0:GRP], AF.Exp)
            e_e8 = b.inc(sc.activation(k8[:, 0:GRP], escr[:, 0:GRP], AF.Exp,
                                       scale=-1.0), "sc")
            mid16 = s_buf[:, 8:8 + (NB16 - 1) * 16 + 1:16].unsqueeze(-1).broadcast_to(
                [128, NB16, 16])
            b.wait(dv, e_e8)
            dv.tensor_tensor(escr[:, 0:GRP].rearrange("p (g c) -> p g c", c=16),
                             mid16, s3_16, op=ALU.subtract)
            dv.tensor_scalar(escr2[:, 0:GRP], escr[:, 0:GRP], 0.0, None, op0=ALU.min)
            e_c16 = b.inc(dv.tensor_scalar(escr[:, 0:GRP], escr[:, 0:GRP], 0.0, None,
                                           op0=ALU.max), "dv")
            b.wait(sc, e_c16)
            sc.activation(q16[:, 0:GRP], escr2[:, 0:GRP], AF.Exp)
            e_e16 = b.inc(sc.activation(k16[:, 0:GRP], escr[:, 0:GRP], AF.Exp,
                                        scale=-1.0), "sc")
            bnd = s_buf[:, 16:16 + (NB32 - 1) * 32 + 1:32].unsqueeze(-1).broadcast_to(
                [128, NB32, 32])
            b.wait(dv, e_e16)
            dv.tensor_tensor(escr[:, 0:GRP].rearrange("p (g c) -> p g c", c=32),
                             bnd, s3_32, op=ALU.subtract)
            dv.tensor_scalar(escr2[:, 0:GRP], escr[:, 0:GRP], 0.0, None, op0=ALU.min)
            e_c32 = b.inc(dv.tensor_scalar(escr[:, 0:GRP], escr[:, 0:GRP], 0.0, None,
                                           op0=ALU.max), "dv")
            b.wait(sc, e_c32)
            sc.activation(q32[:, 0:GRP], escr2[:, 0:GRP], AF.Exp)
            e_e32 = b.inc(sc.activation(k32[:, 0:GRP], escr[:, 0:GRP], AF.Exp,
                                        scale=-1.0), "sc")
            bnd64 = s_buf[:, 32:32 + (NB64 - 1) * 64 + 1:64].unsqueeze(-1).broadcast_to(
                [128, NB64, 64])
            b.wait(dv, e_e32)
            dv.tensor_tensor(escr[:, 0:GRP].rearrange("p (g c) -> p g c", c=64),
                             bnd64, s3_64, op=ALU.subtract)
            dv.tensor_scalar(escr2[:, 0:GRP], escr[:, 0:GRP], 0.0, None, op0=ALU.min)
            e_c64 = b.inc(dv.tensor_scalar(escr[:, 0:GRP], escr[:, 0:GRP], 0.0, None,
                                           op0=ALU.max), "dv")
            b.wait(sc, e_c64)
            sc.activation(q64[:, 0:GRP], escr2[:, 0:GRP], AF.Exp)
            e_e64 = b.inc(sc.activation(k64[:, 0:GRP], escr[:, 0:GRP], AF.Exp,
                                        scale=-1.0), "sc")
            bend = s_buf[:, 64:64 + (NB64 - 1) * 64 + 1:64].unsqueeze(-1).broadcast_to(
                [128, NB64, 64])
            b.wait(dv, e_e64)
            e_t5 = b.inc(dv.tensor_tensor(
                escr[:, 0:GRP].rearrange("p (g c) -> p g c", c=64),
                s3_64, bend, op=ALU.subtract), "dv")
            b.wait(sc, e_t5)
            e_ksb = b.inc(sc.activation(escr[:, 0:GRP], escr[:, 0:GRP], AF.Exp), "sc")
            b.wait(dv, e_k)
            b.wait(dv, e_q)
            b.wait(dv, e_ksb)
            b.wait(dv, e_x1e)
            dv.tensor_tensor(qhat[:, 0:GRP], qb[:, 0:GRP], Eoff[:, 0:GRP], op=ALU.mult)
            dv.tensor_tensor(q8[:, 0:GRP], qb[:, 0:GRP], q8[:, 0:GRP], op=ALU.mult)
            dv.tensor_tensor(k8[:, 0:GRP], kb[:, 0:GRP], k8[:, 0:GRP], op=ALU.mult)
            dv.tensor_tensor(q16[:, 0:GRP], qb[:, 0:GRP], q16[:, 0:GRP], op=ALU.mult)
            dv.tensor_tensor(k16[:, 0:GRP], kb[:, 0:GRP], k16[:, 0:GRP], op=ALU.mult)
            dv.tensor_tensor(q32[:, 0:GRP], qb[:, 0:GRP], q32[:, 0:GRP], op=ALU.mult)
            dv.tensor_tensor(k32[:, 0:GRP], kb[:, 0:GRP], k32[:, 0:GRP], op=ALU.mult)
            dv.tensor_tensor(q64[:, 0:GRP], qb[:, 0:GRP], q64[:, 0:GRP], op=ALU.mult)
            dv.tensor_tensor(k64[:, 0:GRP], kb[:, 0:GRP], k64[:, 0:GRP], op=ALU.mult)
            e_kh = b.inc(dv.tensor_tensor(khT[:, 0:GRP], kb[:, 0:GRP],
                                          escr[:, 0:GRP], op=ALU.mult), "dv")
            e_mults = b.ev("dv")
            omit = _os.environ.get("K_OMIT", "")
            if omit == "inter":
                e_mults = b.inc(dv.memset(qhat[:, 0:GRP], 0.0), "dv")
            elif omit == "A":
                dv.memset(q8[:, 0:GRP], 0.0)
                dv.memset(q16[:, 0:GRP], 0.0)
                dv.memset(q32[:, 0:GRP], 0.0)
                e_mults = b.inc(dv.memset(q64[:, 0:GRP], 0.0), "dv")
            elif omit == "delta":
                e_kh = b.inc(dv.memset(khT[:, 0:GRP], 0.0), "dv")
                e_mults = e_kh
            if gidx_own == 0:
                dump_dbg("qhat", qhat[:, 0:GRP], e_mults)
                dump_dbg("q8", q8[:, 0:GRP], e_mults)
                dump_dbg("k8", k8[:, 0:GRP], e_mults)
                dump_dbg("k64", k64[:, 0:GRP], e_mults)
                dump_dbg("q64", q64[:, 0:GRP], e_mults)
                dump_dbg("khT", khT[:, 0:GRP], e_mults)
                dump_dbg("Eoff", Eoff[:, 0:GRP], e_mults)
            e_khc = khat_to_token_major(e_kh)

            for ci in range(NCH):
                c0 = ci * 128
                b.wait(pe, st["pA_free"])
                b.wait(pe, e_mults)
                specs = [(A1m, mbd, k8, q8), (A2m, mod8, k16, q16),
                         (A3m, mod_, k32, q32), (A4m, mod32, k64, q64)]
                for ai, (Am, msk_t, kt, qt) in enumerate(specs):
                    if ai > 0:
                        b.wait(pe, b.ev("dv"))
                    pe.matmul(pA[:, 128 * ai:128 * ai + 128],
                              kt[:, c0:c0 + 128], qt[:, c0:c0 + 128],
                              start=True, stop=True)
                    e_ai = b.inc(pe.last, "pe")
                    b.wait(dv, e_ai)
                    b.inc(dv.tensor_tensor(Am[:, :], pA[:, 128 * ai:128 * ai + 128],
                                           msk_t[:, :], op=ALU.mult), "dv")
                e_a4m = b.ev("dv")
                st["pA_free"] = e_a4m
                if gidx_own == 0 and ci == 0 and cfg.k_dbg == "Am" and not dbg_done[0]:
                    dbg_done[0] = True
                    b.wait(dv, e_a4m)
                    dv.tensor_copy(dbg_sb[:, 0:128], A1m[:, :])
                    dv.tensor_copy(dbg_sb[:, 128:256], A2m[:, :])
                    dv.tensor_copy(dbg_sb[:, 256:384], A3m[:, :])
                    e_dA = b.inc(dv.tensor_copy(dbg_sb[:, 384:512], A4m[:, :]), "dv")
                    b.wait(sy, e_dA)
                    b.inc(sy.dma_start(dbg_out, dbg_sb[:, :]), "so", 16)
                b.wait(dv, e_khc)
                e_km0 = b.inc(dv.tensor_scalar(khm[:, :], kh_t[:, c0:c0 + 128],
                                               smsk[:, 0:1], None, op0=ALU.mult), "dv")
                b.wait(pe, st["pdel_free"])
                b.wait(pe, e_km0)
                b.wait(pe, e_vev)
                pe.matmul(pdel[:, 0:128], khm[:, :], v_b[:, c0:c0 + 128],
                          start=True, stop=True)
                e_dl0 = b.inc(pe.last, "pe")
                b.wait(dv, e_dl0)
                e_km1 = b.inc(dv.tensor_scalar(khm[:, :], kh_t[:, c0:c0 + 128],
                                               smsk[:, 1:2], None, op0=ALU.mult), "dv")
                b.wait(pe, e_km1)
                pe.matmul(pdel[:, 128:256], khm[:, :], v_b[:, c0:c0 + 128],
                          start=True, stop=True)
                e_dl = b.inc(pe.last, "pe")
                b.wait(pe, st["pout_free"])
                b.wait(pe, e_a4m)
                pe.matmul(pout[:, 0:128], v_b[:, c0:c0 + 128], A1m[:, :],
                          start=True, stop=False)
                pe.matmul(pout[:, 0:128], v_b[:, c0:c0 + 128], A2m[:, :],
                          start=False, stop=False)
                pe.matmul(pout[:, 0:128], v_b[:, c0:c0 + 128], A3m[:, :],
                          start=False, stop=False)
                pe.matmul(pout[:, 0:128], v_b[:, c0:c0 + 128], A4m[:, :],
                          start=False, stop=False)
                for i in range(2):
                    b.wait(pe, st["S_bf_ready"])
                    pe.matmul(pout[:, 64 * i:64 * i + 64], S_b[:, :],
                              qhat[:, c0 + 64 * i: c0 + 64 * i + 64],
                              start=False, stop=(i == 1))
                    e_int = b.inc(pe.last, "pe")
                    b.wait(dv, e_dl if i == 1 else e_dl0)
                    b.wait(dv, e_int)
                    cdec = c0 + 64 * i + 63
                    dv.scalar_tensor_tensor(S_f[:, :], S_f[:, :],
                                            Eoff[:, cdec:cdec + 1],
                                            pdel[:, 128 * i:128 * i + 128],
                                            op0=ALU.mult, op1=ALU.add)
                    e_sb2 = b.inc(dv.tensor_copy(S_b[:, :], S_f[:, :]), "dv")
                    st["S_bf_ready"] = e_sb2
                st["pdel_free"] = b.ev("dv")
                e_out_mm = b.ev("pe")
                b.wait(dv, e_out_mm)
                e_ob = b.inc(dv.tensor_copy(outb[:, :], pout[:, 0:128]), "dv")
                st["pout_free"] = e_ob
                b.wait(pe, st["pp_free"][0])
                b.wait(pe, st["pp_free"][1])
                b.wait(pe, e_ob)
                pe.matmul(pp0[:, 0:512], outb[:, :], wo[:, 0:512], start=True, stop=False)
                pe.matmul(pp0[:, 0:512], ones_b[:, :], borb[:, 0:512], start=False, stop=True)
                pe.matmul(pp1[:, 0:512], outb[:, :], wo[:, 512:1024], start=True, stop=False)
                pe.matmul(pp1[:, 0:512], ones_b[:, :], borb[:, 512:1024], start=False, stop=True)
                e_y = b.inc(pe.last, "pe")
                bi, _ = ch_bufs[ci]
                if gidx_own == 0 and ci == 0:
                    dump_dbg("outb", outb[:, :], e_ob)
                b.wait(dv, e_y)
                e_xn = b.inc(dv.tensor_scalar(
                    xnf[:, :], xc[bi][:, :], mu_c[ci], rs_c[ci],
                    op0=ALU.subtract, op1=ALU.mult), "dv")
                dv.tensor_tensor(ybuf[:, 0:512], xnf[:, 0:512], pp0[:, 0:512], op=ALU.add)
                e_yb = b.inc(dv.tensor_tensor(ybuf[:, 512:1024], xnf[:, 512:1024],
                                              pp1[:, 0:512], op=ALU.add), "dv")
                st["pp_free"][0] = e_yb
                st["pp_free"][1] = e_yb
                if gidx_own == 0 and ci == 0:
                    dump_dbg("ybuf", ybuf[:, :], e_yb)
                e_r3 = ln_stats(ybuf[:, :], mu_s, rs_s, e_src=e_yb)
                b.wait(dv, e_r3)
                dv.tensor_scalar(ybuf[:, :], ybuf[:, :], mu_s,
                                 rs_s, op0=ALU.subtract, op1=ALU.mult)
                dv.tensor_tensor(ybuf[:, :], ybuf[:, :], gb_bc[:, :], op=ALU.mult)
                dv.tensor_tensor(xc[bi][:, :], xc[bi][:, :], bb_bc[:, :], op=ALU.add)
                e_x1 = b.inc(dv.tensor_tensor(xc[bi][:, :], xc[bi][:, :], ybuf[:, :],
                                              op=ALU.add), "dv")
                e_r4 = ln_stats(xc[bi][:, :], mu_s, rs_s, e_src=e_x1)
                b.wait(dv, e_r4)
                e_xn2 = b.inc(dv.tensor_scalar(
                    xnbf[:, :], xc[bi][:, :], mu_s, rs_s,
                    op0=ALU.subtract, op1=ALU.mult), "dv")
                gci = gidx_own * NCH + ci
                if gci == 0:
                    dump_dbg("x1", xc[bi][:, :], e_x1)
                b.wait(dv, e_x1)
                dv.tensor_copy(x1hi[:, :], xc[bi][:, :])
                e_hl = b.inc(dv.tensor_tensor(x1lo[:, :], xc[bi][:, :], x1hi[:, :],
                                              op=ALU.subtract), "dv")
                b.wait(pe, e_hl)
                b.wait(pe, st["trbf_free"])
                for d in range(ND):
                    pe.matmul(tpsum(d)[0][:, tpsum(d)[1]:tpsum(d)[1] + 128],
                              x1hi[:, d * 128:(d + 1) * 128], id_b[:, :],
                              start=True, stop=True)
                e_tp1a = b.inc(pe.last, "pe")
                sgi = gci % 2
                stgA = stg[sgi][:, :].rearrange("p (d two t) -> p two d t", two=2, t=128)
                b.wait(dv, e_tp1a)
                b.wait(dv, st["stg_free"][sgi])
                dv.tensor_copy(stgA[:, 0, :, :],
                               ptf0[:, 0:512].rearrange("p (d t) -> p d t", t=128))
                e_cp1 = b.inc(dv.tensor_copy(
                    stgA[:, 1, :, :],
                    ptf1[:, 0:512].rearrange("p (d t) -> p d t", t=128)), "dv")
                b.wait(pe, e_cp1)
                for d in range(ND):
                    pe.matmul(tpsum(d)[0][:, tpsum(d)[1]:tpsum(d)[1] + 128],
                              x1lo[:, d * 128:(d + 1) * 128], id_b[:, :],
                              start=True, stop=True)
                e_tp1 = b.inc(pe.last, "pe")
                b.wait(dv, e_tp1)
                dv.tensor_tensor(stgA[:, 0, :, :], stgA[:, 0, :, :],
                                 ptf0[:, 0:512].rearrange("p (d t) -> p d t", t=128),
                                 op=ALU.add)
                e_oi = b.inc(dv.tensor_tensor(
                    stgA[:, 1, :, :], stgA[:, 1, :, :],
                    ptf1[:, 0:512].rearrange("p (d t) -> p d t", t=128),
                    op=ALU.add), "dv")
                st["trbf_free"] = e_oi
                if gci == 0:
                    dump_dbg("stg", stg[sgi][:, :], e_oi)
                b.wait(gp, e_oi)
                e_ow = b.inc(gp.dma_start(
                    y_out.rearrange("(c p) t -> p c t", p=128)[:, :, gci * 128:(gci + 1) * 128],
                    stg[sgi][:, :].rearrange("p (c t) -> p c t", c=ND)), "sa", 16)
                st["stg_free"][sgi] = e_ow
                st["xbuf_free"][bi] = [e_oi, e_tp1, e_r4]
                b.wait(pe, e_xn2)
                b.wait(pe, st["trbf_free"])
                for d in range(ND):
                    pe.matmul(tpsum(d)[0][:, tpsum(d)[1]:tpsum(d)[1] + 128],
                              xnbf[:, d * 128:(d + 1) * 128], id_b[:, :],
                              start=True, stop=True)
                e_tp2 = b.inc(pe.last, "pe")
                b.wait(dv, e_tp2)
                for d in range(ND - 1):
                    dv.tensor_copy(xn2T[:, d * T_own + gci * 128: d * T_own + gci * 128 + 128],
                                   tpsum(d)[0][:, tpsum(d)[1]:tpsum(d)[1] + 128])
                d = ND - 1
                e_x2 = b.inc(dv.tensor_copy(
                    xn2T[:, d * T_own + gci * 128: d * T_own + gci * 128 + 128],
                    tpsum(d)[0][:, tpsum(d)[1]:tpsum(d)[1] + 128]), "dv")
                st["trbf_free"] = e_x2

        b.wait(pe, e_small0)
        b.wait(pe, e_idb)
        b.wait(pe, e_smallw)
        if cfg.run_pre:
            for gi in range(n_pre_grp):
                do_group(False, dt_in["x_pre"], gi * GRP)
        if cfg.run_own:
            for gi in range(n_own_grp):
                do_group(True, dt_in["x_own"], gi * GRP, gidx_own=gi)

        e_ab_dv = b.ev("dv")
        full_barrier()
        ab.close()
        if not (cfg.run_gate and cfg.run_own):
            b.wait(gp, b.ev("sa"))
            b.wait(gp, b.ev("sw"))
            b.wait(sy, b.ev("sx"))
            if cfg.k_dbg == "yback":
                e_yb2 = b.inc(gp.dma_start(dbg_sb[:, 0:T_own],
                                           y_out[0:128, :]), "sw", 16)
                b.wait(sy, e_yb2)
                b.inc(sy.dma_start(dbg_out, dbg_sb[:, :]), "so", 16)
                b.wait(sy, b.ev("so"))
            return nc

        gsc = ExitStack()
        al3 = gsc.enter_context

        def sb3(name, shape, dtype):
            return al3(nc.sbuf_tensor(name, shape, dtype))

        wg1 = sb3("wg1_sb", [128, ND * G], BF16)
        wg2 = sb3("wg2_sb", [128, NG * E], BF16)
        bg1c = sb3("bg1c", [128, NG], F32)
        gfm = [sb3(f"gfm{i}", [128, T_own], BF16) for i in range(NG)]
        LG = n_own_ch * E
        pr_all = sb3("pr_all", [128, LG], F32)
        ex_all = sb3("ex_all", [128, LG], F32)
        mk_all = sb3("mk_all", [128, LG], F32)
        tm_all = sb3("tm_all", [128, LG], F32)
        ga_all = sb3("ga_all", [128, LG], BF16)
        gmx = sb3("gmx", [128, n_own_ch], F32)
        gsm = sb3("gsm", [128, n_own_ch], F32)
        gm2x = sb3("gm2x", [128, n_own_ch], F32)
        u_f = sb3("u_f", [128, LG // 2], F32)
        v_f = sb3("v_f", [128, LG // 2], F32)
        w_f = sb3("w_f", [128, LG // 2], F32)
        zb_f = sb3("zb_f", [128, LG], F32)
        pg = [al3(nc.psum_tensor(f"pg{i}", [128, 512], F32)) for i in range(2)]
        pgt = al3(nc.psum_tensor("pgt", [128, 512], F32))

        e_bg1 = b.inc(sy.dma_start(bg1c[:, :], dt_in["bg1C"]),
                      "sx", 16)
        b.inc(gp.dma_start(wg1[:, :].rearrange("p (c g) -> p c g", c=ND),
                           dt_in["wg1"].rearrange("(c p) g -> p c g", p=128)), "sw", 16)
        e_wg = b.inc(gp.dma_start(wg2[:, :].rearrange("p (c e) -> p c e", c=NG),
                                  dt_in["wg2"].rearrange("(c p) e -> p c e", p=128)),
                     "sw", 16)
        pg_free = [None, None]
        pgi = [0]

        def pg_next():
            i = pgi[0]
            pgi[0] ^= 1
            return pg[i], i

        for gi in range(n_own_grp):
            t0 = gi * GRP
            for gt in range(NG):
                p, pi = pg_next()
                b.wait(pe, pg_free[pi])
                b.wait(pe, e_wg)
                b.wait(pe, e_ab_dv)
                for d in range(ND):
                    pe.matmul(p[:, 0:GRP],
                              wg1[:, d * G + gt * 128: d * G + gt * 128 + 128],
                              xn2T[:, d * T_own + t0: d * T_own + t0 + GRP],
                              start=(d == 0), stop=(d == ND - 1))
                e_mm = b.inc(pe.last, "pe")
                b.wait(sc, e_mm)
                b.wait(sc, e_bg1)
                e_ev = b.inc(sc.activation(gfm[gt][:, t0:t0 + GRP], p[:, 0:GRP],
                                           AF.Gelu, bias=bg1c[:, gt:gt + 1]), "sc")
                pg_free[pi] = e_ev
        e_gfm = b.ev("sc")
        pl_free = pg_free[0]
        pgt_free = None
        for ci in range(n_own_ch):
            c0 = ci * 128
            b.wait(pe, pl_free)
            b.wait(pe, e_gfm)
            for gt in range(NG):
                pe.matmul(pg[0][:, 256:256 + E], gfm[gt][:, c0:c0 + 128],
                          wg2[:, gt * E:(gt + 1) * E],
                          start=(gt == 0), stop=(gt == NG - 1))
            e_lg = b.inc(pe.last, "pe")
            b.wait(dv, e_lg)
            e_pr = b.inc(dv.tensor_copy(pr_all[:, ci * E:(ci + 1) * E],
                                        pg[0][:, 256:256 + E]), "dv")
            pl_free = e_pr
        e_prall = b.ev("dv")
        seg = lambda t: t[:, 0:LG].rearrange("p (c e) -> p c e", e=E)
        bc = lambda t: t[:, 0:n_own_ch].unsqueeze(-1).broadcast_to(
            [128, n_own_ch, E])
        b.wait(gp, e_prall)

        def seg_fold(src_all, dst16, is_max):
            width = E
            cur = src_all
            while width > 1:
                h = width // 2
                n2 = n_own_ch * h
                s3 = cur[:, 0:n_own_ch * width].rearrange(
                    "p (c e) -> p c e", e=width)
                u3 = u_f[:, 0:n2].rearrange("p (c e) -> p c e", e=h)
                v3_ = v_f[:, 0:n2].rearrange("p (c e) -> p c e", e=h)
                gp.tensor_copy(u3, s3[:, :, 0:h])
                gp.tensor_copy(v3_, s3[:, :, h:width])
                if is_max:
                    gp.tensor_tensor(w_f[:, 0:n2], u_f[:, 0:n2],
                                     v_f[:, 0:n2], op=ALU.subtract)
                    gp.tensor_scalar(u_f[:, 0:n2], w_f[:, 0:n2], 0.0, None,
                                     op0=ALU.is_ge)
                    gp.tensor_tensor(w_f[:, 0:n2], w_f[:, 0:n2],
                                     u_f[:, 0:n2], op=ALU.mult)
                    gp.tensor_tensor(u_f[:, 0:n2], v_f[:, 0:n2],
                                     w_f[:, 0:n2], op=ALU.add)
                else:
                    gp.tensor_tensor(u_f[:, 0:n2], u_f[:, 0:n2],
                                     v_f[:, 0:n2], op=ALU.add)
                cur = u_f
                width = h
            gp.tensor_copy(dst16, u_f[:, 0:n_own_ch])

        seg_fold(pr_all, gmx[:, :], True)
        e_sub = b.inc(gp.tensor_tensor(seg(ex_all), seg(pr_all), bc(gmx),
                                       op=ALU.subtract), "gp")
        b.wait(sc, e_sub)
        e_exp = b.inc(sc.activation(ex_all[:, 0:LG], ex_all[:, 0:LG],
                                    AF.Exp), "sc")
        b.wait(gp, e_exp)
        seg_fold(ex_all, gsm[:, :], False)
        gp.tensor_scalar(mk_all[:, 0:LG], ex_all[:, 0:LG], 1.0, None,
                         op0=ALU.is_ge)
        gp.tensor_tensor(tm_all[:, 0:LG], ex_all[:, 0:LG], mk_all[:, 0:LG],
                         op=ALU.mult)
        gp.tensor_tensor(tm_all[:, 0:LG], ex_all[:, 0:LG], tm_all[:, 0:LG],
                         op=ALU.subtract)
        seg_fold(tm_all, gm2x[:, :], True)
        gp.tensor_tensor(seg(tm_all), seg(tm_all), bc(gm2x),
                         op=ALU.subtract)
        gp.tensor_scalar(tm_all[:, 0:LG], tm_all[:, 0:LG], 0.0, None,
                         op0=ALU.is_ge)
        gp.tensor_tensor(mk_all[:, 0:LG], mk_all[:, 0:LG], tm_all[:, 0:LG],
                         op=ALU.add)
        gp.tensor_tensor(mk_all[:, 0:LG], mk_all[:, 0:LG], ex_all[:, 0:LG],
                         op=ALU.mult)
        gp.memset(zb_f[:, :], 1.0)
        e_zb = b.inc(gp.tensor_tensor(seg(zb_f), seg(zb_f), bc(gsm),
                                      op=ALU.mult), "gp")
        b.wait(dv, e_zb)
        e_rc = b.inc(dv.reciprocal(zb_f[:, 0:LG], zb_f[:, 0:LG]), "dv")
        b.wait(gp, e_rc)
        e_g = b.inc(gp.tensor_tensor(mk_all[:, 0:LG], mk_all[:, 0:LG],
                                     zb_f[:, 0:LG], op=ALU.mult), "gp")
        b.wait(dv, e_g)
        e_gab = b.inc(dv.tensor_copy(ga_all[:, :], mk_all[:, 0:LG]), "dv")
        for ci in range(n_own_ch):
            c0 = ci * 128
            b.wait(pe, e_gab)
            b.wait(pe, pgt_free)
            pe.matmul(pgt[0:E, 0:128], ga_all[:, ci * E:(ci + 1) * E],
                      id_b[:, :], start=True, stop=True)
            e_gt = b.inc(pe.last, "pe")
            b.wait(dv, e_gt)
            e_gc = b.inc(dv.tensor_copy(gatesT[:, c0:c0 + 128],
                                        pgt[0:E, 0:128]), "dv")
            pgt_free = e_gc

        full_barrier()
        gsc.close()
        if not cfg.run_ffn:
            b.wait(gp, b.ev("sa"))
            return nc

        cst = ExitStack()
        al4 = cst.enter_context

        def sb4(name, shape, dtype):
            return al4(nc.sbuf_tensor(name, shape, dtype))

        htil = sb4("htil", [128, NH * TSUB], BF16)
        gbc = [sb4(f"gbc{i}", [128, TSUB], BF16) for i in range(2)]
        ystg = [sb4(f"ystg{i}", [128, TSUB], F32) for i in range(2)]
        ph = [al4(nc.psum_tensor(f"ph{i}", [128, 512], F32)) for i in range(2)]
        py = [al4(nc.psum_tensor(f"py{i}", [128, 512], F32)) for i in range(2)]
        pgb = al4(nc.psum_tensor("pgb", [128, 512], F32))

        ph_free = [None, None]
        py_free = [None, None]
        pgb_free = None
        gbc_free = [None, None]
        ystg_free = [None, None]
        region_last = {}
        htil_guard = None
        win_ready = [e_win, e_win]
        wout_ready = [e_wout, e_wout]
        e_init_writes = b.ev("sa")
        b.wait(gp, e_init_writes)
        e_gates_dv = b.ev("dv")
        ysi = [0]

        for e_i in range(E):
            for tsi in range(n_tsub):
                ts0 = tsi * TSUB
                gi2 = (e_i * n_tsub + tsi) % 2
                b.wait(pe, pgb_free)
                b.wait(pe, e_gates_dv)
                pe.matmul(pgb[:, 0:TSUB], oneh[:, e_i * 128:(e_i + 1) * 128],
                          gatesT[:, ts0:ts0 + TSUB], start=True, stop=True)
                e_gb = b.inc(pe.last, "pe")
                b.wait(dv, e_gb)
                b.wait(dv, gbc_free[gi2])
                e_gbc2 = b.inc(dv.tensor_copy(gbc[gi2][:, :], pgb[:, 0:TSUB]), "dv")
                pgb_free = e_gbc2
                for hc in range(NH):
                    b.wait(pe, win_ready[0 if hc < NH // 2 else 1])
                    phi = hc % 2
                    b.wait(pe, ph_free[phi])
                    for d in range(ND):
                        pe.matmul(ph[phi][:, 0:TSUB],
                                  win[:, d * H + hc * 128: d * H + hc * 128 + 128],
                                  xn2T[:, d * T_own + ts0: d * T_own + ts0 + TSUB],
                                  start=(d == 0), stop=(d == ND - 1))
                    e_mm = b.inc(pe.last, "pe")
                    if tsi == n_tsub - 1 and hc == NH // 2 - 1 and e_i + 1 < E:
                        b.wait(gp, e_mm)
                        win_ready[0] = b.inc(
                            gp.dma_start(
                                win[:, :].rearrange("p (c h) -> p c h", c=ND)[:, :, 0:H // 2],
                                dt_in["w_in"][e_i + 1].rearrange(
                                    "(c p) h -> p c h", p=128)[:, :, 0:H // 2]), "sw", 16)
                    b.wait(sc, e_mm)
                    b.wait(sc, htil_guard)
                    e_gl = b.inc(sc.activation(htil[:, hc * TSUB:(hc + 1) * TSUB],
                                               ph[phi][:, 0:TSUB], AF.Gelu), "sc")
                    ph_free[phi] = e_gl
                    b.wait(dv, e_gl)
                    b.wait(dv, e_gbc2)
                    b.inc(dv.tensor_tensor(htil[:, hc * TSUB:(hc + 1) * TSUB],
                                           htil[:, hc * TSUB:(hc + 1) * TSUB],
                                           gbc[gi2][:, :], op=ALU.mult), "dv")
                gbc_free[gi2] = b.ev("dv")
                e_p1_done = b.ev("pe")
                e_htil_dv = b.ev("dv")
                if tsi == n_tsub - 1 and e_i + 1 < E:
                    b.wait(gp, e_p1_done)
                    win_ready[1] = b.inc(
                        gp.dma_start(
                            win[:, :].rearrange("p (c h) -> p c h", c=ND)[:, :, H // 2:H],
                            dt_in["w_in"][e_i + 1].rearrange(
                                "(c p) h -> p c h", p=128)[:, :, H // 2:H]), "sw2", 16)
                b.wait(pe, e_htil_dv)
                for d in range(ND):
                    b.wait(pe, wout_ready[0 if d < ND // 2 else 1])
                    pyi = d % 2
                    b.wait(pe, py_free[pyi])
                    for hc in range(NH):
                        pe.matmul(py[pyi][:, 0:TSUB],
                                  wout[:, hc * D + d * 128: hc * D + d * 128 + 128],
                                  htil[:, hc * TSUB:(hc + 1) * TSUB],
                                  start=(hc == 0), stop=(hc == NH - 1))
                    e_mm2 = b.inc(pe.last, "pe")
                    if tsi == n_tsub - 1 and d == ND // 2 - 1 and e_i + 1 < E:
                        b.wait(gp, e_mm2)
                        wout_ready[0] = b.inc(
                            gp.dma_start(
                                wout[:, :].rearrange("p (c d) -> p c d", c=NH)[:, :, 0:D // 2],
                                dt_in["w_out"][e_i + 1].rearrange(
                                    "(c p) d -> p c d", p=128)[:, :, 0:D // 2]), "sw3", 16)
                    yi2 = ysi[0]
                    ysi[0] ^= 1
                    b.wait(dv, e_mm2)
                    b.wait(dv, ystg_free[yi2])
                    e_cp = b.inc(dv.tensor_copy(ystg[yi2][:, :], py[pyi][:, 0:TSUB]), "dv")
                    py_free[pyi] = e_cp
                    b.wait(gp, e_cp)
                    b.wait(gp, region_last.get((d, tsi)))
                    e_ad = b.inc(gp.dma_start(
                        y_out[d * 128:(d + 1) * 128, ts0:ts0 + TSUB],
                        ystg[yi2][:, :], accum_op=ALU.add), "sa", 16)
                    region_last[(d, tsi)] = e_ad
                    ystg_free[yi2] = e_ad
                htil_guard = b.ev("pe")
            if e_i + 1 < E:
                e_p2_done = b.ev("pe")
                b.wait(gp, e_p2_done)
                wout_ready[1] = b.inc(
                    gp.dma_start(
                        wout[:, :].rearrange("p (c d) -> p c d", c=NH)[:, :, D // 2:D],
                        dt_in["w_out"][e_i + 1].rearrange(
                            "(c p) d -> p c d", p=128)[:, :, D // 2:D]), "sw4", 16)

        b.wait(gp, b.ev("sa"))
        cst.close()
    return nc


def make_consts():
    ident = np.eye(128, dtype=np.float32)
    ii, jj = np.meshgrid(np.arange(128), np.arange(128), indexing="ij")
    m_bd8 = (((ii // 8) == (jj // 8)) & (ii <= jj)).astype(np.float32)
    m_od8 = (((ii // 16) == (jj // 16)) & ((ii % 16) < 8) & ((jj % 16) >= 8)).astype(np.float32)
    m_od16 = (((ii // 32) == (jj // 32)) & ((ii % 32) < 16) & ((jj % 32) >= 16)).astype(np.float32)
    m_od32 = (((ii // 64) == (jj // 64)) & ((ii % 64) < 32) & ((jj % 64) >= 32)).astype(np.float32)
    submask = np.zeros((128, 2), np.float32)
    submask[0:64, 0] = 1.0
    submask[64:128, 1] = 1.0
    return ident, m_bd8, m_od8, m_od16, m_od32, submask


def make_in_maps(cfg, inputs):
    x = np.ascontiguousarray(np.asarray(inputs["x"], dtype=np.float32)).reshape(-1, cfg.D)
    T = x.shape[0]
    assert T == cfg.n_cores * cfg.T_own, (T, cfg.n_cores, cfg.T_own)
    ident, m_bd8, m_od8, m_od16, m_od32, submask = make_consts()
    D = cfg.D
    oneh = np.zeros((cfg.E, cfg.E * 128), dtype=np.float32)
    for e in range(cfg.E):
        oneh[e, e * 128:(e + 1) * 128] = 1.0
    base = dict(ident=ident, m_bd8=m_bd8, m_od8=m_od8, m_od16=m_od16,
                m_od32=m_od32, submask=submask, onehot=oneh)
    cc = np.zeros((128, 4), dtype=np.float32)
    cc[:, 0] = 1e-5
    cc[:, 2] = 1.0
    base["cconst"] = cc
    base["onesrow"] = np.ones((1, 128), dtype=np.float32)
    base["zer128"] = np.zeros((128, 128), dtype=np.float32)
    base["gammaB"] = np.ascontiguousarray(np.broadcast_to(
        np.asarray(inputs["gamma"], np.float32)[None, :], (128, D)))
    base["betaB"] = np.ascontiguousarray(np.broadcast_to(
        np.asarray(inputs["beta"], np.float32)[None, :], (128, D)))
    base["bg1C"] = np.ascontiguousarray(
        np.asarray(inputs["bg1"], np.float32).reshape(cfg.G // 128, 128).T)
    for nm in ["Wq", "Wk", "Wv", "Wg", "Wo", "wg1", "wg2", "w_in", "w_out",
               "bq", "bk", "bv", "bg", "bo", "gamma", "beta", "bg1"]:
        base[nm] = np.ascontiguousarray(np.asarray(inputs[nm], dtype=np.float32))
    maps = []
    npch = max(cfg.T_pre // cfg.CH, 1)
    for c in range(cfg.n_cores):
        t0 = c * cfg.T_own
        m = dict(base)
        m["x_own"] = np.ascontiguousarray(x[t0:t0 + cfg.T_own])
        if cfg.T_pre:
            xp = np.zeros((cfg.T_pre, cfg.D), dtype=np.float32)
            npad = cfg.T_pre - t0
            if t0 > 0:
                xp[npad:] = x[:t0]
            m["x_pre"] = xp
            pm = np.zeros(cfg.T_pre, dtype=np.float32)
            pm[npad:] = 1.0
            m["pmask"] = np.ascontiguousarray(pm.reshape(npch, cfg.CH).T)
        else:
            m["pmask"] = np.zeros((128, npch), dtype=np.float32)
        maps.append(m)
    return maps


def run(cfg, inputs, runner=None):
    maps = make_in_maps(cfg, inputs)
    nc = build_program(cfg)
    if runner is None:
        res = run_bass_kernel_spmd(nc, maps, core_ids=list(range(cfg.n_cores)))
        results = res.results
    else:
        results = runner(nc, maps)
    import os as _os2
    if _os2.environ.get("K_DBG", ""):
        np.save("/root/problem/_dbg0.npy", np.asarray(results[0]["dbg"]))
        if cfg.n_cores > 1:
            np.save("/root/problem/_dbg1.npy", np.asarray(results[1]["dbg"]))
    parts = [np.asarray(results[c]["y"]).T for c in range(cfg.n_cores)]
    out = np.concatenate(parts, axis=0).astype(np.float32)
    return out.reshape(1, cfg.n_cores * cfg.T_own, cfg.D)


def kernel(L0_attn_x, L0_attn_Wq, L0_attn_bq, L0_attn_Wk, L0_attn_bk,
           L0_attn_Wv, L0_attn_bv, L0_attn_Wg, L0_attn_bg, L0_attn_Wo,
           L0_attn_bo, L0_attn_gamma, L0_attn_beta, L0_ffn_wg1, L0_ffn_bg1,
           L0_ffn_wg2, L0_ffn_w_in, L0_ffn_w_out):
    cfg = CFG()
    inputs = dict(x=L0_attn_x, Wq=L0_attn_Wq, bq=L0_attn_bq, Wk=L0_attn_Wk,
                  bk=L0_attn_bk, Wv=L0_attn_Wv, bv=L0_attn_bv, Wg=L0_attn_Wg,
                  bg=L0_attn_bg, Wo=L0_attn_Wo, bo=L0_attn_bo,
                  gamma=L0_attn_gamma, beta=L0_attn_beta, wg1=L0_ffn_wg1,
                  bg1=L0_ffn_bg1, wg2=L0_ffn_wg2, w_in=L0_ffn_w_in,
                  w_out=L0_ffn_w_out)
    return run(cfg, inputs)


# version 09
# speedup vs baseline: 1.0120x; 1.0120x over previous
import numpy as np
import concourse.bass as bass
import concourse.mybir as mybir
from concourse.bass_utils import run_bass_kernel_spmd
from contextlib import ExitStack

F32 = mybir.dt.float32
BF16 = mybir.dt.bfloat16
U8 = mybir.dt.uint8
AF = mybir.ActivationFunctionType
ALU = mybir.AluOpType
AX = mybir.AxisListType

EPS = 1e-5


import os as _os


class CFG:
    p_ln = _os.environ.get("P_LN", "1") == "1"
    t_src_idb = _os.environ.get("T_SRC", "x") == "idb"
    t_evac = _os.environ.get("T_EVAC", "1") == "1"
    k_now = _os.environ.get("K_NOW", "0") == "1"
    k_dbg = _os.environ.get("K_DBG", "")
    p_tr = _os.environ.get("P_TR", "1") == "1"
    p_proj = _os.environ.get("P_PROJ", "1") == "1"
    p_v = _os.environ.get("P_V", "1") == "1"
    p_sc = _os.environ.get("P_SC", "1") == "1"
    p_dl = _os.environ.get("P_DL", "1") == "1"
    run_pre = True
    run_own = True
    run_gate = True
    run_ffn = True
    D = 1024
    S = 128
    G = 256
    E = 8
    H = 2048
    T_own = 2048
    T_pre = 14336
    GRP = 512
    CH = 128
    TSUB = 512
    n_cores = 8


class PeTrack:

    def __init__(self, eng):
        object.__setattr__(self, "_e", eng)
        object.__setattr__(self, "last", None)

    def matmul(self, *a, **k):
        r = self._e.matmul(*a, **k)
        object.__setattr__(self, "last", r)
        return r

    def __getattr__(self, n):
        return getattr(self._e, n)


class Bld:
    def __init__(self, nc, stack):
        self.nc = nc
        self.stack = stack
        self.sem = {}
        self.cnt = {}
        self.last_wait = {}
        self.engs = {}

    def new_sem(self, name):
        self.sem[name] = self.stack.enter_context(self.nc.semaphore(name))
        self.cnt[name] = 0

    def inc(self, instr, name, k=1):
        self.cnt[name] += k
        if name in self.engs:
            instr = self.engs[name].drain()
        instr.then_inc(self.sem[name], k)
        return (name, self.cnt[name])

    def ev(self, name):
        return (name, self.cnt[name])

    DMA_SEMS = ("sx", "sw", "sa", "so", "sw2", "sw3", "sw4")

    def wait(self, eng, evt):
        if evt is None:
            return
        if isinstance(evt, (list, tuple)) and evt and isinstance(evt[0], tuple):
            for e in evt:
                self.wait(eng, e)
            return
        name, val = evt
        if name in self.DMA_SEMS:
            val = self.cnt[name]
        if val <= 0:
            return
        assert val <= self.cnt[name], f"forward wait {name} {val} > {self.cnt[name]}"
        key = (id(eng), name)
        if self.last_wait.get(key, -1) >= val:
            return
        eng.wait_ge(self.sem[name], val)
        self.last_wait[key] = val


def build_program(cfg):
    nc = bass.Bass("TRN2", target_bir_lowering=False, detect_race_conditions=False)
    D, S, G, E, H = cfg.D, cfg.S, cfg.G, cfg.E, cfg.H
    ND, NG, NH = D // 128, G // 128, H // 128
    GRP, CH, TSUB = cfg.GRP, cfg.CH, cfg.TSUB
    NCH = GRP // CH
    n_pre_grp = cfg.T_pre // GRP
    n_own_grp = cfg.T_own // GRP
    n_own_ch = cfg.T_own // CH
    n_tsub = cfg.T_own // TSUB
    NPRECH = max(cfg.T_pre // CH, 1)
    T_own = cfg.T_own

    dt_in = {}

    def din(name, shape, dtype=F32):
        dt_in[name] = nc.dram_tensor(name, shape, dtype, kind="ExternalInput").ap()
        return dt_in[name]

    if n_pre_grp:
        din("x_pre", [cfg.T_pre, D])
    din("pmask", [128, NPRECH])
    din("x_own", [T_own, D])
    for nm, shp in [("Wq", [D, S]), ("Wk", [D, S]), ("Wv", [D, S]), ("Wg", [D, S]),
                    ("Wo", [S, D]), ("wg1", [D, G]), ("wg2", [G, E]),
                    ("w_in", [E, D, H]), ("w_out", [E, H, D]),
                    ("bq", [S]), ("bk", [S]), ("bv", [S]), ("bg", [S]),
                    ("bo", [D]), ("gamma", [D]), ("beta", [D]), ("bg1", [G])]:
        din(nm, shp)
    din("gammaB", [128, D])
    din("betaB", [128, D])
    din("bg1C", [128, G // 128])
    din("onehot", [E, E * 128])
    din("ident", [128, 128])
    din("m_bd8", [128, 128])
    din("m_od8", [128, 128])
    din("m_od16", [128, 128])
    din("m_od32", [128, 128])
    din("submask", [128, 2])
    din("cconst", [128, 4])
    din("onesrow", [1, 128])
    din("zer128", [128, 128])
    y_out = nc.dram_tensor("y", [D, T_own], F32, kind="ExternalOutput").ap()
    dbg_out = nc.dram_tensor("dbg", [128, 2048], F32, kind="ExternalOutput").ap()

    with ExitStack() as stack:
        b = Bld(nc, stack)
        al = stack.enter_context

        def sb(name, shape, dtype):
            return al(nc.sbuf_tensor(name, shape, dtype))

        for s_ in ["sx", "sw", "so", "sa", "sw2", "sw3", "sw4", "pe", "dv", "sc", "gp"]:
            b.new_sem(s_)
        pe, dv, sc, gp, sy = PeTrack(nc.tensor), nc.vector, nc.scalar, nc.gpsimd, nc.sync
        b.engs = {"pe": nc.tensor, "dv": nc.vector, "sc": nc.scalar, "gp": nc.gpsimd}

        id_f = sb("id_f", [128, 128], F32)
        id_b = sb("id_b", [128, 128], BF16)
        mbd = sb("mbd", [128, 128], BF16)
        mod8 = sb("mod8", [128, 128], BF16)
        mod_ = sb("mod_", [128, 128], BF16)
        mod32 = sb("mod32", [128, 128], BF16)
        smsk = sb("smsk", [128, 2], F32)
        ones_b = sb("ones_b", [1, 128], BF16)
        ones_f = sb("ones_f", [1, 128], F32)
        zer_b = sb("zer_b", [128, 1], BF16)
        gb_bc = sb("gb_bc", [128, D], F32)
        bb_bc = sb("bb_bc", [128, D], F32)
        xn2T = sb("xn2T", [128, ND * T_own], BF16)
        gatesT = sb("gatesT", [E, T_own], BF16)
        oneh = sb("oneh", [E, E * 128], BF16)
        win = sb("win", [128, ND * H], BF16)
        wout = sb("wout", [128, NH * D], BF16)
        dbg_sb = sb("dbg_sb", [128, 2048 if cfg.k_dbg else 8], F32)
        csp = sb("csp", [128, 32], F32)

        b.inc(sy.dma_start(id_f[:, :], dt_in["ident"]), "sx", 16)
        b.inc(gp.dma_start(mbd[:, :], dt_in["m_bd8"]), "sw", 16)
        b.inc(gp.dma_start(mod8[:, :], dt_in["m_od8"]), "sw", 16)
        b.inc(gp.dma_start(mod_[:, :], dt_in["m_od16"]), "sw", 16)
        b.inc(gp.dma_start(mod32[:, :], dt_in["m_od32"]), "sw", 16)
        b.inc(sy.dma_start(smsk[:, :], dt_in["submask"]), "sx", 16)
        b.inc(sy.dma_start(gb_bc[:, :], dt_in["gammaB"]), "sx", 16)
        e_small0 = b.inc(sy.dma_start(bb_bc[:, :], dt_in["betaB"]), "sx", 16)
        b.inc(gp.dma_start(oneh[:, :], dt_in["onehot"]), "sw", 16)
        e_idb = b.inc(gp.dma_start(id_b[:, :], dt_in["ident"]), "sw", 16)
        if not cfg.k_now:
            e_win = b.inc(gp.dma_start(win[:, :].rearrange("p (c h) -> p c h", c=ND),
                                       dt_in["w_in"][0].rearrange("(c p) h -> p c h", p=128)),
                          "sw", 16)
            e_wout = b.inc(gp.dma_start(wout[:, :].rearrange("p (c d) -> p c d", c=NH),
                                        dt_in["w_out"][0].rearrange("(c p) d -> p c d", p=128)),
                           "sw", 16)
        else:
            e_win = e_wout = None
        cpool = sb("cpool", [128, 4], F32)
        b.inc(sy.dma_start(cpool[:, :], dt_in["cconst"]), "sx", 16)
        b.inc(sy.dma_start(ones_f[:, :], dt_in["onesrow"]), "sx", 16)
        b.inc(gp.dma_start(ones_b[:, :], dt_in["onesrow"]), "sw", 16)
        nc.const_aps.register(F32, EPS, cpool[:, 0:1])

        ab = ExitStack()
        al2 = ab.enter_context

        def sb2(name, shape, dtype):
            return al2(nc.sbuf_tensor(name, shape, dtype))

        def ps2(name, shape, dtype):
            return al2(nc.psum_tensor(name, shape, dtype))

        wq = sb2("wq", [128, ND * S], BF16)
        wk = sb2("wk", [128, ND * S], BF16)
        wv = sb2("wv", [128, ND * S], BF16)
        wgt = sb2("wgt", [128, ND * S], BF16)
        wo = sb2("wo", [S, D], BF16)
        bqc = sb2("bqc", [128, 1], F32)
        bkc = sb2("bkc", [128, 1], F32)
        bgc = sb2("bgc", [128, 1], F32)
        bvrb = sb2("bvrb", [1, S], BF16)
        borb = sb2("borb", [1, D], BF16)
        pmk = sb2("pmk", [128, NPRECH], F32)
        S_f = sb2("S_f", [128, 128], F32)
        S_b = sb2("S_b", [128, 128], BF16)
        NXB = 4
        xc = [sb2(f"xc{i}", [128, D], F32) for i in range(NXB)]
        xnf = sb2("xnf", [128, D], F32)
        stg = [sb2(f"stg{i}", [128, D], F32) for i in range(2)]
        xnbf = sb2("xnbf", [128, D], BF16)
        xnT = sb2("xnT", [128, ND * GRP], BF16)
        qb = sb2("qb", [128, GRP], BF16)
        kb = sb2("kb", [128, GRP], BF16)
        spb = sb2("spb", [128, GRP], F32)
        escr = sb2("escr", [128, GRP], F32)
        s_buf = sb2("s_buf", [128, GRP + 1], F32)
        Eoff = sb2("Eoff", [128, GRP], F32)
        qhat = sb2("qhat", [128, GRP], BF16)
        q8 = sb2("q8", [128, GRP], BF16)
        k8 = sb2("k8", [128, GRP], BF16)
        q16 = sb2("q16", [128, GRP], BF16)
        k16 = sb2("k16", [128, GRP], BF16)
        q32 = sb2("q32", [128, GRP], BF16)
        k32 = sb2("k32", [128, GRP], BF16)
        escr2 = sb2("escr2", [128, GRP], F32)
        khT = sb2("khT", [128, GRP], BF16)
        kh_t = sb2("kh_t", [128, GRP], BF16)
        v_b = sb2("v_b", [128, GRP], BF16)
        A1m = sb2("A1m", [128, 128], BF16)
        A2m = sb2("A2m", [128, 128], BF16)
        A3m = sb2("A3m", [128, 128], BF16)
        A4m = sb2("A4m", [128, 128], BF16)
        q64 = sb2("q64", [128, GRP], BF16)
        k64 = sb2("k64", [128, GRP], BF16)
        x1hi = sb2("x1hi", [128, D], BF16)
        x1lo = sb2("x1lo", [128, D], BF16)
        khm = sb2("khm", [128, 128], BF16)
        outb = sb2("outb", [128, 128], BF16)
        ybuf = sb2("ybuf", [128, D], F32)
        ddump = sb2("ddump", [128, D // 2], F32)
        sdump = ddump
        lnsets = []
        for li in range(2):
            lnsets.append({nm: sb2(f"{nm}_{li}", [128, 1], F32)
                           for nm in ("acc_sa0", "acc_sa1", "acc_qa0",
                                      "acc_qa1", "msqt", "ts_t", "tq_t",
                                      "m2t", "nm2t", "t6t")})
        mu_cd = [sb2(f"mu_cd{i}", [128, 1], F32) for i in range(4)]
        rs_cd = [sb2(f"rs_cd{i}", [128, 1], F32) for i in range(4)]
        mu_sd = sb2("mu_sd", [128, 1], F32)
        rs_sd = sb2("rs_sd", [128, 1], F32)
        t4c = csp[:, 18:19]
        t5c = csp[:, 19:20]
        t6c = csp[:, 20:21]
        ts_c = csp[:, 25:26]
        tq_c = csp[:, 26:27]
        m2_c = csp[:, 27:28]
        nm2_c = csp[:, 28:29]
        decc = csp[:, 21:22]
        mu_s = mu_sd[:, 0:1]
        rs_s = rs_sd[:, 0:1]
        mu_c = [mu_cd[i][:, 0:1] for i in range(NCH)]
        rs_c = [rs_cd[i][:, 0:1] for i in range(NCH)]
        ptf0 = ps2("ptf0", [128, 512], F32)
        ptf1 = ps2("ptf1", [128, 512], F32)
        pp0 = ps2("pp0", [128, 512], F32)
        pp1 = ps2("pp1", [128, 512], F32)
        pA = ps2("pA", [128, 512], F32)
        pdel = ps2("pdel", [128, 512], F32)
        pout = ps2("pout", [128, 512], F32)
        pps = [pp0, pp1]

        b.inc(sy.dma_start(bqc[:, :], dt_in["bq"].unsqueeze(-1)), "sx", 16)
        b.inc(sy.dma_start(bkc[:, :], dt_in["bk"].unsqueeze(-1)), "sx", 16)
        b.inc(sy.dma_start(bgc[:, :], dt_in["bg"].unsqueeze(-1)), "sx", 16)
        b.inc(sy.dma_start(S_f[:, :], dt_in["zer128"]), "sx", 16)
        e_small = b.inc(sy.dma_start(pmk[:, :], dt_in["pmask"]), "sx", 16)
        for w_sb, nm in [(wq, "Wq"), (wk, "Wk"), (wv, "Wv"), (wgt, "Wg")]:
            b.inc(gp.dma_start(w_sb[:, :].rearrange("p (c s) -> p c s", c=ND),
                               dt_in[nm].rearrange("(c p) s -> p c s", p=128)), "sw", 16)
        b.inc(gp.dma_start(wo[:, :], dt_in["Wo"]), "sw", 16)
        b.inc(gp.dma_start(S_b[:, :], dt_in["zer128"]), "sw", 16)
        b.inc(gp.dma_start(bvrb[:, :], dt_in["bv"].unsqueeze(0)), "sw", 16)
        e_smallw = b.inc(gp.dma_start(borb[:, :], dt_in["bo"].unsqueeze(0)), "sw", 16)

        b.wait(sc, e_small)
        b.wait(sc, e_smallw)
        sc.activation(s_buf[:, 0:1], cpool[:, 1:2], AF.Identity)
        e_dvj = b.inc(sc.activation(csp[:, 23:24], bgc[:, 0:1], AF.Identity,
                                    scale=-1.0), "sc")

        e_bc_done = e_dvj

        dbg_done = [False]
        ln_call = [0]
        ln_trigger = int(_os.environ.get("K_LNTRIG", "1"))

        def dump_dbg(name, src_ap, evt=None):
            if cfg.k_dbg != name or dbg_done[0]:
                return
            dbg_done[0] = True
            b.wait(dv, evt)
            n = min(src_ap.shape[-1], 2048)
            e_d = b.inc(dv.tensor_copy(dbg_sb[0:src_ap.shape[0], 0:n], src_ap[:, 0:n]), "dv")
            b.wait(sy, e_d)
            b.inc(sy.dma_start(dbg_out, dbg_sb[:, :]), "so", 16)

        st = {
            "pp_free": [e_bc_done, e_bc_done],
            "pp_i": 0,
            "trbf_free": e_bc_done,
            "pA_free": None, "pdel_free": None, "pout_free": None,
            "S_bf_ready": e_smallw,
            "xbuf_free": [None] * NXB,
            "stg_free": [None, None],
            "xch": 0,
            "ln_guard": [None, None],
        }

        def load_x_chunk(src_ap):
            i = st["xch"] % NXB
            b.wait(sy, st["xbuf_free"][i])
            ev = b.inc(sy.dma_start(xc[i][:, :], src_ap), "sx", 16)
            st["xch"] += 1
            return i, ev

        def group_barrier():
            epe, edv, esc = b.ev("pe"), b.ev("dv"), b.ev("sc")
            b.wait(pe, edv)
            b.wait(pe, esc)
            b.wait(dv, esc)
            b.wait(dv, epe)
            b.wait(sc, edv)
            b.wait(sc, epe)

        def full_barrier():
            group_barrier()
            for eng in (gp, sy):
                for s_ in ("pe", "dv", "sc"):
                    b.wait(eng, b.ev(s_))

        def pp_next():
            i = st["pp_i"]
            st["pp_i"] ^= 1
            return pps[i], i

        def ln_start(xap, e_src=None):
            par = ln_call[0] % 2
            ln_call[0] += 1
            L = lnsets[par]
            b.wait(sc, st["ln_guard"][par])
            b.wait(sc, e_src)
            HD = D // 2
            sc.activation(ddump[:, 0:HD], xap[:, 0:HD], AF.Identity,
                          accum_out=L["acc_sa0"][:, 0:1])
            sc.activation(ddump[:, 0:HD], xap[:, HD:D], AF.Identity,
                          accum_out=L["acc_sa1"][:, 0:1])
            sc.activation(sdump[:, 0:HD], xap[:, 0:HD], AF.Square,
                          accum_out=L["acc_qa0"][:, 0:1])
            sc.activation(sdump[:, 0:HD], xap[:, HD:D], AF.Square,
                          accum_out=L["acc_qa1"][:, 0:1])
            e_sq = b.inc(nc.scalar.drain(), "sc")
            return par, e_sq

        def ln_finish(par, e_sq, mu_ap, rstd_ap):
            L = lnsets[par]
            b.wait(gp, st["ln_guard"][par])
            b.wait(gp, e_sq)
            gp.tensor_add(L["ts_t"][:, 0:1], L["acc_sa0"][:, 0:1],
                          L["acc_sa1"][:, 0:1])
            gp.tensor_scalar_mul(mu_ap, L["ts_t"][:, 0:1], 1.0 / D)
            gp.tensor_add(L["tq_t"][:, 0:1], L["acc_qa0"][:, 0:1],
                          L["acc_qa1"][:, 0:1])
            gp.tensor_scalar_mul(L["msqt"][:, 0:1], L["tq_t"][:, 0:1], 1.0 / D)
            gp.tensor_mul(L["m2t"][:, 0:1], mu_ap, mu_ap)
            e_v = b.inc(gp.tensor_sub(L["nm2t"][:, 0:1], L["msqt"][:, 0:1],
                                      L["m2t"][:, 0:1]), "gp")
            b.wait(sc, e_v)
            sc.activation(ddump[:, 0:1], L["nm2t"][:, 0:1], AF.Ln,
                          bias=EPS, accum_out=L["t6t"][:, 0:1])
            e_r = b.inc(sc.activation(ddump[:, 0:1], L["t6t"][:, 0:1], AF.Exp,
                                      scale=-0.5, accum_out=rstd_ap), "sc")
            st["ln_guard"][par] = e_v
            return e_r

        def ln_stats(xap, mu_ap, rstd_ap, e_src=None):
            par, e_sq = ln_start(xap, e_src)
            e_r = ln_finish(par, e_sq, mu_ap, rstd_ap)
            ln_dbg(xap, mu_ap, rstd_ap, e_r)
            return e_r

        def ln_dbg(xap, mu_ap, rstd_ap, e_r):
            if cfg.k_dbg == "lnx" and ln_call[0] == ln_trigger and not dbg_done[0]:
                dbg_done[0] = True
                b.wait(dv, e_r)
                dv.tensor_copy(dbg_sb[:, 0:1024], xap[:, 0:1024])
                dv.tensor_copy(dbg_sb[:, 1028:1029], mu_ap)
                e_d = b.inc(dv.tensor_copy(dbg_sb[:, 1031:1032], rstd_ap), "dv")
                b.wait(sy, e_d)
                b.inc(sy.dma_start(dbg_out, dbg_sb[:, :]), "so", 16)

        def proj(w_sb, dst_bf, bias_col, e_ready):
            p, pi = pp_next()
            b.wait(pe, st["pp_free"][pi])
            b.wait(pe, e_ready)
            for d in range(ND):
                pe.matmul(p[:, 0:GRP], w_sb[:, d * S:(d + 1) * S],
                          xnT[:, d * GRP:(d + 1) * GRP],
                          start=(d == 0), stop=(d == ND - 1))
            e_mm = b.inc(pe.last, "pe")
            b.wait(sc, e_mm)
            e_ev = b.inc(sc.activation(dst_bf[:, 0:GRP], p[:, 0:GRP],
                                       AF.Identity, bias=bias_col), "sc")
            st["pp_free"][pi] = e_ev
            return e_ev


        def tpsum(d):
            return (ptf0 if d % 2 == 0 else ptf1), (d // 2) * 128

        def khat_to_token_major(e_khT):
            b.wait(pe, e_khT)
            b.wait(pe, st["trbf_free"])
            for ci in range(NCH):
                pe.matmul(tpsum(ci)[0][:, tpsum(ci)[1]:tpsum(ci)[1] + 128],
                          khT[:, ci * 128:(ci + 1) * 128], id_b[:, :],
                          start=True, stop=True)
            e_tp = b.inc(pe.last, "pe")
            b.wait(dv, e_tp)
            for ci in range(NCH - 1):
                dv.tensor_copy(kh_t[:, ci * 128:(ci + 1) * 128],
                               tpsum(ci)[0][:, tpsum(ci)[1]:tpsum(ci)[1] + 128])
            e_c = b.inc(dv.tensor_copy(kh_t[:, (NCH - 1) * 128:NCH * 128],
                                       tpsum(NCH - 1)[0][:, tpsum(NCH - 1)[1]:tpsum(NCH - 1)[1] + 128]), "dv")
            st["trbf_free"] = e_c
            return e_c

        def do_group(own, src, first_tok, gidx_own=None):
            group_barrier()
            ch_bufs = [load_x_chunk(src[first_tok + ci * CH: first_tok + ci * CH + CH, :])
                       for ci in range(NCH)]
            e_tr_all = None
            lnp = [None] * NCH
            if cfg.p_ln:
                bi0, e_x0 = ch_bufs[0]
                b.wait(sc, e_x0)
                lnp[0] = ln_start(xc[bi0][:, :])
            for ci in range(NCH):
                bi, e_x = ch_bufs[ci]
                b.wait(dv, e_x)
                if cfg.p_ln:
                    if ci + 1 < NCH:
                        bi1, e_x1_ = ch_bufs[ci + 1]
                        b.wait(sc, e_x1_)
                        lnp[ci + 1] = ln_start(xc[bi1][:, :])
                    e_r = ln_finish(lnp[ci][0], lnp[ci][1], mu_c[ci],
                                    rs_c[ci])
                    b.wait(dv, e_r)
                    e_ap = b.inc(dv.tensor_scalar(
                        xnbf[:, :], xc[bi][:, :], mu_c[ci],
                        rs_c[ci],
                        op0=ALU.subtract, op1=ALU.mult), "dv")
                else:
                    e_r = None
                    e_ap = b.inc(dv.tensor_copy(xnbf[:, :], xc[bi][:, :]), "dv")
                if not own:
                    st["xbuf_free"][bi] = [e_ap] + ([e_r] if e_r else [])
                if own and gidx_own == 0 and ci == 0:
                    dump_dbg("xnbf", xnbf[:, :], e_ap)
                    dump_dbg("xc", xc[bi][:, :], e_ap)
                    dump_dbg("murst", csp[:, 0:16], e_ap)
                    dump_dbg("idb", id_b[:, :], e_ap)
                if (not own) and ci == 0:
                    dump_dbg("pxnbf", xnbf[:, :], e_ap)
                    dump_dbg("pxc", xc[bi][:, :], e_ap)
                if cfg.p_tr:
                    b.wait(pe, e_ap)
                    b.wait(pe, st["trbf_free"])
                    for d in range(ND):
                        src = id_b[:, :] if cfg.t_src_idb else xnbf[:, d * 128:(d + 1) * 128]
                        pe.matmul(tpsum(d)[0][:, tpsum(d)[1]:tpsum(d)[1] + 128],
                                  src, id_b[:, :],
                                  start=True, stop=True)
                    e_tp = b.inc(pe.last, "pe")
                    b.wait(dv, e_tp)
                    if cfg.t_evac:
                        for d in range(ND - 1):
                            dv.tensor_copy(xnT[:, d * GRP + ci * 128: d * GRP + ci * 128 + 128],
                                           tpsum(d)[0][:, tpsum(d)[1]:tpsum(d)[1] + 128])
                        d = ND - 1
                        e_tc = b.inc(dv.tensor_copy(
                            xnT[:, d * GRP + ci * 128: d * GRP + ci * 128 + 128],
                            tpsum(d)[0][:, tpsum(d)[1]:tpsum(d)[1] + 128]), "dv")
                    else:
                        e_tc = b.inc(dv.nop(), "dv")
                    if own and gidx_own == 0 and ci == 0 and cfg.k_dbg == "ptf":
                        b.wait(dv, e_tp)
                        e_d2 = b.inc(dv.tensor_copy(dbg_sb[:, 0:512], ptf0[:, 0:512]), "dv")
                        e_d2 = b.inc(dv.tensor_copy(dbg_sb[:, 512:1024], ptf1[:, 0:512]), "dv")
                        dbg_done[0] = True
                        b.wait(sy, e_d2)
                        b.inc(sy.dma_start(dbg_out, dbg_sb[:, :]), "so", 16)
                        e_tc = e_d2
                    st["trbf_free"] = e_tc
                    e_tr_all = e_tc
                else:
                    e_tr_all = e_ap

            if cfg.p_proj:
                e_k = proj(wk, kb, bkc[:, 0:1], e_tr_all)
                e_q = proj(wq, qb, bqc[:, 0:1], e_tr_all) if own else None
                p, pi = pp_next()
                b.wait(pe, st["pp_free"][pi])
                b.wait(pe, e_tr_all)
                for d in range(ND):
                    pe.matmul(p[:, 0:GRP], wgt[:, d * S:(d + 1) * S],
                              xnT[:, d * GRP:(d + 1) * GRP],
                              start=(d == 0), stop=(d == ND - 1))
                e_gmm = b.inc(pe.last, "pe")
                b.wait(sc, e_gmm)
                sc.activation(escr[:, 0:GRP], p[:, 0:GRP], AF.Exp, scale=-1.0,
                              bias=csp[:, 23:24])
                e_sp = b.inc(sc.activation(spb[:, 0:GRP], escr[:, 0:GRP],
                                           AF.Ln, bias=1.0), "sc")
                st["pp_free"][pi] = e_sp
            else:
                dv.memset(kb[:, 0:GRP], 0.01)
                dv.memset(qb[:, 0:GRP], 0.01)
                e_sp = e_k = e_q = b.inc(dv.memset(spb[:, 0:GRP], 0.01), "dv")

            e_vev = None
            if cfg.p_v:
                for ci in range(NCH):
                    p, pi = pp_next()
                    b.wait(pe, st["pp_free"][pi])
                    for d in range(ND):
                        pe.matmul(p[:, 0:128],
                                  xnT[:, d * GRP + ci * 128: d * GRP + ci * 128 + 128],
                                  wv[:, d * S:(d + 1) * S], start=(d == 0), stop=False)
                    pe.matmul(p[:, 0:128], ones_b[:, :], bvrb[:, :], start=False, stop=True)
                    e_mm = b.inc(pe.last, "pe")
                    b.wait(dv, e_mm)
                    e_vev = b.inc(dv.tensor_copy(v_b[:, ci * 128:(ci + 1) * 128],
                                                 p[:, 0:128]), "dv")
                    st["pp_free"][pi] = e_vev
            else:
                e_vev = b.inc(dv.memset(v_b[:, 0:GRP], 0.01), "dv")

            if cfg.p_sc:
                b.wait(dv, e_sp)
                e_s = b.inc(dv.tensor_tensor_scan(s_buf[:, 1:GRP + 1], spb[:, 0:GRP],
                                                  spb[:, 0:GRP], 0.0,
                                                  op0=ALU.add, op1=ALU.bypass), "dv")
            else:
                e_s = b.inc(dv.memset(s_buf[:, 1:GRP + 1], 0.02), "dv")

            if not own:
                dump_dbg("pxnT", xnT[:, 0:2048], e_tr_all)
                dump_dbg("pspb", spb[:, 0:GRP], e_sp)
                dump_dbg("psbuf", s_buf[:, :], e_s)
                dump_dbg("pkb", kb[:, 0:GRP], e_k)
                dump_dbg("pvb", v_b[:, 0:GRP], e_vev)
                if cfg.p_sc:
                    b.wait(dv, e_s)
                    e_sub = b.inc(dv.tensor_scalar(escr[:, 0:GRP], s_buf[:, 1:GRP + 1],
                                                   s_buf[:, GRP:GRP + 1], None,
                                                   op0=ALU.subtract), "dv")
                    b.wait(sc, e_sub)
                    e_eh = b.inc(sc.activation(escr[:, 0:GRP], escr[:, 0:GRP], AF.Exp), "sc")
                    b.wait(dv, e_eh)
                    b.wait(dv, e_k)
                    e_kh = b.inc(dv.tensor_tensor(khT[:, 0:GRP], kb[:, 0:GRP],
                                                  escr[:, 0:GRP], op=ALU.mult), "dv")
                    e_khc = khat_to_token_major(e_kh)
                    ch0 = first_tok // CH
                    e_msk = e_khc
                    for ci in range(NCH):
                        e_msk = b.inc(dv.tensor_scalar(kh_t[:, ci * 128:(ci + 1) * 128],
                                                       kh_t[:, ci * 128:(ci + 1) * 128],
                                                       pmk[:, ch0 + ci:ch0 + ci + 1], None,
                                                       op0=ALU.mult), "dv")
                else:
                    e_msk = b.inc(dv.memset(kh_t[:, 0:GRP], 0.01), "dv")
                if cfg.p_dl:
                    b.wait(pe, e_msk)
                    b.wait(pe, e_vev)
                    b.wait(pe, st["pdel_free"])
                    for ci in range(NCH):
                        pe.matmul(pdel[:, 0:128], kh_t[:, ci * 128:(ci + 1) * 128],
                                  v_b[:, ci * 128:(ci + 1) * 128],
                                  start=(ci == 0), stop=(ci == NCH - 1))
                    e_dl = b.inc(pe.last, "pe")
                    b.wait(sc, e_s)
                    e_dc = b.inc(sc.activation(decc, s_buf[:, GRP:GRP + 1],
                                               AF.Exp, scale=-1.0), "sc")
                    b.wait(dv, e_dl)
                    b.wait(dv, e_dc)
                    dv.scalar_tensor_tensor(S_f[:, :], S_f[:, :], decc,
                                            pdel[:, 0:128], op0=ALU.mult, op1=ALU.add)
                    e_sb = b.inc(dv.tensor_copy(S_b[:, :], S_f[:, :]), "dv")
                    st["pdel_free"] = e_sb
                    st["S_bf_ready"] = e_sb
                return

            if gidx_own == 0:
                dump_dbg("S", S_f[:, :], st["S_bf_ready"])
                dump_dbg("qb", qb[:, 0:GRP], e_q)
                dump_dbg("kb", kb[:, 0:GRP], e_k)
                dump_dbg("spb", spb[:, 0:GRP], e_sp)
                dump_dbg("vb", v_b[:, 0:GRP], e_vev)
                dump_dbg("sbuf", s_buf[:, :], e_s)
            NB64, NB32, NB16, NB8 = GRP // 64, GRP // 32, GRP // 16, GRP // 8
            s1 = s_buf[:, 1:GRP + 1]
            s3_64 = s1.rearrange("p (g c) -> p g c", c=64)
            s3_32 = s1.rearrange("p (g c) -> p g c", c=32)
            s3_16 = s1.rearrange("p (g c) -> p g c", c=16)
            s3_8 = s1.rearrange("p (g c) -> p g c", c=8)
            CL = 24.0
            b.wait(dv, e_s)
            bas = s_buf[:, 0:GRP:64].unsqueeze(-1).broadcast_to([128, NB64, 64])
            e_t1 = b.inc(dv.tensor_tensor(
                Eoff[:, 0:GRP].rearrange("p (g c) -> p g c", c=64),
                bas, s3_64, op=ALU.subtract), "dv")
            b.wait(sc, e_t1)
            e_x1e = b.inc(sc.activation(Eoff[:, 0:GRP], Eoff[:, 0:GRP], AF.Exp), "sc")
            mid8 = s_buf[:, 4:4 + (NB8 - 1) * 8 + 1:8].unsqueeze(-1).broadcast_to(
                [128, NB8, 8])
            dv.tensor_tensor(escr[:, 0:GRP].rearrange("p (g c) -> p g c", c=8),
                             mid8, s3_8, op=ALU.subtract)
            e_c8 = b.inc(dv.tensor_scalar(escr[:, 0:GRP], escr[:, 0:GRP], CL, -CL,
                                          op0=ALU.min, op1=ALU.max), "dv")
            b.wait(sc, e_c8)
            sc.activation(q8[:, 0:GRP], escr[:, 0:GRP], AF.Exp)
            e_e8 = b.inc(sc.activation(k8[:, 0:GRP], escr[:, 0:GRP], AF.Exp,
                                       scale=-1.0), "sc")
            mid16 = s_buf[:, 8:8 + (NB16 - 1) * 16 + 1:16].unsqueeze(-1).broadcast_to(
                [128, NB16, 16])
            b.wait(dv, e_e8)
            dv.tensor_tensor(escr[:, 0:GRP].rearrange("p (g c) -> p g c", c=16),
                             mid16, s3_16, op=ALU.subtract)
            dv.tensor_scalar(escr2[:, 0:GRP], escr[:, 0:GRP], 0.0, None, op0=ALU.min)
            e_c16 = b.inc(dv.tensor_scalar(escr[:, 0:GRP], escr[:, 0:GRP], 0.0, None,
                                           op0=ALU.max), "dv")
            b.wait(sc, e_c16)
            sc.activation(q16[:, 0:GRP], escr2[:, 0:GRP], AF.Exp)
            e_e16 = b.inc(sc.activation(k16[:, 0:GRP], escr[:, 0:GRP], AF.Exp,
                                        scale=-1.0), "sc")
            bnd = s_buf[:, 16:16 + (NB32 - 1) * 32 + 1:32].unsqueeze(-1).broadcast_to(
                [128, NB32, 32])
            b.wait(dv, e_e16)
            dv.tensor_tensor(escr[:, 0:GRP].rearrange("p (g c) -> p g c", c=32),
                             bnd, s3_32, op=ALU.subtract)
            dv.tensor_scalar(escr2[:, 0:GRP], escr[:, 0:GRP], 0.0, None, op0=ALU.min)
            e_c32 = b.inc(dv.tensor_scalar(escr[:, 0:GRP], escr[:, 0:GRP], 0.0, None,
                                           op0=ALU.max), "dv")
            b.wait(sc, e_c32)
            sc.activation(q32[:, 0:GRP], escr2[:, 0:GRP], AF.Exp)
            e_e32 = b.inc(sc.activation(k32[:, 0:GRP], escr[:, 0:GRP], AF.Exp,
                                        scale=-1.0), "sc")
            bnd64 = s_buf[:, 32:32 + (NB64 - 1) * 64 + 1:64].unsqueeze(-1).broadcast_to(
                [128, NB64, 64])
            b.wait(dv, e_e32)
            dv.tensor_tensor(escr[:, 0:GRP].rearrange("p (g c) -> p g c", c=64),
                             bnd64, s3_64, op=ALU.subtract)
            dv.tensor_scalar(escr2[:, 0:GRP], escr[:, 0:GRP], 0.0, None, op0=ALU.min)
            e_c64 = b.inc(dv.tensor_scalar(escr[:, 0:GRP], escr[:, 0:GRP], 0.0, None,
                                           op0=ALU.max), "dv")
            b.wait(sc, e_c64)
            sc.activation(q64[:, 0:GRP], escr2[:, 0:GRP], AF.Exp)
            e_e64 = b.inc(sc.activation(k64[:, 0:GRP], escr[:, 0:GRP], AF.Exp,
                                        scale=-1.0), "sc")
            bend = s_buf[:, 64:64 + (NB64 - 1) * 64 + 1:64].unsqueeze(-1).broadcast_to(
                [128, NB64, 64])
            b.wait(dv, e_e64)
            e_t5 = b.inc(dv.tensor_tensor(
                escr[:, 0:GRP].rearrange("p (g c) -> p g c", c=64),
                s3_64, bend, op=ALU.subtract), "dv")
            b.wait(sc, e_t5)
            e_ksb = b.inc(sc.activation(escr[:, 0:GRP], escr[:, 0:GRP], AF.Exp), "sc")
            b.wait(dv, e_k)
            b.wait(dv, e_q)
            b.wait(dv, e_ksb)
            b.wait(dv, e_x1e)
            dv.tensor_tensor(qhat[:, 0:GRP], qb[:, 0:GRP], Eoff[:, 0:GRP], op=ALU.mult)
            dv.tensor_tensor(q8[:, 0:GRP], qb[:, 0:GRP], q8[:, 0:GRP], op=ALU.mult)
            dv.tensor_tensor(k8[:, 0:GRP], kb[:, 0:GRP], k8[:, 0:GRP], op=ALU.mult)
            dv.tensor_tensor(q16[:, 0:GRP], qb[:, 0:GRP], q16[:, 0:GRP], op=ALU.mult)
            dv.tensor_tensor(k16[:, 0:GRP], kb[:, 0:GRP], k16[:, 0:GRP], op=ALU.mult)
            dv.tensor_tensor(q32[:, 0:GRP], qb[:, 0:GRP], q32[:, 0:GRP], op=ALU.mult)
            dv.tensor_tensor(k32[:, 0:GRP], kb[:, 0:GRP], k32[:, 0:GRP], op=ALU.mult)
            dv.tensor_tensor(q64[:, 0:GRP], qb[:, 0:GRP], q64[:, 0:GRP], op=ALU.mult)
            dv.tensor_tensor(k64[:, 0:GRP], kb[:, 0:GRP], k64[:, 0:GRP], op=ALU.mult)
            e_kh = b.inc(dv.tensor_tensor(khT[:, 0:GRP], kb[:, 0:GRP],
                                          escr[:, 0:GRP], op=ALU.mult), "dv")
            e_mults = b.ev("dv")
            omit = _os.environ.get("K_OMIT", "")
            if omit == "inter":
                e_mults = b.inc(dv.memset(qhat[:, 0:GRP], 0.0), "dv")
            elif omit == "A":
                dv.memset(q8[:, 0:GRP], 0.0)
                dv.memset(q16[:, 0:GRP], 0.0)
                dv.memset(q32[:, 0:GRP], 0.0)
                e_mults = b.inc(dv.memset(q64[:, 0:GRP], 0.0), "dv")
            elif omit == "delta":
                e_kh = b.inc(dv.memset(khT[:, 0:GRP], 0.0), "dv")
                e_mults = e_kh
            if gidx_own == 0:
                dump_dbg("qhat", qhat[:, 0:GRP], e_mults)
                dump_dbg("q8", q8[:, 0:GRP], e_mults)
                dump_dbg("k8", k8[:, 0:GRP], e_mults)
                dump_dbg("k64", k64[:, 0:GRP], e_mults)
                dump_dbg("q64", q64[:, 0:GRP], e_mults)
                dump_dbg("khT", khT[:, 0:GRP], e_mults)
                dump_dbg("Eoff", Eoff[:, 0:GRP], e_mults)
            e_khc = khat_to_token_major(e_kh)

            for ci in range(NCH):
                c0 = ci * 128
                b.wait(pe, st["pA_free"])
                b.wait(pe, e_mults)
                specs = [(A1m, mbd, k8, q8), (A2m, mod8, k16, q16),
                         (A3m, mod_, k32, q32), (A4m, mod32, k64, q64)]
                for ai, (Am, msk_t, kt, qt) in enumerate(specs):
                    if ai > 0:
                        b.wait(pe, b.ev("dv"))
                    pe.matmul(pA[:, 128 * ai:128 * ai + 128],
                              kt[:, c0:c0 + 128], qt[:, c0:c0 + 128],
                              start=True, stop=True)
                    e_ai = b.inc(pe.last, "pe")
                    b.wait(dv, e_ai)
                    b.inc(dv.tensor_tensor(Am[:, :], pA[:, 128 * ai:128 * ai + 128],
                                           msk_t[:, :], op=ALU.mult), "dv")
                e_a4m = b.ev("dv")
                st["pA_free"] = e_a4m
                if gidx_own == 0 and ci == 0 and cfg.k_dbg == "Am" and not dbg_done[0]:
                    dbg_done[0] = True
                    b.wait(dv, e_a4m)
                    dv.tensor_copy(dbg_sb[:, 0:128], A1m[:, :])
                    dv.tensor_copy(dbg_sb[:, 128:256], A2m[:, :])
                    dv.tensor_copy(dbg_sb[:, 256:384], A3m[:, :])
                    e_dA = b.inc(dv.tensor_copy(dbg_sb[:, 384:512], A4m[:, :]), "dv")
                    b.wait(sy, e_dA)
                    b.inc(sy.dma_start(dbg_out, dbg_sb[:, :]), "so", 16)
                b.wait(dv, e_khc)
                e_km0 = b.inc(dv.tensor_scalar(khm[:, :], kh_t[:, c0:c0 + 128],
                                               smsk[:, 0:1], None, op0=ALU.mult), "dv")
                b.wait(pe, st["pdel_free"])
                b.wait(pe, e_km0)
                b.wait(pe, e_vev)
                pe.matmul(pdel[:, 0:128], khm[:, :], v_b[:, c0:c0 + 128],
                          start=True, stop=True)
                e_dl0 = b.inc(pe.last, "pe")
                b.wait(dv, e_dl0)
                e_km1 = b.inc(dv.tensor_scalar(khm[:, :], kh_t[:, c0:c0 + 128],
                                               smsk[:, 1:2], None, op0=ALU.mult), "dv")
                b.wait(pe, e_km1)
                pe.matmul(pdel[:, 128:256], khm[:, :], v_b[:, c0:c0 + 128],
                          start=True, stop=True)
                e_dl = b.inc(pe.last, "pe")
                b.wait(pe, st["pout_free"])
                b.wait(pe, e_a4m)
                pe.matmul(pout[:, 0:128], v_b[:, c0:c0 + 128], A1m[:, :],
                          start=True, stop=False)
                pe.matmul(pout[:, 0:128], v_b[:, c0:c0 + 128], A2m[:, :],
                          start=False, stop=False)
                pe.matmul(pout[:, 0:128], v_b[:, c0:c0 + 128], A3m[:, :],
                          start=False, stop=False)
                pe.matmul(pout[:, 0:128], v_b[:, c0:c0 + 128], A4m[:, :],
                          start=False, stop=False)
                for i in range(2):
                    b.wait(pe, st["S_bf_ready"])
                    pe.matmul(pout[:, 64 * i:64 * i + 64], S_b[:, :],
                              qhat[:, c0 + 64 * i: c0 + 64 * i + 64],
                              start=False, stop=(i == 1))
                    e_int = b.inc(pe.last, "pe")
                    b.wait(dv, e_dl if i == 1 else e_dl0)
                    b.wait(dv, e_int)
                    cdec = c0 + 64 * i + 63
                    dv.scalar_tensor_tensor(S_f[:, :], S_f[:, :],
                                            Eoff[:, cdec:cdec + 1],
                                            pdel[:, 128 * i:128 * i + 128],
                                            op0=ALU.mult, op1=ALU.add)
                    e_sb2 = b.inc(dv.tensor_copy(S_b[:, :], S_f[:, :]), "dv")
                    st["S_bf_ready"] = e_sb2
                st["pdel_free"] = b.ev("dv")
                e_out_mm = b.ev("pe")
                b.wait(dv, e_out_mm)
                e_ob = b.inc(dv.tensor_copy(outb[:, :], pout[:, 0:128]), "dv")
                st["pout_free"] = e_ob
                b.wait(pe, st["pp_free"][0])
                b.wait(pe, st["pp_free"][1])
                b.wait(pe, e_ob)
                pe.matmul(pp0[:, 0:512], outb[:, :], wo[:, 0:512], start=True, stop=False)
                pe.matmul(pp0[:, 0:512], ones_b[:, :], borb[:, 0:512], start=False, stop=True)
                pe.matmul(pp1[:, 0:512], outb[:, :], wo[:, 512:1024], start=True, stop=False)
                pe.matmul(pp1[:, 0:512], ones_b[:, :], borb[:, 512:1024], start=False, stop=True)
                e_y = b.inc(pe.last, "pe")
                bi, _ = ch_bufs[ci]
                if gidx_own == 0 and ci == 0:
                    dump_dbg("outb", outb[:, :], e_ob)
                b.wait(dv, e_y)
                e_xn = b.inc(dv.tensor_scalar(
                    xnf[:, :], xc[bi][:, :], mu_c[ci], rs_c[ci],
                    op0=ALU.subtract, op1=ALU.mult), "dv")
                dv.tensor_tensor(ybuf[:, 0:512], xnf[:, 0:512], pp0[:, 0:512], op=ALU.add)
                e_yb = b.inc(dv.tensor_tensor(ybuf[:, 512:1024], xnf[:, 512:1024],
                                              pp1[:, 0:512], op=ALU.add), "dv")
                st["pp_free"][0] = e_yb
                st["pp_free"][1] = e_yb
                if gidx_own == 0 and ci == 0:
                    dump_dbg("ybuf", ybuf[:, :], e_yb)
                e_r3 = ln_stats(ybuf[:, :], mu_s, rs_s, e_src=e_yb)
                b.wait(dv, e_r3)
                dv.tensor_scalar(ybuf[:, :], ybuf[:, :], mu_s,
                                 rs_s, op0=ALU.subtract, op1=ALU.mult)
                dv.tensor_tensor(ybuf[:, :], ybuf[:, :], gb_bc[:, :], op=ALU.mult)
                dv.tensor_tensor(xc[bi][:, :], xc[bi][:, :], bb_bc[:, :], op=ALU.add)
                e_x1 = b.inc(dv.tensor_tensor(xc[bi][:, :], xc[bi][:, :], ybuf[:, :],
                                              op=ALU.add), "dv")
                e_r4 = ln_stats(xc[bi][:, :], mu_s, rs_s, e_src=e_x1)
                b.wait(dv, e_r4)
                e_xn2 = b.inc(dv.tensor_scalar(
                    xnbf[:, :], xc[bi][:, :], mu_s, rs_s,
                    op0=ALU.subtract, op1=ALU.mult), "dv")
                gci = gidx_own * NCH + ci
                if gci == 0:
                    dump_dbg("x1", xc[bi][:, :], e_x1)
                b.wait(dv, e_x1)
                dv.tensor_copy(x1hi[:, :], xc[bi][:, :])
                e_hl = b.inc(dv.tensor_tensor(x1lo[:, :], xc[bi][:, :], x1hi[:, :],
                                              op=ALU.subtract), "dv")
                b.wait(pe, e_hl)
                b.wait(pe, st["trbf_free"])
                for d in range(ND):
                    pe.matmul(tpsum(d)[0][:, tpsum(d)[1]:tpsum(d)[1] + 128],
                              x1hi[:, d * 128:(d + 1) * 128], id_b[:, :],
                              start=True, stop=True)
                e_tp1a = b.inc(pe.last, "pe")
                sgi = gci % 2
                stgA = stg[sgi][:, :].rearrange("p (d two t) -> p two d t", two=2, t=128)
                b.wait(dv, e_tp1a)
                b.wait(dv, st["stg_free"][sgi])
                dv.tensor_copy(stgA[:, 0, :, :],
                               ptf0[:, 0:512].rearrange("p (d t) -> p d t", t=128))
                e_cp1 = b.inc(dv.tensor_copy(
                    stgA[:, 1, :, :],
                    ptf1[:, 0:512].rearrange("p (d t) -> p d t", t=128)), "dv")
                b.wait(pe, e_cp1)
                for d in range(ND):
                    pe.matmul(tpsum(d)[0][:, tpsum(d)[1]:tpsum(d)[1] + 128],
                              x1lo[:, d * 128:(d + 1) * 128], id_b[:, :],
                              start=True, stop=True)
                e_tp1 = b.inc(pe.last, "pe")
                b.wait(dv, e_tp1)
                dv.tensor_tensor(stgA[:, 0, :, :], stgA[:, 0, :, :],
                                 ptf0[:, 0:512].rearrange("p (d t) -> p d t", t=128),
                                 op=ALU.add)
                e_oi = b.inc(dv.tensor_tensor(
                    stgA[:, 1, :, :], stgA[:, 1, :, :],
                    ptf1[:, 0:512].rearrange("p (d t) -> p d t", t=128),
                    op=ALU.add), "dv")
                st["trbf_free"] = e_oi
                if gci == 0:
                    dump_dbg("stg", stg[sgi][:, :], e_oi)
                b.wait(gp, e_oi)
                e_ow = b.inc(gp.dma_start(
                    y_out.rearrange("(c p) t -> p c t", p=128)[:, :, gci * 128:(gci + 1) * 128],
                    stg[sgi][:, :].rearrange("p (c t) -> p c t", c=ND)), "sa", 16)
                st["stg_free"][sgi] = e_ow
                st["xbuf_free"][bi] = [e_oi, e_tp1, e_r4]
                b.wait(pe, e_xn2)
                b.wait(pe, st["trbf_free"])
                for d in range(ND):
                    pe.matmul(tpsum(d)[0][:, tpsum(d)[1]:tpsum(d)[1] + 128],
                              xnbf[:, d * 128:(d + 1) * 128], id_b[:, :],
                              start=True, stop=True)
                e_tp2 = b.inc(pe.last, "pe")
                b.wait(dv, e_tp2)
                for d in range(ND - 1):
                    dv.tensor_copy(xn2T[:, d * T_own + gci * 128: d * T_own + gci * 128 + 128],
                                   tpsum(d)[0][:, tpsum(d)[1]:tpsum(d)[1] + 128])
                d = ND - 1
                e_x2 = b.inc(dv.tensor_copy(
                    xn2T[:, d * T_own + gci * 128: d * T_own + gci * 128 + 128],
                    tpsum(d)[0][:, tpsum(d)[1]:tpsum(d)[1] + 128]), "dv")
                st["trbf_free"] = e_x2

        b.wait(pe, e_small0)
        b.wait(pe, e_idb)
        b.wait(pe, e_smallw)
        if cfg.run_pre:
            for gi in range(n_pre_grp):
                do_group(False, dt_in["x_pre"], gi * GRP)
        if cfg.run_own:
            for gi in range(n_own_grp):
                do_group(True, dt_in["x_own"], gi * GRP, gidx_own=gi)

        e_ab_dv = b.ev("dv")
        full_barrier()
        ab.close()
        if not (cfg.run_gate and cfg.run_own):
            b.wait(gp, b.ev("sa"))
            b.wait(gp, b.ev("sw"))
            b.wait(sy, b.ev("sx"))
            if cfg.k_dbg == "yback":
                e_yb2 = b.inc(gp.dma_start(dbg_sb[:, 0:T_own],
                                           y_out[0:128, :]), "sw", 16)
                b.wait(sy, e_yb2)
                b.inc(sy.dma_start(dbg_out, dbg_sb[:, :]), "so", 16)
                b.wait(sy, b.ev("so"))
            return nc

        gsc = ExitStack()
        al3 = gsc.enter_context

        def sb3(name, shape, dtype):
            return al3(nc.sbuf_tensor(name, shape, dtype))

        wg1 = sb3("wg1_sb", [128, ND * G], BF16)
        wg2 = sb3("wg2_sb", [128, NG * E], BF16)
        bg1c = sb3("bg1c", [128, NG], F32)
        gfm = [sb3(f"gfm{i}", [128, T_own], BF16) for i in range(NG)]
        LG = n_own_ch * E
        pr_all = sb3("pr_all", [128, LG], F32)
        ex_all = sb3("ex_all", [128, LG], F32)
        mk_all = sb3("mk_all", [128, LG], F32)
        tm_all = sb3("tm_all", [128, LG], F32)
        ga_all = sb3("ga_all", [128, LG], BF16)
        gmx = sb3("gmx", [128, n_own_ch], F32)
        gsm = sb3("gsm", [128, n_own_ch], F32)
        gm2x = sb3("gm2x", [128, n_own_ch], F32)
        u_f = sb3("u_f", [128, LG // 2], F32)
        v_f = sb3("v_f", [128, LG // 2], F32)
        w_f = sb3("w_f", [128, LG // 2], F32)
        zb_f = sb3("zb_f", [128, LG], F32)
        pg = [al3(nc.psum_tensor(f"pg{i}", [128, 512], F32)) for i in range(2)]
        pgt = al3(nc.psum_tensor("pgt", [128, 512], F32))

        e_bg1 = b.inc(sy.dma_start(bg1c[:, :], dt_in["bg1C"]),
                      "sx", 16)
        b.inc(gp.dma_start(wg1[:, :].rearrange("p (c g) -> p c g", c=ND),
                           dt_in["wg1"].rearrange("(c p) g -> p c g", p=128)), "sw", 16)
        e_wg = b.inc(gp.dma_start(wg2[:, :].rearrange("p (c e) -> p c e", c=NG),
                                  dt_in["wg2"].rearrange("(c p) e -> p c e", p=128)),
                     "sw", 16)
        pg_free = [None, None]
        pgi = [0]

        def pg_next():
            i = pgi[0]
            pgi[0] ^= 1
            return pg[i], i

        for gi in range(n_own_grp):
            t0 = gi * GRP
            for gt in range(NG):
                p, pi = pg_next()
                b.wait(pe, pg_free[pi])
                b.wait(pe, e_wg)
                b.wait(pe, e_ab_dv)
                for d in range(ND):
                    pe.matmul(p[:, 0:GRP],
                              wg1[:, d * G + gt * 128: d * G + gt * 128 + 128],
                              xn2T[:, d * T_own + t0: d * T_own + t0 + GRP],
                              start=(d == 0), stop=(d == ND - 1))
                e_mm = b.inc(pe.last, "pe")
                b.wait(sc, e_mm)
                b.wait(sc, e_bg1)
                e_ev = b.inc(sc.activation(gfm[gt][:, t0:t0 + GRP], p[:, 0:GRP],
                                           AF.Gelu, bias=bg1c[:, gt:gt + 1]), "sc")
                pg_free[pi] = e_ev
        e_gfm = b.ev("sc")
        pl_free = pg_free[0]
        pgt_free = None
        for ci in range(n_own_ch):
            c0 = ci * 128
            b.wait(pe, pl_free)
            b.wait(pe, e_gfm)
            for gt in range(NG):
                pe.matmul(pg[0][:, 256:256 + E], gfm[gt][:, c0:c0 + 128],
                          wg2[:, gt * E:(gt + 1) * E],
                          start=(gt == 0), stop=(gt == NG - 1))
            e_lg = b.inc(pe.last, "pe")
            b.wait(dv, e_lg)
            e_pr = b.inc(dv.tensor_copy(pr_all[:, ci * E:(ci + 1) * E],
                                        pg[0][:, 256:256 + E]), "dv")
            pl_free = e_pr
        e_prall = b.ev("dv")
        seg = lambda t: t[:, 0:LG].rearrange("p (c e) -> p c e", e=E)
        bc = lambda t: t[:, 0:n_own_ch].unsqueeze(-1).broadcast_to(
            [128, n_own_ch, E])
        b.wait(gp, e_prall)

        def seg_fold(src_all, dst16, is_max):
            width = E
            cur = src_all
            while width > 1:
                h = width // 2
                n2 = n_own_ch * h
                s3 = cur[:, 0:n_own_ch * width].rearrange(
                    "p (c e) -> p c e", e=width)
                u3 = u_f[:, 0:n2].rearrange("p (c e) -> p c e", e=h)
                v3_ = v_f[:, 0:n2].rearrange("p (c e) -> p c e", e=h)
                gp.tensor_copy(u3, s3[:, :, 0:h])
                gp.tensor_copy(v3_, s3[:, :, h:width])
                if is_max:
                    gp.tensor_tensor(w_f[:, 0:n2], u_f[:, 0:n2],
                                     v_f[:, 0:n2], op=ALU.subtract)
                    gp.tensor_scalar(u_f[:, 0:n2], w_f[:, 0:n2], 0.0, None,
                                     op0=ALU.is_ge)
                    gp.tensor_tensor(w_f[:, 0:n2], w_f[:, 0:n2],
                                     u_f[:, 0:n2], op=ALU.mult)
                    gp.tensor_tensor(u_f[:, 0:n2], v_f[:, 0:n2],
                                     w_f[:, 0:n2], op=ALU.add)
                else:
                    gp.tensor_tensor(u_f[:, 0:n2], u_f[:, 0:n2],
                                     v_f[:, 0:n2], op=ALU.add)
                cur = u_f
                width = h
            gp.tensor_copy(dst16, u_f[:, 0:n_own_ch])

        seg_fold(pr_all, gmx[:, :], True)
        e_sub = b.inc(gp.tensor_tensor(seg(ex_all), seg(pr_all), bc(gmx),
                                       op=ALU.subtract), "gp")
        b.wait(sc, e_sub)
        e_exp = b.inc(sc.activation(ex_all[:, 0:LG], ex_all[:, 0:LG],
                                    AF.Exp), "sc")
        b.wait(gp, e_exp)
        seg_fold(ex_all, gsm[:, :], False)
        gp.tensor_scalar(mk_all[:, 0:LG], ex_all[:, 0:LG], 1.0, None,
                         op0=ALU.is_ge)
        gp.tensor_tensor(tm_all[:, 0:LG], ex_all[:, 0:LG], mk_all[:, 0:LG],
                         op=ALU.mult)
        gp.tensor_tensor(tm_all[:, 0:LG], ex_all[:, 0:LG], tm_all[:, 0:LG],
                         op=ALU.subtract)
        seg_fold(tm_all, gm2x[:, :], True)
        gp.tensor_tensor(seg(tm_all), seg(tm_all), bc(gm2x),
                         op=ALU.subtract)
        gp.tensor_scalar(tm_all[:, 0:LG], tm_all[:, 0:LG], 0.0, None,
                         op0=ALU.is_ge)
        gp.tensor_tensor(mk_all[:, 0:LG], mk_all[:, 0:LG], tm_all[:, 0:LG],
                         op=ALU.add)
        gp.tensor_tensor(mk_all[:, 0:LG], mk_all[:, 0:LG], ex_all[:, 0:LG],
                         op=ALU.mult)
        gp.memset(zb_f[:, :], 1.0)
        e_zb = b.inc(gp.tensor_tensor(seg(zb_f), seg(zb_f), bc(gsm),
                                      op=ALU.mult), "gp")
        b.wait(dv, e_zb)
        e_rc = b.inc(dv.reciprocal(zb_f[:, 0:LG], zb_f[:, 0:LG]), "dv")
        b.wait(gp, e_rc)
        e_g = b.inc(gp.tensor_tensor(mk_all[:, 0:LG], mk_all[:, 0:LG],
                                     zb_f[:, 0:LG], op=ALU.mult), "gp")
        b.wait(dv, e_g)
        e_gab = b.inc(dv.tensor_copy(ga_all[:, :], mk_all[:, 0:LG]), "dv")
        for ci in range(n_own_ch):
            c0 = ci * 128
            b.wait(pe, e_gab)
            b.wait(pe, pgt_free)
            pe.matmul(pgt[0:E, 0:128], ga_all[:, ci * E:(ci + 1) * E],
                      id_b[:, :], start=True, stop=True)
            e_gt = b.inc(pe.last, "pe")
            b.wait(dv, e_gt)
            e_gc = b.inc(dv.tensor_copy(gatesT[:, c0:c0 + 128],
                                        pgt[0:E, 0:128]), "dv")
            pgt_free = e_gc

        full_barrier()
        gsc.close()
        if not cfg.run_ffn:
            b.wait(gp, b.ev("sa"))
            return nc

        cst = ExitStack()
        al4 = cst.enter_context

        def sb4(name, shape, dtype):
            return al4(nc.sbuf_tensor(name, shape, dtype))

        htil = sb4("htil", [128, NH * TSUB], BF16)
        gbc = [sb4(f"gbc{i}", [128, TSUB], BF16) for i in range(2)]
        ystg = [sb4(f"ystg{i}", [128, TSUB], F32) for i in range(2)]
        ph = [al4(nc.psum_tensor(f"ph{i}", [128, 512], F32)) for i in range(2)]
        py = [al4(nc.psum_tensor(f"py{i}", [128, 512], F32)) for i in range(2)]
        pgb = al4(nc.psum_tensor("pgb", [128, 512], F32))

        ph_free = [None, None]
        py_free = [None, None]
        pgb_free = None
        gbc_free = [None, None]
        ystg_free = [None, None]
        region_last = {}
        htil_guard = None
        win_ready = [e_win, e_win]
        wout_ready = [e_wout, e_wout]
        e_init_writes = b.ev("sa")
        b.wait(gp, e_init_writes)
        e_gates_dv = b.ev("dv")
        ysi = [0]

        for e_i in range(E):
            for tsi in range(n_tsub):
                ts0 = tsi * TSUB
                gi2 = (e_i * n_tsub + tsi) % 2
                b.wait(pe, pgb_free)
                b.wait(pe, e_gates_dv)
                pe.matmul(pgb[:, 0:TSUB], oneh[:, e_i * 128:(e_i + 1) * 128],
                          gatesT[:, ts0:ts0 + TSUB], start=True, stop=True)
                e_gb = b.inc(pe.last, "pe")
                b.wait(dv, e_gb)
                b.wait(dv, gbc_free[gi2])
                e_gbc2 = b.inc(dv.tensor_copy(gbc[gi2][:, :], pgb[:, 0:TSUB]), "dv")
                pgb_free = e_gbc2
                for hc in range(NH):
                    b.wait(pe, win_ready[0 if hc < NH // 2 else 1])
                    phi = hc % 2
                    b.wait(pe, ph_free[phi])
                    for d in range(ND):
                        pe.matmul(ph[phi][:, 0:TSUB],
                                  win[:, d * H + hc * 128: d * H + hc * 128 + 128],
                                  xn2T[:, d * T_own + ts0: d * T_own + ts0 + TSUB],
                                  start=(d == 0), stop=(d == ND - 1))
                    e_mm = b.inc(pe.last, "pe")
                    if tsi == n_tsub - 1 and hc == NH // 2 - 1 and e_i + 1 < E:
                        b.wait(gp, e_mm)
                        win_ready[0] = b.inc(
                            gp.dma_start(
                                win[:, :].rearrange("p (c h) -> p c h", c=ND)[:, :, 0:H // 2],
                                dt_in["w_in"][e_i + 1].rearrange(
                                    "(c p) h -> p c h", p=128)[:, :, 0:H // 2]), "sw", 16)
                    b.wait(sc, e_mm)
                    b.wait(sc, htil_guard)
                    e_gl = b.inc(sc.activation(htil[:, hc * TSUB:(hc + 1) * TSUB],
                                               ph[phi][:, 0:TSUB], AF.Gelu), "sc")
                    ph_free[phi] = e_gl
                    b.wait(dv, e_gl)
                    b.wait(dv, e_gbc2)
                    b.inc(dv.tensor_tensor(htil[:, hc * TSUB:(hc + 1) * TSUB],
                                           htil[:, hc * TSUB:(hc + 1) * TSUB],
                                           gbc[gi2][:, :], op=ALU.mult), "dv")
                gbc_free[gi2] = b.ev("dv")
                e_p1_done = b.ev("pe")
                e_htil_dv = b.ev("dv")
                if tsi == n_tsub - 1 and e_i + 1 < E:
                    b.wait(gp, e_p1_done)
                    win_ready[1] = b.inc(
                        gp.dma_start(
                            win[:, :].rearrange("p (c h) -> p c h", c=ND)[:, :, H // 2:H],
                            dt_in["w_in"][e_i + 1].rearrange(
                                "(c p) h -> p c h", p=128)[:, :, H // 2:H]), "sw2", 16)
                b.wait(pe, e_htil_dv)
                for d in range(ND):
                    b.wait(pe, wout_ready[0 if d < ND // 2 else 1])
                    pyi = d % 2
                    b.wait(pe, py_free[pyi])
                    for hc in range(NH):
                        pe.matmul(py[pyi][:, 0:TSUB],
                                  wout[:, hc * D + d * 128: hc * D + d * 128 + 128],
                                  htil[:, hc * TSUB:(hc + 1) * TSUB],
                                  start=(hc == 0), stop=(hc == NH - 1))
                    e_mm2 = b.inc(pe.last, "pe")
                    if tsi == n_tsub - 1 and d == ND // 2 - 1 and e_i + 1 < E:
                        b.wait(gp, e_mm2)
                        wout_ready[0] = b.inc(
                            gp.dma_start(
                                wout[:, :].rearrange("p (c d) -> p c d", c=NH)[:, :, 0:D // 2],
                                dt_in["w_out"][e_i + 1].rearrange(
                                    "(c p) d -> p c d", p=128)[:, :, 0:D // 2]), "sw3", 16)
                    yi2 = ysi[0]
                    ysi[0] ^= 1
                    b.wait(dv, e_mm2)
                    b.wait(dv, ystg_free[yi2])
                    e_cp = b.inc(dv.tensor_copy(ystg[yi2][:, :], py[pyi][:, 0:TSUB]), "dv")
                    py_free[pyi] = e_cp
                    b.wait(gp, e_cp)
                    b.wait(gp, region_last.get((d, tsi)))
                    e_ad = b.inc(gp.dma_start(
                        y_out[d * 128:(d + 1) * 128, ts0:ts0 + TSUB],
                        ystg[yi2][:, :], accum_op=ALU.add), "sa", 16)
                    region_last[(d, tsi)] = e_ad
                    ystg_free[yi2] = e_ad
                htil_guard = b.ev("pe")
            if e_i + 1 < E:
                e_p2_done = b.ev("pe")
                b.wait(gp, e_p2_done)
                wout_ready[1] = b.inc(
                    gp.dma_start(
                        wout[:, :].rearrange("p (c d) -> p c d", c=NH)[:, :, D // 2:D],
                        dt_in["w_out"][e_i + 1].rearrange(
                            "(c p) d -> p c d", p=128)[:, :, D // 2:D]), "sw4", 16)

        b.wait(gp, b.ev("sa"))
        cst.close()
    return nc


def make_consts():
    ident = np.eye(128, dtype=np.float32)
    ii, jj = np.meshgrid(np.arange(128), np.arange(128), indexing="ij")
    m_bd8 = (((ii // 8) == (jj // 8)) & (ii <= jj)).astype(np.float32)
    m_od8 = (((ii // 16) == (jj // 16)) & ((ii % 16) < 8) & ((jj % 16) >= 8)).astype(np.float32)
    m_od16 = (((ii // 32) == (jj // 32)) & ((ii % 32) < 16) & ((jj % 32) >= 16)).astype(np.float32)
    m_od32 = (((ii // 64) == (jj // 64)) & ((ii % 64) < 32) & ((jj % 64) >= 32)).astype(np.float32)
    submask = np.zeros((128, 2), np.float32)
    submask[0:64, 0] = 1.0
    submask[64:128, 1] = 1.0
    return ident, m_bd8, m_od8, m_od16, m_od32, submask


def make_in_maps(cfg, inputs):
    x = np.ascontiguousarray(np.asarray(inputs["x"], dtype=np.float32)).reshape(-1, cfg.D)
    T = x.shape[0]
    assert T == cfg.n_cores * cfg.T_own, (T, cfg.n_cores, cfg.T_own)
    ident, m_bd8, m_od8, m_od16, m_od32, submask = make_consts()
    D = cfg.D
    oneh = np.zeros((cfg.E, cfg.E * 128), dtype=np.float32)
    for e in range(cfg.E):
        oneh[e, e * 128:(e + 1) * 128] = 1.0
    base = dict(ident=ident, m_bd8=m_bd8, m_od8=m_od8, m_od16=m_od16,
                m_od32=m_od32, submask=submask, onehot=oneh)
    cc = np.zeros((128, 4), dtype=np.float32)
    cc[:, 0] = 1e-5
    cc[:, 2] = 1.0
    base["cconst"] = cc
    base["onesrow"] = np.ones((1, 128), dtype=np.float32)
    base["zer128"] = np.zeros((128, 128), dtype=np.float32)
    base["gammaB"] = np.ascontiguousarray(np.broadcast_to(
        np.asarray(inputs["gamma"], np.float32)[None, :], (128, D)))
    base["betaB"] = np.ascontiguousarray(np.broadcast_to(
        np.asarray(inputs["beta"], np.float32)[None, :], (128, D)))
    base["bg1C"] = np.ascontiguousarray(
        np.asarray(inputs["bg1"], np.float32).reshape(cfg.G // 128, 128).T)
    for nm in ["Wq", "Wk", "Wv", "Wg", "Wo", "wg1", "wg2", "w_in", "w_out",
               "bq", "bk", "bv", "bg", "bo", "gamma", "beta", "bg1"]:
        base[nm] = np.ascontiguousarray(np.asarray(inputs[nm], dtype=np.float32))
    maps = []
    npch = max(cfg.T_pre // cfg.CH, 1)
    for c in range(cfg.n_cores):
        t0 = c * cfg.T_own
        m = dict(base)
        m["x_own"] = np.ascontiguousarray(x[t0:t0 + cfg.T_own])
        if cfg.T_pre:
            xp = np.zeros((cfg.T_pre, cfg.D), dtype=np.float32)
            npad = cfg.T_pre - t0
            if t0 > 0:
                xp[npad:] = x[:t0]
            m["x_pre"] = xp
            pm = np.zeros(cfg.T_pre, dtype=np.float32)
            pm[npad:] = 1.0
            m["pmask"] = np.ascontiguousarray(pm.reshape(npch, cfg.CH).T)
        else:
            m["pmask"] = np.zeros((128, npch), dtype=np.float32)
        maps.append(m)
    return maps


def run(cfg, inputs, runner=None):
    maps = make_in_maps(cfg, inputs)
    nc = build_program(cfg)
    if runner is None:
        res = run_bass_kernel_spmd(nc, maps, core_ids=list(range(cfg.n_cores)))
        results = res.results
    else:
        results = runner(nc, maps)
    import os as _os2
    if _os2.environ.get("K_DBG", ""):
        np.save("/root/problem/_dbg0.npy", np.asarray(results[0]["dbg"]))
        if cfg.n_cores > 1:
            np.save("/root/problem/_dbg1.npy", np.asarray(results[1]["dbg"]))
    parts = [np.asarray(results[c]["y"]).T for c in range(cfg.n_cores)]
    out = np.concatenate(parts, axis=0).astype(np.float32)
    return out.reshape(1, cfg.n_cores * cfg.T_own, cfg.D)


def kernel(L0_attn_x, L0_attn_Wq, L0_attn_bq, L0_attn_Wk, L0_attn_bk,
           L0_attn_Wv, L0_attn_bv, L0_attn_Wg, L0_attn_bg, L0_attn_Wo,
           L0_attn_bo, L0_attn_gamma, L0_attn_beta, L0_ffn_wg1, L0_ffn_bg1,
           L0_ffn_wg2, L0_ffn_w_in, L0_ffn_w_out):
    cfg = CFG()
    inputs = dict(x=L0_attn_x, Wq=L0_attn_Wq, bq=L0_attn_bq, Wk=L0_attn_Wk,
                  bk=L0_attn_bk, Wv=L0_attn_Wv, bv=L0_attn_bv, Wg=L0_attn_Wg,
                  bg=L0_attn_bg, Wo=L0_attn_Wo, bo=L0_attn_bo,
                  gamma=L0_attn_gamma, beta=L0_attn_beta, wg1=L0_ffn_wg1,
                  bg1=L0_ffn_bg1, wg2=L0_ffn_wg2, w_in=L0_ffn_w_in,
                  w_out=L0_ffn_w_out)
    return run(cfg, inputs)


# version 10 (final)
# speedup vs baseline: 1.0908x; 1.0779x over previous
import numpy as np
import concourse.bass as bass
import concourse.mybir as mybir
from concourse.bass_utils import run_bass_kernel_spmd
from contextlib import ExitStack

F32 = mybir.dt.float32
BF16 = mybir.dt.bfloat16
U8 = mybir.dt.uint8
AF = mybir.ActivationFunctionType
ALU = mybir.AluOpType
AX = mybir.AxisListType

EPS = 1e-5


import os as _os


class CFG:
    p_ln = _os.environ.get("P_LN", "1") == "1"
    t_src_idb = _os.environ.get("T_SRC", "x") == "idb"
    t_evac = _os.environ.get("T_EVAC", "1") == "1"
    k_now = _os.environ.get("K_NOW", "0") == "1"
    k_dbg = _os.environ.get("K_DBG", "")
    p_tr = _os.environ.get("P_TR", "1") == "1"
    p_proj = _os.environ.get("P_PROJ", "1") == "1"
    p_v = _os.environ.get("P_V", "1") == "1"
    p_sc = _os.environ.get("P_SC", "1") == "1"
    p_dl = _os.environ.get("P_DL", "1") == "1"
    run_pre = True
    run_own = True
    run_gate = True
    run_ffn = True
    D = 1024
    S = 128
    G = 256
    E = 8
    H = 2048
    T_own = 2048
    T_pre = 14336
    GRP = 512
    CH = 128
    TSUB = 512
    n_cores = 8


class PeTrack:

    def __init__(self, eng):
        object.__setattr__(self, "_e", eng)
        object.__setattr__(self, "last", None)

    def matmul(self, *a, **k):
        r = self._e.matmul(*a, **k)
        object.__setattr__(self, "last", r)
        return r

    def __getattr__(self, n):
        return getattr(self._e, n)


class Bld:
    def __init__(self, nc, stack):
        self.nc = nc
        self.stack = stack
        self.sem = {}
        self.cnt = {}
        self.last_wait = {}
        self.engs = {}

    def new_sem(self, name):
        self.sem[name] = self.stack.enter_context(self.nc.semaphore(name))
        self.cnt[name] = 0

    def inc(self, instr, name, k=1):
        self.cnt[name] += k
        if name in self.engs:
            instr = self.engs[name].drain()
        instr.then_inc(self.sem[name], k)
        return (name, self.cnt[name])

    def ev(self, name):
        return (name, self.cnt[name])

    DMA_SEMS = ("sx", "sw", "sa", "so", "sw2", "sw3", "sw4")

    def wait(self, eng, evt):
        if evt is None:
            return
        if isinstance(evt, (list, tuple)) and evt and isinstance(evt[0], tuple):
            for e in evt:
                self.wait(eng, e)
            return
        name, val = evt
        if name in self.DMA_SEMS:
            val = self.cnt[name]
        if val <= 0:
            return
        assert val <= self.cnt[name], f"forward wait {name} {val} > {self.cnt[name]}"
        key = (id(eng), name)
        if self.last_wait.get(key, -1) >= val:
            return
        eng.wait_ge(self.sem[name], val)
        self.last_wait[key] = val


def build_program(cfg):
    nc = bass.Bass("TRN2", target_bir_lowering=False, detect_race_conditions=False)
    D, S, G, E, H = cfg.D, cfg.S, cfg.G, cfg.E, cfg.H
    ND, NG, NH = D // 128, G // 128, H // 128
    GRP, CH, TSUB = cfg.GRP, cfg.CH, cfg.TSUB
    NCH = GRP // CH
    n_pre_grp = cfg.T_pre // GRP
    n_own_grp = cfg.T_own // GRP
    n_own_ch = cfg.T_own // CH
    n_tsub = cfg.T_own // TSUB
    NPRECH = max(cfg.T_pre // CH, 1)
    T_own = cfg.T_own

    dt_in = {}

    def din(name, shape, dtype=F32):
        dt_in[name] = nc.dram_tensor(name, shape, dtype, kind="ExternalInput").ap()
        return dt_in[name]

    if n_pre_grp:
        din("x_pre", [cfg.T_pre, D])
    din("pmask", [128, NPRECH])
    din("x_own", [T_own, D])
    for nm, shp in [("Wq", [D, S]), ("Wk", [D, S]), ("Wv", [D, S]), ("Wg", [D, S]),
                    ("Wo", [S, D]), ("wg1", [D, G]), ("wg2", [G, E]),
                    ("w_in", [E, D, H]), ("w_out", [E, H, D]),
                    ("bq", [S]), ("bk", [S]), ("bv", [S]), ("bg", [S]),
                    ("bo", [D]), ("gamma", [D]), ("beta", [D]), ("bg1", [G])]:
        din(nm, shp)
    din("gammaB", [128, D])
    din("betaB", [128, D])
    din("bg1C", [128, G // 128])
    din("onehot", [E, E * 128])
    din("ident", [128, 128])
    din("m_bd8", [128, 128])
    din("m_od8", [128, 128])
    din("m_od16", [128, 128])
    din("m_od32", [128, 128])
    din("submask", [128, 2])
    din("cconst", [128, 4])
    din("onesrow", [1, 128])
    din("zer128", [128, 128])
    y_out = nc.dram_tensor("y", [D, T_own], F32, kind="ExternalOutput").ap()
    dbg_out = nc.dram_tensor("dbg", [128, 2048], F32, kind="ExternalOutput").ap()

    with ExitStack() as stack:
        b = Bld(nc, stack)
        al = stack.enter_context

        def sb(name, shape, dtype):
            return al(nc.sbuf_tensor(name, shape, dtype))

        for s_ in ["sx", "sw", "so", "sa", "sw2", "sw3", "sw4", "pe", "dv", "sc", "gp"]:
            b.new_sem(s_)
        pe, dv, sc, gp, sy = PeTrack(nc.tensor), nc.vector, nc.scalar, nc.gpsimd, nc.sync
        b.engs = {"pe": nc.tensor, "dv": nc.vector, "sc": nc.scalar, "gp": nc.gpsimd}

        id_f = sb("id_f", [128, 128], F32)
        id_b = sb("id_b", [128, 128], BF16)
        mbd = sb("mbd", [128, 128], BF16)
        mod8 = sb("mod8", [128, 128], BF16)
        mod_ = sb("mod_", [128, 128], BF16)
        mod32 = sb("mod32", [128, 128], BF16)
        smsk = sb("smsk", [128, 2], F32)
        ones_b = sb("ones_b", [1, 128], BF16)
        ones_f = sb("ones_f", [1, 128], F32)
        zer_b = sb("zer_b", [128, 1], BF16)
        gb_bc = sb("gb_bc", [128, D], F32)
        bb_bc = sb("bb_bc", [128, D], F32)
        xn2T = sb("xn2T", [128, ND * T_own], BF16)
        gatesT = sb("gatesT", [E, T_own], BF16)
        oneh = sb("oneh", [E, E * 128], BF16)
        win = sb("win", [128, ND * H], BF16)
        wout = sb("wout", [128, NH * D], BF16)
        dbg_sb = sb("dbg_sb", [128, 2048 if cfg.k_dbg else 8], F32)
        csp = sb("csp", [128, 32], F32)

        b.inc(sy.dma_start(id_f[:, :], dt_in["ident"]), "sx", 16)
        b.inc(gp.dma_start(mbd[:, :], dt_in["m_bd8"]), "sw", 16)
        b.inc(gp.dma_start(mod8[:, :], dt_in["m_od8"]), "sw", 16)
        b.inc(gp.dma_start(mod_[:, :], dt_in["m_od16"]), "sw", 16)
        b.inc(gp.dma_start(mod32[:, :], dt_in["m_od32"]), "sw", 16)
        b.inc(sy.dma_start(smsk[:, :], dt_in["submask"]), "sx", 16)
        b.inc(sy.dma_start(gb_bc[:, :], dt_in["gammaB"]), "sx", 16)
        e_small0 = b.inc(sy.dma_start(bb_bc[:, :], dt_in["betaB"]), "sx", 16)
        b.inc(gp.dma_start(oneh[:, :], dt_in["onehot"]), "sw", 16)
        e_idb = b.inc(gp.dma_start(id_b[:, :], dt_in["ident"]), "sw", 16)
        if not cfg.k_now:
            e_win = b.inc(gp.dma_start(win[:, :].rearrange("p (c h) -> p c h", c=ND),
                                       dt_in["w_in"][0].rearrange("(c p) h -> p c h", p=128)),
                          "sw", 16)
            e_wout = b.inc(gp.dma_start(wout[:, :].rearrange("p (c d) -> p c d", c=NH),
                                        dt_in["w_out"][0].rearrange("(c p) d -> p c d", p=128)),
                           "sw", 16)
        else:
            e_win = e_wout = None
        cpool = sb("cpool", [128, 4], F32)
        b.inc(sy.dma_start(cpool[:, :], dt_in["cconst"]), "sx", 16)
        b.inc(sy.dma_start(ones_f[:, :], dt_in["onesrow"]), "sx", 16)
        b.inc(gp.dma_start(ones_b[:, :], dt_in["onesrow"]), "sw", 16)
        nc.const_aps.register(F32, EPS, cpool[:, 0:1])

        ab = ExitStack()
        al2 = ab.enter_context

        def sb2(name, shape, dtype):
            return al2(nc.sbuf_tensor(name, shape, dtype))

        def ps2(name, shape, dtype):
            return al2(nc.psum_tensor(name, shape, dtype))

        wq = sb2("wq", [128, ND * S], BF16)
        wk = sb2("wk", [128, ND * S], BF16)
        wv = sb2("wv", [128, ND * S], BF16)
        wgt = sb2("wgt", [128, ND * S], BF16)
        wo = sb2("wo", [S, D], BF16)
        bqc = sb2("bqc", [128, 1], F32)
        bkc = sb2("bkc", [128, 1], F32)
        bgc = sb2("bgc", [128, 1], F32)
        bvrb = sb2("bvrb", [1, S], BF16)
        borb = sb2("borb", [1, D], BF16)
        pmk = sb2("pmk", [128, NPRECH], F32)
        S_f = sb2("S_f", [128, 128], F32)
        S_b = sb2("S_b", [128, 128], BF16)
        NXB = 4
        xc = [sb2(f"xc{i}", [128, D], F32) for i in range(NXB)]
        xnf = sb2("xnf", [128, D], F32)
        stg = [sb2(f"stg{i}", [128, D], F32) for i in range(2)]
        xnbf = sb2("xnbf", [128, D], BF16)
        xnT = sb2("xnT", [128, ND * GRP], BF16)
        qb = sb2("qb", [128, GRP], BF16)
        kb = sb2("kb", [128, GRP], BF16)
        spb = sb2("spb", [128, GRP], F32)
        escr = sb2("escr", [128, GRP], F32)
        s_buf = sb2("s_buf", [128, GRP + 1], F32)
        Eoff = sb2("Eoff", [128, GRP], F32)
        qhat = sb2("qhat", [128, GRP], BF16)
        q8 = sb2("q8", [128, GRP], BF16)
        k8 = sb2("k8", [128, GRP], BF16)
        q16 = sb2("q16", [128, GRP], BF16)
        k16 = sb2("k16", [128, GRP], BF16)
        q32 = sb2("q32", [128, GRP], BF16)
        k32 = sb2("k32", [128, GRP], BF16)
        escr2 = sb2("escr2", [128, GRP], F32)
        khT = sb2("khT", [128, GRP], BF16)
        kh_t = sb2("kh_t", [128, GRP], BF16)
        v_b = sb2("v_b", [128, GRP], BF16)
        A1m = sb2("A1m", [128, 128], BF16)
        A2m = sb2("A2m", [128, 128], BF16)
        A3m = sb2("A3m", [128, 128], BF16)
        A4m = sb2("A4m", [128, 128], BF16)
        q64 = sb2("q64", [128, GRP], BF16)
        k64 = sb2("k64", [128, GRP], BF16)
        x1hi = sb2("x1hi", [128, D], BF16)
        x1lo = sb2("x1lo", [128, D], BF16)
        khm = sb2("khm", [128, 128], BF16)
        outb = sb2("outb", [128, 128], BF16)
        ybuf = sb2("ybuf", [128, D], F32)
        ddump = sb2("ddump", [128, D // 2], F32)
        sdump = ddump
        lnsets = []
        for li in range(2):
            lnsets.append({nm: sb2(f"{nm}_{li}", [128, 1], F32)
                           for nm in ("acc_sa0", "acc_sa1", "acc_qa0",
                                      "acc_qa1", "msqt", "ts_t", "tq_t",
                                      "m2t", "nm2t", "t6t")})
        mu_cd = [sb2(f"mu_cd{i}", [128, 1], F32) for i in range(4)]
        rs_cd = [sb2(f"rs_cd{i}", [128, 1], F32) for i in range(4)]
        mu_sd = sb2("mu_sd", [128, 1], F32)
        rs_sd = sb2("rs_sd", [128, 1], F32)
        t4c = csp[:, 18:19]
        t5c = csp[:, 19:20]
        t6c = csp[:, 20:21]
        ts_c = csp[:, 25:26]
        tq_c = csp[:, 26:27]
        m2_c = csp[:, 27:28]
        nm2_c = csp[:, 28:29]
        decc = csp[:, 21:22]
        mu_s = mu_sd[:, 0:1]
        rs_s = rs_sd[:, 0:1]
        mu_c = [mu_cd[i][:, 0:1] for i in range(NCH)]
        rs_c = [rs_cd[i][:, 0:1] for i in range(NCH)]
        ptf0 = ps2("ptf0", [128, 512], F32)
        ptf1 = ps2("ptf1", [128, 512], F32)
        pp0 = ps2("pp0", [128, 512], F32)
        pp1 = ps2("pp1", [128, 512], F32)
        pA = ps2("pA", [128, 512], F32)
        pdel = ps2("pdel", [128, 512], F32)
        pout = ps2("pout", [128, 512], F32)
        pps = [pp0, pp1]

        b.inc(sy.dma_start(bqc[:, :], dt_in["bq"].unsqueeze(-1)), "sx", 16)
        b.inc(sy.dma_start(bkc[:, :], dt_in["bk"].unsqueeze(-1)), "sx", 16)
        b.inc(sy.dma_start(bgc[:, :], dt_in["bg"].unsqueeze(-1)), "sx", 16)
        b.inc(sy.dma_start(S_f[:, :], dt_in["zer128"]), "sx", 16)
        e_small = b.inc(sy.dma_start(pmk[:, :], dt_in["pmask"]), "sx", 16)
        for w_sb, nm in [(wq, "Wq"), (wk, "Wk"), (wv, "Wv"), (wgt, "Wg")]:
            b.inc(gp.dma_start(w_sb[:, :].rearrange("p (c s) -> p c s", c=ND),
                               dt_in[nm].rearrange("(c p) s -> p c s", p=128)), "sw", 16)
        b.inc(gp.dma_start(wo[:, :], dt_in["Wo"]), "sw", 16)
        b.inc(gp.dma_start(S_b[:, :], dt_in["zer128"]), "sw", 16)
        b.inc(gp.dma_start(bvrb[:, :], dt_in["bv"].unsqueeze(0)), "sw", 16)
        e_smallw = b.inc(gp.dma_start(borb[:, :], dt_in["bo"].unsqueeze(0)), "sw", 16)

        b.wait(sc, e_small)
        b.wait(sc, e_smallw)
        sc.activation(s_buf[:, 0:1], cpool[:, 1:2], AF.Identity)
        e_dvj = b.inc(sc.activation(csp[:, 23:24], bgc[:, 0:1], AF.Identity,
                                    scale=-1.0), "sc")

        e_bc_done = e_dvj

        dbg_done = [False]
        ln_call = [0]
        ln_trigger = int(_os.environ.get("K_LNTRIG", "1"))

        def dump_dbg(name, src_ap, evt=None):
            if cfg.k_dbg != name or dbg_done[0]:
                return
            dbg_done[0] = True
            b.wait(dv, evt)
            n = min(src_ap.shape[-1], 2048)
            e_d = b.inc(dv.tensor_copy(dbg_sb[0:src_ap.shape[0], 0:n], src_ap[:, 0:n]), "dv")
            b.wait(sy, e_d)
            b.inc(sy.dma_start(dbg_out, dbg_sb[:, :]), "so", 16)

        st = {
            "pp_free": [e_bc_done, e_bc_done],
            "pp_i": 0,
            "trbf_free": e_bc_done,
            "pA_free": None, "pdel_free": None, "pout_free": None,
            "S_bf_ready": e_smallw,
            "xbuf_free": [None] * NXB,
            "stg_free": [None, None],
            "xch": 0,
            "ln_guard": [None, None],
        }

        def load_x_chunk(src_ap):
            i = st["xch"] % NXB
            b.wait(sy, st["xbuf_free"][i])
            ev = b.inc(sy.dma_start(xc[i][:, :], src_ap), "sx", 16)
            st["xch"] += 1
            return i, ev

        def group_barrier():
            epe, edv, esc = b.ev("pe"), b.ev("dv"), b.ev("sc")
            b.wait(pe, edv)
            b.wait(pe, esc)
            b.wait(dv, esc)
            b.wait(dv, epe)
            b.wait(sc, edv)
            b.wait(sc, epe)

        def full_barrier():
            group_barrier()
            for eng in (gp, sy):
                for s_ in ("pe", "dv", "sc"):
                    b.wait(eng, b.ev(s_))

        def pp_next():
            i = st["pp_i"]
            st["pp_i"] ^= 1
            return pps[i], i

        def ln_start(xap, e_src=None):
            par = ln_call[0] % 2
            ln_call[0] += 1
            L = lnsets[par]
            b.wait(sc, st["ln_guard"][par])
            b.wait(sc, e_src)
            HD = D // 2
            sc.activation(ddump[:, 0:HD], xap[:, 0:HD], AF.Identity,
                          accum_out=L["acc_sa0"][:, 0:1])
            sc.activation(ddump[:, 0:HD], xap[:, HD:D], AF.Identity,
                          accum_out=L["acc_sa1"][:, 0:1])
            sc.activation(sdump[:, 0:HD], xap[:, 0:HD], AF.Square,
                          accum_out=L["acc_qa0"][:, 0:1])
            sc.activation(sdump[:, 0:HD], xap[:, HD:D], AF.Square,
                          accum_out=L["acc_qa1"][:, 0:1])
            e_sq = b.inc(nc.scalar.drain(), "sc")
            return par, e_sq

        def ln_finish(par, e_sq, mu_ap, rstd_ap):
            L = lnsets[par]
            b.wait(gp, st["ln_guard"][par])
            b.wait(gp, e_sq)
            gp.tensor_add(L["ts_t"][:, 0:1], L["acc_sa0"][:, 0:1],
                          L["acc_sa1"][:, 0:1])
            gp.tensor_scalar_mul(mu_ap, L["ts_t"][:, 0:1], 1.0 / D)
            gp.tensor_add(L["tq_t"][:, 0:1], L["acc_qa0"][:, 0:1],
                          L["acc_qa1"][:, 0:1])
            gp.tensor_scalar_mul(L["msqt"][:, 0:1], L["tq_t"][:, 0:1], 1.0 / D)
            gp.tensor_mul(L["m2t"][:, 0:1], mu_ap, mu_ap)
            e_v = b.inc(gp.tensor_sub(L["nm2t"][:, 0:1], L["msqt"][:, 0:1],
                                      L["m2t"][:, 0:1]), "gp")
            b.wait(sc, e_v)
            sc.activation(ddump[:, 0:1], L["nm2t"][:, 0:1], AF.Ln,
                          bias=EPS, accum_out=L["t6t"][:, 0:1])
            e_r = b.inc(sc.activation(ddump[:, 0:1], L["t6t"][:, 0:1], AF.Exp,
                                      scale=-0.5, accum_out=rstd_ap), "sc")
            st["ln_guard"][par] = e_v
            return e_r

        def ln_stats(xap, mu_ap, rstd_ap, e_src=None):
            par, e_sq = ln_start(xap, e_src)
            e_r = ln_finish(par, e_sq, mu_ap, rstd_ap)
            ln_dbg(xap, mu_ap, rstd_ap, e_r)
            return e_r

        def ln_dbg(xap, mu_ap, rstd_ap, e_r):
            if cfg.k_dbg == "lnx" and ln_call[0] == ln_trigger and not dbg_done[0]:
                dbg_done[0] = True
                b.wait(dv, e_r)
                dv.tensor_copy(dbg_sb[:, 0:1024], xap[:, 0:1024])
                dv.tensor_copy(dbg_sb[:, 1028:1029], mu_ap)
                e_d = b.inc(dv.tensor_copy(dbg_sb[:, 1031:1032], rstd_ap), "dv")
                b.wait(sy, e_d)
                b.inc(sy.dma_start(dbg_out, dbg_sb[:, :]), "so", 16)

        def proj(w_sb, dst_bf, bias_col, e_ready):
            p, pi = pp_next()
            b.wait(pe, st["pp_free"][pi])
            b.wait(pe, e_ready)
            for d in range(ND):
                pe.matmul(p[:, 0:GRP], w_sb[:, d * S:(d + 1) * S],
                          xnT[:, d * GRP:(d + 1) * GRP],
                          start=(d == 0), stop=(d == ND - 1))
            e_mm = b.inc(pe.last, "pe")
            b.wait(sc, e_mm)
            e_ev = b.inc(sc.activation(dst_bf[:, 0:GRP], p[:, 0:GRP],
                                       AF.Identity, bias=bias_col), "sc")
            st["pp_free"][pi] = e_ev
            return e_ev


        def tpsum(d):
            return (ptf0 if d % 2 == 0 else ptf1), (d // 2) * 128

        def khat_to_token_major(e_khT):
            b.wait(pe, e_khT)
            b.wait(pe, st["trbf_free"])
            for ci in range(NCH):
                pe.matmul(tpsum(ci)[0][:, tpsum(ci)[1]:tpsum(ci)[1] + 128],
                          khT[:, ci * 128:(ci + 1) * 128], id_b[:, :],
                          start=True, stop=True)
            e_tp = b.inc(pe.last, "pe")
            b.wait(dv, e_tp)
            for ci in range(NCH - 1):
                dv.tensor_copy(kh_t[:, ci * 128:(ci + 1) * 128],
                               tpsum(ci)[0][:, tpsum(ci)[1]:tpsum(ci)[1] + 128])
            e_c = b.inc(dv.tensor_copy(kh_t[:, (NCH - 1) * 128:NCH * 128],
                                       tpsum(NCH - 1)[0][:, tpsum(NCH - 1)[1]:tpsum(NCH - 1)[1] + 128]), "dv")
            st["trbf_free"] = e_c
            return e_c

        def do_group(own, src, first_tok, gidx_own=None):
            group_barrier()
            ch_bufs = [load_x_chunk(src[first_tok + ci * CH: first_tok + ci * CH + CH, :])
                       for ci in range(NCH)]
            e_tr_all = None
            lnp = [None] * NCH
            if cfg.p_ln:
                bi0, e_x0 = ch_bufs[0]
                b.wait(sc, e_x0)
                lnp[0] = ln_start(xc[bi0][:, :])
            for ci in range(NCH):
                bi, e_x = ch_bufs[ci]
                b.wait(dv, e_x)
                if cfg.p_ln:
                    if ci + 1 < NCH:
                        bi1, e_x1_ = ch_bufs[ci + 1]
                        b.wait(sc, e_x1_)
                        lnp[ci + 1] = ln_start(xc[bi1][:, :])
                    e_r = ln_finish(lnp[ci][0], lnp[ci][1], mu_c[ci],
                                    rs_c[ci])
                    b.wait(dv, e_r)
                    e_ap = b.inc(dv.tensor_scalar(
                        xnbf[:, :], xc[bi][:, :], mu_c[ci],
                        rs_c[ci],
                        op0=ALU.subtract, op1=ALU.mult), "dv")
                else:
                    e_r = None
                    e_ap = b.inc(dv.tensor_copy(xnbf[:, :], xc[bi][:, :]), "dv")
                if not own:
                    st["xbuf_free"][bi] = [e_ap] + ([e_r] if e_r else [])
                if own and gidx_own == 0 and ci == 0:
                    dump_dbg("xnbf", xnbf[:, :], e_ap)
                    dump_dbg("xc", xc[bi][:, :], e_ap)
                    dump_dbg("murst", csp[:, 0:16], e_ap)
                    dump_dbg("idb", id_b[:, :], e_ap)
                if (not own) and ci == 0:
                    dump_dbg("pxnbf", xnbf[:, :], e_ap)
                    dump_dbg("pxc", xc[bi][:, :], e_ap)
                if cfg.p_tr:
                    b.wait(pe, e_ap)
                    b.wait(pe, st["trbf_free"])
                    for d in range(ND):
                        src = id_b[:, :] if cfg.t_src_idb else xnbf[:, d * 128:(d + 1) * 128]
                        pe.matmul(tpsum(d)[0][:, tpsum(d)[1]:tpsum(d)[1] + 128],
                                  src, id_b[:, :],
                                  start=True, stop=True)
                    e_tp = b.inc(pe.last, "pe")
                    b.wait(dv, e_tp)
                    if cfg.t_evac:
                        for d in range(ND - 1):
                            dv.tensor_copy(xnT[:, d * GRP + ci * 128: d * GRP + ci * 128 + 128],
                                           tpsum(d)[0][:, tpsum(d)[1]:tpsum(d)[1] + 128])
                        d = ND - 1
                        e_tc = b.inc(dv.tensor_copy(
                            xnT[:, d * GRP + ci * 128: d * GRP + ci * 128 + 128],
                            tpsum(d)[0][:, tpsum(d)[1]:tpsum(d)[1] + 128]), "dv")
                    else:
                        e_tc = b.inc(dv.nop(), "dv")
                    if own and gidx_own == 0 and ci == 0 and cfg.k_dbg == "ptf":
                        b.wait(dv, e_tp)
                        e_d2 = b.inc(dv.tensor_copy(dbg_sb[:, 0:512], ptf0[:, 0:512]), "dv")
                        e_d2 = b.inc(dv.tensor_copy(dbg_sb[:, 512:1024], ptf1[:, 0:512]), "dv")
                        dbg_done[0] = True
                        b.wait(sy, e_d2)
                        b.inc(sy.dma_start(dbg_out, dbg_sb[:, :]), "so", 16)
                        e_tc = e_d2
                    st["trbf_free"] = e_tc
                    e_tr_all = e_tc
                else:
                    e_tr_all = e_ap

            if cfg.p_proj:
                e_k = proj(wk, kb, bkc[:, 0:1], e_tr_all)
                e_q = proj(wq, qb, bqc[:, 0:1], e_tr_all) if own else None
                p, pi = pp_next()
                b.wait(pe, st["pp_free"][pi])
                b.wait(pe, e_tr_all)
                for d in range(ND):
                    pe.matmul(p[:, 0:GRP], wgt[:, d * S:(d + 1) * S],
                              xnT[:, d * GRP:(d + 1) * GRP],
                              start=(d == 0), stop=(d == ND - 1))
                e_gmm = b.inc(pe.last, "pe")
                b.wait(sc, e_gmm)
                sc.activation(escr[:, 0:GRP], p[:, 0:GRP], AF.Exp, scale=-1.0,
                              bias=csp[:, 23:24])
                e_sp = b.inc(sc.activation(spb[:, 0:GRP], escr[:, 0:GRP],
                                           AF.Ln, bias=1.0), "sc")
                st["pp_free"][pi] = e_sp
            else:
                dv.memset(kb[:, 0:GRP], 0.01)
                dv.memset(qb[:, 0:GRP], 0.01)
                e_sp = e_k = e_q = b.inc(dv.memset(spb[:, 0:GRP], 0.01), "dv")

            e_vev = None
            if cfg.p_v:
                for ci in range(NCH):
                    p, pi = pp_next()
                    b.wait(pe, st["pp_free"][pi])
                    for d in range(ND):
                        pe.matmul(p[:, 0:128],
                                  xnT[:, d * GRP + ci * 128: d * GRP + ci * 128 + 128],
                                  wv[:, d * S:(d + 1) * S], start=(d == 0), stop=False)
                    pe.matmul(p[:, 0:128], ones_b[:, :], bvrb[:, :], start=False, stop=True)
                    e_mm = b.inc(pe.last, "pe")
                    b.wait(dv, e_mm)
                    e_vev = b.inc(dv.tensor_copy(v_b[:, ci * 128:(ci + 1) * 128],
                                                 p[:, 0:128]), "dv")
                    st["pp_free"][pi] = e_vev
            else:
                e_vev = b.inc(dv.memset(v_b[:, 0:GRP], 0.01), "dv")

            if cfg.p_sc:
                b.wait(dv, e_sp)
                e_s = b.inc(dv.tensor_tensor_scan(s_buf[:, 1:GRP + 1], spb[:, 0:GRP],
                                                  spb[:, 0:GRP], 0.0,
                                                  op0=ALU.add, op1=ALU.bypass), "dv")
            else:
                e_s = b.inc(dv.memset(s_buf[:, 1:GRP + 1], 0.02), "dv")

            if not own:
                dump_dbg("pxnT", xnT[:, 0:2048], e_tr_all)
                dump_dbg("pspb", spb[:, 0:GRP], e_sp)
                dump_dbg("psbuf", s_buf[:, :], e_s)
                dump_dbg("pkb", kb[:, 0:GRP], e_k)
                dump_dbg("pvb", v_b[:, 0:GRP], e_vev)
                if cfg.p_sc:
                    b.wait(dv, e_s)
                    e_sub = b.inc(dv.tensor_scalar(escr[:, 0:GRP], s_buf[:, 1:GRP + 1],
                                                   s_buf[:, GRP:GRP + 1], None,
                                                   op0=ALU.subtract), "dv")
                    b.wait(sc, e_sub)
                    e_eh = b.inc(sc.activation(escr[:, 0:GRP], escr[:, 0:GRP], AF.Exp), "sc")
                    b.wait(dv, e_eh)
                    b.wait(dv, e_k)
                    e_kh = b.inc(dv.tensor_tensor(khT[:, 0:GRP], kb[:, 0:GRP],
                                                  escr[:, 0:GRP], op=ALU.mult), "dv")
                    e_khc = khat_to_token_major(e_kh)
                    ch0 = first_tok // CH
                    e_msk = e_khc
                    for ci in range(NCH):
                        e_msk = b.inc(dv.tensor_scalar(kh_t[:, ci * 128:(ci + 1) * 128],
                                                       kh_t[:, ci * 128:(ci + 1) * 128],
                                                       pmk[:, ch0 + ci:ch0 + ci + 1], None,
                                                       op0=ALU.mult), "dv")
                else:
                    e_msk = b.inc(dv.memset(kh_t[:, 0:GRP], 0.01), "dv")
                if cfg.p_dl:
                    b.wait(pe, e_msk)
                    b.wait(pe, e_vev)
                    b.wait(pe, st["pdel_free"])
                    for ci in range(NCH):
                        pe.matmul(pdel[:, 0:128], kh_t[:, ci * 128:(ci + 1) * 128],
                                  v_b[:, ci * 128:(ci + 1) * 128],
                                  start=(ci == 0), stop=(ci == NCH - 1))
                    e_dl = b.inc(pe.last, "pe")
                    b.wait(sc, e_s)
                    e_dc = b.inc(sc.activation(decc, s_buf[:, GRP:GRP + 1],
                                               AF.Exp, scale=-1.0), "sc")
                    b.wait(dv, e_dl)
                    b.wait(dv, e_dc)
                    dv.scalar_tensor_tensor(S_f[:, :], S_f[:, :], decc,
                                            pdel[:, 0:128], op0=ALU.mult, op1=ALU.add)
                    e_sb = b.inc(dv.tensor_copy(S_b[:, :], S_f[:, :]), "dv")
                    st["pdel_free"] = e_sb
                    st["S_bf_ready"] = e_sb
                return

            if gidx_own == 0:
                dump_dbg("S", S_f[:, :], st["S_bf_ready"])
                dump_dbg("qb", qb[:, 0:GRP], e_q)
                dump_dbg("kb", kb[:, 0:GRP], e_k)
                dump_dbg("spb", spb[:, 0:GRP], e_sp)
                dump_dbg("vb", v_b[:, 0:GRP], e_vev)
                dump_dbg("sbuf", s_buf[:, :], e_s)
            NB64, NB32, NB16, NB8 = GRP // 64, GRP // 32, GRP // 16, GRP // 8
            s1 = s_buf[:, 1:GRP + 1]
            s3_64 = s1.rearrange("p (g c) -> p g c", c=64)
            s3_32 = s1.rearrange("p (g c) -> p g c", c=32)
            s3_16 = s1.rearrange("p (g c) -> p g c", c=16)
            s3_8 = s1.rearrange("p (g c) -> p g c", c=8)
            CL = 24.0
            b.wait(dv, e_s)
            bas = s_buf[:, 0:GRP:64].unsqueeze(-1).broadcast_to([128, NB64, 64])
            e_t1 = b.inc(dv.tensor_tensor(
                Eoff[:, 0:GRP].rearrange("p (g c) -> p g c", c=64),
                bas, s3_64, op=ALU.subtract), "dv")
            b.wait(sc, e_t1)
            e_x1e = b.inc(sc.activation(Eoff[:, 0:GRP], Eoff[:, 0:GRP], AF.Exp), "sc")
            mid8 = s_buf[:, 4:4 + (NB8 - 1) * 8 + 1:8].unsqueeze(-1).broadcast_to(
                [128, NB8, 8])
            dv.tensor_tensor(escr[:, 0:GRP].rearrange("p (g c) -> p g c", c=8),
                             mid8, s3_8, op=ALU.subtract)
            e_c8 = b.inc(dv.tensor_scalar(escr[:, 0:GRP], escr[:, 0:GRP], CL, -CL,
                                          op0=ALU.min, op1=ALU.max), "dv")
            b.wait(sc, e_c8)
            sc.activation(q8[:, 0:GRP], escr[:, 0:GRP], AF.Exp)
            e_e8 = b.inc(sc.activation(k8[:, 0:GRP], escr[:, 0:GRP], AF.Exp,
                                       scale=-1.0), "sc")
            mid16 = s_buf[:, 8:8 + (NB16 - 1) * 16 + 1:16].unsqueeze(-1).broadcast_to(
                [128, NB16, 16])
            b.wait(dv, e_e8)
            dv.tensor_tensor(escr[:, 0:GRP].rearrange("p (g c) -> p g c", c=16),
                             mid16, s3_16, op=ALU.subtract)
            dv.tensor_scalar(escr2[:, 0:GRP], escr[:, 0:GRP], 0.0, None, op0=ALU.min)
            e_c16 = b.inc(dv.tensor_scalar(escr[:, 0:GRP], escr[:, 0:GRP], 0.0, None,
                                           op0=ALU.max), "dv")
            b.wait(sc, e_c16)
            sc.activation(q16[:, 0:GRP], escr2[:, 0:GRP], AF.Exp)
            e_e16 = b.inc(sc.activation(k16[:, 0:GRP], escr[:, 0:GRP], AF.Exp,
                                        scale=-1.0), "sc")
            bnd = s_buf[:, 16:16 + (NB32 - 1) * 32 + 1:32].unsqueeze(-1).broadcast_to(
                [128, NB32, 32])
            b.wait(dv, e_e16)
            dv.tensor_tensor(escr[:, 0:GRP].rearrange("p (g c) -> p g c", c=32),
                             bnd, s3_32, op=ALU.subtract)
            dv.tensor_scalar(escr2[:, 0:GRP], escr[:, 0:GRP], 0.0, None, op0=ALU.min)
            e_c32 = b.inc(dv.tensor_scalar(escr[:, 0:GRP], escr[:, 0:GRP], 0.0, None,
                                           op0=ALU.max), "dv")
            b.wait(sc, e_c32)
            sc.activation(q32[:, 0:GRP], escr2[:, 0:GRP], AF.Exp)
            e_e32 = b.inc(sc.activation(k32[:, 0:GRP], escr[:, 0:GRP], AF.Exp,
                                        scale=-1.0), "sc")
            bnd64 = s_buf[:, 32:32 + (NB64 - 1) * 64 + 1:64].unsqueeze(-1).broadcast_to(
                [128, NB64, 64])
            b.wait(dv, e_e32)
            dv.tensor_tensor(escr[:, 0:GRP].rearrange("p (g c) -> p g c", c=64),
                             bnd64, s3_64, op=ALU.subtract)
            dv.tensor_scalar(escr2[:, 0:GRP], escr[:, 0:GRP], 0.0, None, op0=ALU.min)
            e_c64 = b.inc(dv.tensor_scalar(escr[:, 0:GRP], escr[:, 0:GRP], 0.0, None,
                                           op0=ALU.max), "dv")
            b.wait(sc, e_c64)
            sc.activation(q64[:, 0:GRP], escr2[:, 0:GRP], AF.Exp)
            e_e64 = b.inc(sc.activation(k64[:, 0:GRP], escr[:, 0:GRP], AF.Exp,
                                        scale=-1.0), "sc")
            bend = s_buf[:, 64:64 + (NB64 - 1) * 64 + 1:64].unsqueeze(-1).broadcast_to(
                [128, NB64, 64])
            b.wait(dv, e_e64)
            e_t5 = b.inc(dv.tensor_tensor(
                escr[:, 0:GRP].rearrange("p (g c) -> p g c", c=64),
                s3_64, bend, op=ALU.subtract), "dv")
            b.wait(sc, e_t5)
            e_ksb = b.inc(sc.activation(escr[:, 0:GRP], escr[:, 0:GRP], AF.Exp), "sc")
            b.wait(dv, e_k)
            b.wait(dv, e_q)
            b.wait(dv, e_ksb)
            b.wait(dv, e_x1e)
            dv.tensor_tensor(qhat[:, 0:GRP], qb[:, 0:GRP], Eoff[:, 0:GRP], op=ALU.mult)
            dv.tensor_tensor(q8[:, 0:GRP], qb[:, 0:GRP], q8[:, 0:GRP], op=ALU.mult)
            dv.tensor_tensor(k8[:, 0:GRP], kb[:, 0:GRP], k8[:, 0:GRP], op=ALU.mult)
            dv.tensor_tensor(q16[:, 0:GRP], qb[:, 0:GRP], q16[:, 0:GRP], op=ALU.mult)
            dv.tensor_tensor(k16[:, 0:GRP], kb[:, 0:GRP], k16[:, 0:GRP], op=ALU.mult)
            dv.tensor_tensor(q32[:, 0:GRP], qb[:, 0:GRP], q32[:, 0:GRP], op=ALU.mult)
            dv.tensor_tensor(k32[:, 0:GRP], kb[:, 0:GRP], k32[:, 0:GRP], op=ALU.mult)
            dv.tensor_tensor(q64[:, 0:GRP], qb[:, 0:GRP], q64[:, 0:GRP], op=ALU.mult)
            dv.tensor_tensor(k64[:, 0:GRP], kb[:, 0:GRP], k64[:, 0:GRP], op=ALU.mult)
            e_kh = b.inc(dv.tensor_tensor(khT[:, 0:GRP], kb[:, 0:GRP],
                                          escr[:, 0:GRP], op=ALU.mult), "dv")
            e_mults = b.ev("dv")
            omit = _os.environ.get("K_OMIT", "")
            if omit == "inter":
                e_mults = b.inc(dv.memset(qhat[:, 0:GRP], 0.0), "dv")
            elif omit == "A":
                dv.memset(q8[:, 0:GRP], 0.0)
                dv.memset(q16[:, 0:GRP], 0.0)
                dv.memset(q32[:, 0:GRP], 0.0)
                e_mults = b.inc(dv.memset(q64[:, 0:GRP], 0.0), "dv")
            elif omit == "delta":
                e_kh = b.inc(dv.memset(khT[:, 0:GRP], 0.0), "dv")
                e_mults = e_kh
            if gidx_own == 0:
                dump_dbg("qhat", qhat[:, 0:GRP], e_mults)
                dump_dbg("q8", q8[:, 0:GRP], e_mults)
                dump_dbg("k8", k8[:, 0:GRP], e_mults)
                dump_dbg("k64", k64[:, 0:GRP], e_mults)
                dump_dbg("q64", q64[:, 0:GRP], e_mults)
                dump_dbg("khT", khT[:, 0:GRP], e_mults)
                dump_dbg("Eoff", Eoff[:, 0:GRP], e_mults)
            e_khc = khat_to_token_major(e_kh)

            for ci in range(NCH):
                c0 = ci * 128
                b.wait(pe, st["pA_free"])
                b.wait(pe, e_mults)
                specs = [(A1m, mbd, k8, q8), (A2m, mod8, k16, q16),
                         (A3m, mod_, k32, q32), (A4m, mod32, k64, q64)]
                for ai, (Am, msk_t, kt, qt) in enumerate(specs):
                    if ai > 0:
                        b.wait(pe, b.ev("dv"))
                    pe.matmul(pA[:, 128 * ai:128 * ai + 128],
                              kt[:, c0:c0 + 128], qt[:, c0:c0 + 128],
                              start=True, stop=True)
                    e_ai = b.inc(pe.last, "pe")
                    b.wait(dv, e_ai)
                    b.inc(dv.tensor_tensor(Am[:, :], pA[:, 128 * ai:128 * ai + 128],
                                           msk_t[:, :], op=ALU.mult), "dv")
                e_a4m = b.ev("dv")
                st["pA_free"] = e_a4m
                if gidx_own == 0 and ci == 0 and cfg.k_dbg == "Am" and not dbg_done[0]:
                    dbg_done[0] = True
                    b.wait(dv, e_a4m)
                    dv.tensor_copy(dbg_sb[:, 0:128], A1m[:, :])
                    dv.tensor_copy(dbg_sb[:, 128:256], A2m[:, :])
                    dv.tensor_copy(dbg_sb[:, 256:384], A3m[:, :])
                    e_dA = b.inc(dv.tensor_copy(dbg_sb[:, 384:512], A4m[:, :]), "dv")
                    b.wait(sy, e_dA)
                    b.inc(sy.dma_start(dbg_out, dbg_sb[:, :]), "so", 16)
                b.wait(dv, e_khc)
                e_km0 = b.inc(dv.tensor_scalar(khm[:, :], kh_t[:, c0:c0 + 128],
                                               smsk[:, 0:1], None, op0=ALU.mult), "dv")
                b.wait(pe, st["pdel_free"])
                b.wait(pe, e_km0)
                b.wait(pe, e_vev)
                pe.matmul(pdel[:, 0:128], khm[:, :], v_b[:, c0:c0 + 128],
                          start=True, stop=True)
                e_dl0 = b.inc(pe.last, "pe")
                b.wait(dv, e_dl0)
                e_km1 = b.inc(dv.tensor_scalar(khm[:, :], kh_t[:, c0:c0 + 128],
                                               smsk[:, 1:2], None, op0=ALU.mult), "dv")
                b.wait(pe, e_km1)
                pe.matmul(pdel[:, 128:256], khm[:, :], v_b[:, c0:c0 + 128],
                          start=True, stop=True)
                e_dl = b.inc(pe.last, "pe")
                b.wait(pe, st["pout_free"])
                b.wait(pe, e_a4m)
                pe.matmul(pout[:, 0:128], v_b[:, c0:c0 + 128], A1m[:, :],
                          start=True, stop=False)
                pe.matmul(pout[:, 0:128], v_b[:, c0:c0 + 128], A2m[:, :],
                          start=False, stop=False)
                pe.matmul(pout[:, 0:128], v_b[:, c0:c0 + 128], A3m[:, :],
                          start=False, stop=False)
                pe.matmul(pout[:, 0:128], v_b[:, c0:c0 + 128], A4m[:, :],
                          start=False, stop=False)
                for i in range(2):
                    b.wait(pe, st["S_bf_ready"])
                    pe.matmul(pout[:, 64 * i:64 * i + 64], S_b[:, :],
                              qhat[:, c0 + 64 * i: c0 + 64 * i + 64],
                              start=False, stop=(i == 1))
                    e_int = b.inc(pe.last, "pe")
                    b.wait(dv, e_dl if i == 1 else e_dl0)
                    b.wait(dv, e_int)
                    cdec = c0 + 64 * i + 63
                    dv.scalar_tensor_tensor(S_f[:, :], S_f[:, :],
                                            Eoff[:, cdec:cdec + 1],
                                            pdel[:, 128 * i:128 * i + 128],
                                            op0=ALU.mult, op1=ALU.add)
                    e_sb2 = b.inc(dv.tensor_copy(S_b[:, :], S_f[:, :]), "dv")
                    st["S_bf_ready"] = e_sb2
                st["pdel_free"] = b.ev("dv")
                e_out_mm = b.ev("pe")
                b.wait(dv, e_out_mm)
                e_ob = b.inc(dv.tensor_copy(outb[:, :], pout[:, 0:128]), "dv")
                st["pout_free"] = e_ob
                b.wait(pe, st["pp_free"][0])
                b.wait(pe, st["pp_free"][1])
                b.wait(pe, e_ob)
                pe.matmul(pp0[:, 0:512], outb[:, :], wo[:, 0:512], start=True, stop=False)
                pe.matmul(pp0[:, 0:512], ones_b[:, :], borb[:, 0:512], start=False, stop=True)
                pe.matmul(pp1[:, 0:512], outb[:, :], wo[:, 512:1024], start=True, stop=False)
                pe.matmul(pp1[:, 0:512], ones_b[:, :], borb[:, 512:1024], start=False, stop=True)
                e_y = b.inc(pe.last, "pe")
                bi, _ = ch_bufs[ci]
                if gidx_own == 0 and ci == 0:
                    dump_dbg("outb", outb[:, :], e_ob)
                b.wait(dv, e_y)
                e_xn = b.inc(dv.tensor_scalar(
                    xnf[:, :], xc[bi][:, :], mu_c[ci], rs_c[ci],
                    op0=ALU.subtract, op1=ALU.mult), "dv")
                dv.tensor_tensor(ybuf[:, 0:512], xnf[:, 0:512], pp0[:, 0:512], op=ALU.add)
                e_yb = b.inc(dv.tensor_tensor(ybuf[:, 512:1024], xnf[:, 512:1024],
                                              pp1[:, 0:512], op=ALU.add), "dv")
                st["pp_free"][0] = e_yb
                st["pp_free"][1] = e_yb
                if gidx_own == 0 and ci == 0:
                    dump_dbg("ybuf", ybuf[:, :], e_yb)
                e_r3 = ln_stats(ybuf[:, :], mu_s, rs_s, e_src=e_yb)
                b.wait(dv, e_r3)
                dv.tensor_scalar(ybuf[:, :], ybuf[:, :], mu_s,
                                 rs_s, op0=ALU.subtract, op1=ALU.mult)
                dv.tensor_tensor(ybuf[:, :], ybuf[:, :], gb_bc[:, :], op=ALU.mult)
                dv.tensor_tensor(xc[bi][:, :], xc[bi][:, :], bb_bc[:, :], op=ALU.add)
                e_x1 = b.inc(dv.tensor_tensor(xc[bi][:, :], xc[bi][:, :], ybuf[:, :],
                                              op=ALU.add), "dv")
                e_r4 = ln_stats(xc[bi][:, :], mu_s, rs_s, e_src=e_x1)
                b.wait(dv, e_r4)
                e_xn2 = b.inc(dv.tensor_scalar(
                    xnbf[:, :], xc[bi][:, :], mu_s, rs_s,
                    op0=ALU.subtract, op1=ALU.mult), "dv")
                gci = gidx_own * NCH + ci
                if gci == 0:
                    dump_dbg("x1", xc[bi][:, :], e_x1)
                b.wait(dv, e_x1)
                dv.tensor_copy(x1hi[:, :], xc[bi][:, :])
                e_hl = b.inc(dv.tensor_tensor(x1lo[:, :], xc[bi][:, :], x1hi[:, :],
                                              op=ALU.subtract), "dv")
                b.wait(pe, e_hl)
                b.wait(pe, st["trbf_free"])
                for d in range(ND):
                    pe.matmul(tpsum(d)[0][:, tpsum(d)[1]:tpsum(d)[1] + 128],
                              x1hi[:, d * 128:(d + 1) * 128], id_b[:, :],
                              start=True, stop=True)
                e_tp1a = b.inc(pe.last, "pe")
                sgi = gci % 2
                stgA = stg[sgi][:, :].rearrange("p (d two t) -> p two d t", two=2, t=128)
                b.wait(dv, e_tp1a)
                b.wait(dv, st["stg_free"][sgi])
                dv.tensor_copy(stgA[:, 0, :, :],
                               ptf0[:, 0:512].rearrange("p (d t) -> p d t", t=128))
                e_cp1 = b.inc(dv.tensor_copy(
                    stgA[:, 1, :, :],
                    ptf1[:, 0:512].rearrange("p (d t) -> p d t", t=128)), "dv")
                b.wait(pe, e_cp1)
                for d in range(ND):
                    pe.matmul(tpsum(d)[0][:, tpsum(d)[1]:tpsum(d)[1] + 128],
                              x1lo[:, d * 128:(d + 1) * 128], id_b[:, :],
                              start=True, stop=True)
                e_tp1 = b.inc(pe.last, "pe")
                b.wait(dv, e_tp1)
                dv.tensor_tensor(stgA[:, 0, :, :], stgA[:, 0, :, :],
                                 ptf0[:, 0:512].rearrange("p (d t) -> p d t", t=128),
                                 op=ALU.add)
                e_oi = b.inc(dv.tensor_tensor(
                    stgA[:, 1, :, :], stgA[:, 1, :, :],
                    ptf1[:, 0:512].rearrange("p (d t) -> p d t", t=128),
                    op=ALU.add), "dv")
                st["trbf_free"] = e_oi
                if gci == 0:
                    dump_dbg("stg", stg[sgi][:, :], e_oi)
                b.wait(gp, e_oi)
                e_ow = b.inc(gp.dma_start(
                    y_out.rearrange("(c p) t -> p c t", p=128)[:, :, gci * 128:(gci + 1) * 128],
                    stg[sgi][:, :].rearrange("p (c t) -> p c t", c=ND)), "sa", 16)
                st["stg_free"][sgi] = e_ow
                st["xbuf_free"][bi] = [e_oi, e_tp1, e_r4]
                b.wait(pe, e_xn2)
                b.wait(pe, st["trbf_free"])
                for d in range(ND):
                    pe.matmul(tpsum(d)[0][:, tpsum(d)[1]:tpsum(d)[1] + 128],
                              xnbf[:, d * 128:(d + 1) * 128], id_b[:, :],
                              start=True, stop=True)
                e_tp2 = b.inc(pe.last, "pe")
                b.wait(dv, e_tp2)
                for d in range(ND - 1):
                    dv.tensor_copy(xn2T[:, d * T_own + gci * 128: d * T_own + gci * 128 + 128],
                                   tpsum(d)[0][:, tpsum(d)[1]:tpsum(d)[1] + 128])
                d = ND - 1
                e_x2 = b.inc(dv.tensor_copy(
                    xn2T[:, d * T_own + gci * 128: d * T_own + gci * 128 + 128],
                    tpsum(d)[0][:, tpsum(d)[1]:tpsum(d)[1] + 128]), "dv")
                st["trbf_free"] = e_x2

        b.wait(pe, e_small0)
        b.wait(pe, e_idb)
        b.wait(pe, e_smallw)
        if cfg.run_pre:
            for gi in range(n_pre_grp):
                do_group(False, dt_in["x_pre"], gi * GRP)
        if cfg.run_own:
            for gi in range(n_own_grp):
                do_group(True, dt_in["x_own"], gi * GRP, gidx_own=gi)

        e_ab_dv = b.ev("dv")
        full_barrier()
        ab.close()
        if not (cfg.run_gate and cfg.run_own):
            b.wait(gp, b.ev("sa"))
            b.wait(gp, b.ev("sw"))
            b.wait(sy, b.ev("sx"))
            if cfg.k_dbg == "yback":
                e_yb2 = b.inc(gp.dma_start(dbg_sb[:, 0:T_own],
                                           y_out[0:128, :]), "sw", 16)
                b.wait(sy, e_yb2)
                b.inc(sy.dma_start(dbg_out, dbg_sb[:, :]), "so", 16)
                b.wait(sy, b.ev("so"))
            return nc

        gsc = ExitStack()
        al3 = gsc.enter_context

        def sb3(name, shape, dtype):
            return al3(nc.sbuf_tensor(name, shape, dtype))

        wg1 = sb3("wg1_sb", [128, ND * G], BF16)
        wg2 = sb3("wg2_sb", [128, NG * E], BF16)
        bg1c = sb3("bg1c", [128, NG], F32)
        gfm = [sb3(f"gfm{i}", [128, T_own], BF16) for i in range(NG)]
        LG = n_own_ch * E
        pr_all = sb3("pr_all", [128, LG], F32)
        ex_all = sb3("ex_all", [128, LG], F32)
        mk_all = sb3("mk_all", [128, LG], F32)
        tm_all = sb3("tm_all", [128, LG], F32)
        ga_all = sb3("ga_all", [128, LG], BF16)
        gmx = sb3("gmx", [128, n_own_ch], F32)
        gsm = sb3("gsm", [128, n_own_ch], F32)
        gm2x = sb3("gm2x", [128, n_own_ch], F32)
        u_f = sb3("u_f", [128, LG // 2], F32)
        v_f = sb3("v_f", [128, LG // 2], F32)
        w_f = sb3("w_f", [128, LG // 2], F32)
        zb_f = sb3("zb_f", [128, LG], F32)
        pg = [al3(nc.psum_tensor(f"pg{i}", [128, 512], F32)) for i in range(2)]
        pgt = al3(nc.psum_tensor("pgt", [128, 512], F32))

        e_bg1 = b.inc(sy.dma_start(bg1c[:, :], dt_in["bg1C"]),
                      "sx", 16)
        b.inc(gp.dma_start(wg1[:, :].rearrange("p (c g) -> p c g", c=ND),
                           dt_in["wg1"].rearrange("(c p) g -> p c g", p=128)), "sw", 16)
        e_wg = b.inc(gp.dma_start(wg2[:, :].rearrange("p (c e) -> p c e", c=NG),
                                  dt_in["wg2"].rearrange("(c p) e -> p c e", p=128)),
                     "sw", 16)
        pg_free = [None, None]
        pgi = [0]

        def pg_next():
            i = pgi[0]
            pgi[0] ^= 1
            return pg[i], i

        for gi in range(n_own_grp):
            t0 = gi * GRP
            for gt in range(NG):
                p, pi = pg_next()
                b.wait(pe, pg_free[pi])
                b.wait(pe, e_wg)
                b.wait(pe, e_ab_dv)
                for d in range(ND):
                    pe.matmul(p[:, 0:GRP],
                              wg1[:, d * G + gt * 128: d * G + gt * 128 + 128],
                              xn2T[:, d * T_own + t0: d * T_own + t0 + GRP],
                              start=(d == 0), stop=(d == ND - 1))
                e_mm = b.inc(pe.last, "pe")
                b.wait(sc, e_mm)
                b.wait(sc, e_bg1)
                e_ev = b.inc(sc.activation(gfm[gt][:, t0:t0 + GRP], p[:, 0:GRP],
                                           AF.Gelu, bias=bg1c[:, gt:gt + 1]), "sc")
                pg_free[pi] = e_ev
        e_gfm = b.ev("sc")
        pl_free = pg_free[0]
        pgt_free = None
        for ci in range(n_own_ch):
            c0 = ci * 128
            b.wait(pe, pl_free)
            b.wait(pe, e_gfm)
            for gt in range(NG):
                pe.matmul(pg[0][:, 256:256 + E], gfm[gt][:, c0:c0 + 128],
                          wg2[:, gt * E:(gt + 1) * E],
                          start=(gt == 0), stop=(gt == NG - 1))
            e_lg = b.inc(pe.last, "pe")
            b.wait(dv, e_lg)
            e_pr = b.inc(dv.tensor_copy(pr_all[:, ci * E:(ci + 1) * E],
                                        pg[0][:, 256:256 + E]), "dv")
            pl_free = e_pr
        e_prall = b.ev("dv")
        seg = lambda t: t[:, 0:LG].rearrange("p (c e) -> p c e", e=E)
        bc = lambda t: t[:, 0:n_own_ch].unsqueeze(-1).broadcast_to(
            [128, n_own_ch, E])
        b.wait(gp, e_prall)

        def seg_fold(src_all, dst16, is_max):
            width = E
            cur = src_all
            while width > 1:
                h = width // 2
                n2 = n_own_ch * h
                s3 = cur[:, 0:n_own_ch * width].rearrange(
                    "p (c e) -> p c e", e=width)
                u3 = u_f[:, 0:n2].rearrange("p (c e) -> p c e", e=h)
                v3_ = v_f[:, 0:n2].rearrange("p (c e) -> p c e", e=h)
                gp.tensor_copy(u3, s3[:, :, 0:h])
                gp.tensor_copy(v3_, s3[:, :, h:width])
                if is_max:
                    gp.tensor_tensor(w_f[:, 0:n2], u_f[:, 0:n2],
                                     v_f[:, 0:n2], op=ALU.subtract)
                    gp.tensor_scalar(u_f[:, 0:n2], w_f[:, 0:n2], 0.0, None,
                                     op0=ALU.is_ge)
                    gp.tensor_tensor(w_f[:, 0:n2], w_f[:, 0:n2],
                                     u_f[:, 0:n2], op=ALU.mult)
                    gp.tensor_tensor(u_f[:, 0:n2], v_f[:, 0:n2],
                                     w_f[:, 0:n2], op=ALU.add)
                else:
                    gp.tensor_tensor(u_f[:, 0:n2], u_f[:, 0:n2],
                                     v_f[:, 0:n2], op=ALU.add)
                cur = u_f
                width = h
            gp.tensor_copy(dst16, u_f[:, 0:n_own_ch])

        seg_fold(pr_all, gmx[:, :], True)
        e_sub = b.inc(gp.tensor_tensor(seg(ex_all), seg(pr_all), bc(gmx),
                                       op=ALU.subtract), "gp")
        b.wait(sc, e_sub)
        e_exp = b.inc(sc.activation(ex_all[:, 0:LG], ex_all[:, 0:LG],
                                    AF.Exp), "sc")
        b.wait(gp, e_exp)
        seg_fold(ex_all, gsm[:, :], False)
        gp.tensor_scalar(mk_all[:, 0:LG], ex_all[:, 0:LG], 1.0, None,
                         op0=ALU.is_ge)
        gp.tensor_tensor(tm_all[:, 0:LG], ex_all[:, 0:LG], mk_all[:, 0:LG],
                         op=ALU.mult)
        gp.tensor_tensor(tm_all[:, 0:LG], ex_all[:, 0:LG], tm_all[:, 0:LG],
                         op=ALU.subtract)
        seg_fold(tm_all, gm2x[:, :], True)
        gp.tensor_tensor(seg(tm_all), seg(tm_all), bc(gm2x),
                         op=ALU.subtract)
        gp.tensor_scalar(tm_all[:, 0:LG], tm_all[:, 0:LG], 0.0, None,
                         op0=ALU.is_ge)
        gp.tensor_tensor(mk_all[:, 0:LG], mk_all[:, 0:LG], tm_all[:, 0:LG],
                         op=ALU.add)
        gp.tensor_tensor(mk_all[:, 0:LG], mk_all[:, 0:LG], ex_all[:, 0:LG],
                         op=ALU.mult)
        gp.memset(zb_f[:, :], 1.0)
        e_zb = b.inc(gp.tensor_tensor(seg(zb_f), seg(zb_f), bc(gsm),
                                      op=ALU.mult), "gp")
        b.wait(dv, e_zb)
        e_rc = b.inc(dv.reciprocal(zb_f[:, 0:LG], zb_f[:, 0:LG]), "dv")
        b.wait(gp, e_rc)
        e_g = b.inc(gp.tensor_tensor(mk_all[:, 0:LG], mk_all[:, 0:LG],
                                     zb_f[:, 0:LG], op=ALU.mult), "gp")
        b.wait(dv, e_g)
        e_gab = b.inc(dv.tensor_copy(ga_all[:, :], mk_all[:, 0:LG]), "dv")
        for ci in range(n_own_ch):
            c0 = ci * 128
            b.wait(pe, e_gab)
            b.wait(pe, pgt_free)
            pe.matmul(pgt[0:E, 0:128], ga_all[:, ci * E:(ci + 1) * E],
                      id_b[:, :], start=True, stop=True)
            e_gt = b.inc(pe.last, "pe")
            b.wait(dv, e_gt)
            e_gc = b.inc(dv.tensor_copy(gatesT[:, c0:c0 + 128],
                                        pgt[0:E, 0:128]), "dv")
            pgt_free = e_gc

        full_barrier()
        gsc.close()
        if not cfg.run_ffn:
            b.wait(gp, b.ev("sa"))
            return nc

        cst = ExitStack()
        al4 = cst.enter_context

        def sb4(name, shape, dtype):
            return al4(nc.sbuf_tensor(name, shape, dtype))

        htil = sb4("htil", [128, NH * TSUB], BF16)
        gbc = [sb4(f"gbc{i}", [128, TSUB], BF16) for i in range(2)]
        ystg = [sb4(f"ystg{i}", [128, TSUB], F32) for i in range(3)]
        ph = [al4(nc.psum_tensor(f"ph{i}", [128, 512], F32)) for i in range(2)]
        py = [al4(nc.psum_tensor(f"py{i}", [128, 512], F32)) for i in range(3)]
        pgb = al4(nc.psum_tensor("pgb", [128, 512], F32))

        ph_free = [None, None]
        py_free = [None, None, None]
        pgb_free = None
        gbc_free = [None, None]
        ystg_free = [None, None, None]
        region_last = {}
        htil_guard = None
        win_ready = [e_win, e_win]
        wout_ready = [e_wout, e_wout]
        e_init_writes = b.ev("sa")
        b.wait(gp, e_init_writes)
        e_gates_dv = b.ev("dv")
        ysi = [0]

        for e_i in range(E):
            for tsi in range(n_tsub):
                ts0 = tsi * TSUB
                gi2 = (e_i * n_tsub + tsi) % 2
                b.wait(pe, pgb_free)
                b.wait(pe, e_gates_dv)
                pe.matmul(pgb[:, 0:TSUB], oneh[:, e_i * 128:(e_i + 1) * 128],
                          gatesT[:, ts0:ts0 + TSUB], start=True, stop=True)
                e_gb = b.inc(pe.last, "pe")
                b.wait(dv, e_gb)
                b.wait(dv, gbc_free[gi2])
                e_gbc2 = b.inc(dv.tensor_copy(gbc[gi2][:, :], pgb[:, 0:TSUB]), "dv")
                pgb_free = e_gbc2
                for hc in range(NH):
                    b.wait(pe, win_ready[0 if hc < NH // 2 else 1])
                    phi = hc % 2
                    b.wait(pe, ph_free[phi])
                    for d in range(ND):
                        pe.matmul(ph[phi][:, 0:TSUB],
                                  win[:, d * H + hc * 128: d * H + hc * 128 + 128],
                                  xn2T[:, d * T_own + ts0: d * T_own + ts0 + TSUB],
                                  start=(d == 0), stop=(d == ND - 1))
                    e_mm = b.inc(pe.last, "pe")
                    if tsi == n_tsub - 1 and hc == NH // 2 - 1 and e_i + 1 < E:
                        b.wait(gp, e_mm)
                        win_ready[0] = b.inc(
                            gp.dma_start(
                                win[:, :].rearrange("p (c h) -> p c h", c=ND)[:, :, 0:H // 2],
                                dt_in["w_in"][e_i + 1].rearrange(
                                    "(c p) h -> p c h", p=128)[:, :, 0:H // 2]), "sw", 16)
                    b.wait(sc, e_mm)
                    b.wait(sc, htil_guard)
                    e_gl = b.inc(sc.activation(htil[:, hc * TSUB:(hc + 1) * TSUB],
                                               ph[phi][:, 0:TSUB], AF.Gelu), "sc")
                    ph_free[phi] = e_gl
                    b.wait(dv, e_gl)
                    b.wait(dv, e_gbc2)
                    b.inc(dv.tensor_tensor(htil[:, hc * TSUB:(hc + 1) * TSUB],
                                           htil[:, hc * TSUB:(hc + 1) * TSUB],
                                           gbc[gi2][:, :], op=ALU.mult), "dv")
                gbc_free[gi2] = b.ev("dv")
                e_p1_done = b.ev("pe")
                e_htil_dv = b.ev("dv")
                if tsi == n_tsub - 1 and e_i + 1 < E:
                    b.wait(gp, e_p1_done)
                    win_ready[1] = b.inc(
                        gp.dma_start(
                            win[:, :].rearrange("p (c h) -> p c h", c=ND)[:, :, H // 2:H],
                            dt_in["w_in"][e_i + 1].rearrange(
                                "(c p) h -> p c h", p=128)[:, :, H // 2:H]), "sw2", 16)
                b.wait(pe, e_htil_dv)
                for d in range(ND):
                    b.wait(pe, wout_ready[0 if d < ND // 2 else 1])
                    pyi = d % 3
                    b.wait(pe, py_free[pyi])
                    for hc in range(NH):
                        pe.matmul(py[pyi][:, 0:TSUB],
                                  wout[:, hc * D + d * 128: hc * D + d * 128 + 128],
                                  htil[:, hc * TSUB:(hc + 1) * TSUB],
                                  start=(hc == 0), stop=(hc == NH - 1))
                    e_mm2 = b.inc(pe.last, "pe")
                    if tsi == n_tsub - 1 and d == ND // 2 - 1 and e_i + 1 < E:
                        b.wait(gp, e_mm2)
                        wout_ready[0] = b.inc(
                            gp.dma_start(
                                wout[:, :].rearrange("p (c d) -> p c d", c=NH)[:, :, 0:D // 2],
                                dt_in["w_out"][e_i + 1].rearrange(
                                    "(c p) d -> p c d", p=128)[:, :, 0:D // 2]), "sw3", 16)
                    yi2 = ysi[0]
                    ysi[0] = (ysi[0] + 1) % 3
                    b.wait(dv, e_mm2)
                    b.wait(dv, ystg_free[yi2])
                    e_cp = b.inc(dv.tensor_copy(ystg[yi2][:, :], py[pyi][:, 0:TSUB]), "dv")
                    py_free[pyi] = e_cp
                    b.wait(gp, e_cp)
                    b.wait(gp, region_last.get((d, tsi)))
                    e_ad = b.inc(gp.dma_start(
                        y_out[d * 128:(d + 1) * 128, ts0:ts0 + TSUB],
                        ystg[yi2][:, :], accum_op=ALU.add), "sa", 16)
                    region_last[(d, tsi)] = e_ad
                    ystg_free[yi2] = e_ad
                htil_guard = b.ev("pe")
            if e_i + 1 < E:
                e_p2_done = b.ev("pe")
                b.wait(gp, e_p2_done)
                wout_ready[1] = b.inc(
                    gp.dma_start(
                        wout[:, :].rearrange("p (c d) -> p c d", c=NH)[:, :, D // 2:D],
                        dt_in["w_out"][e_i + 1].rearrange(
                            "(c p) d -> p c d", p=128)[:, :, D // 2:D]), "sw4", 16)

        b.wait(gp, b.ev("sa"))
        cst.close()
    return nc


def make_consts():
    ident = np.eye(128, dtype=np.float32)
    ii, jj = np.meshgrid(np.arange(128), np.arange(128), indexing="ij")
    m_bd8 = (((ii // 8) == (jj // 8)) & (ii <= jj)).astype(np.float32)
    m_od8 = (((ii // 16) == (jj // 16)) & ((ii % 16) < 8) & ((jj % 16) >= 8)).astype(np.float32)
    m_od16 = (((ii // 32) == (jj // 32)) & ((ii % 32) < 16) & ((jj % 32) >= 16)).astype(np.float32)
    m_od32 = (((ii // 64) == (jj // 64)) & ((ii % 64) < 32) & ((jj % 64) >= 32)).astype(np.float32)
    submask = np.zeros((128, 2), np.float32)
    submask[0:64, 0] = 1.0
    submask[64:128, 1] = 1.0
    return ident, m_bd8, m_od8, m_od16, m_od32, submask


def make_in_maps(cfg, inputs):
    x = np.ascontiguousarray(np.asarray(inputs["x"], dtype=np.float32)).reshape(-1, cfg.D)
    T = x.shape[0]
    assert T == cfg.n_cores * cfg.T_own, (T, cfg.n_cores, cfg.T_own)
    ident, m_bd8, m_od8, m_od16, m_od32, submask = make_consts()
    D = cfg.D
    oneh = np.zeros((cfg.E, cfg.E * 128), dtype=np.float32)
    for e in range(cfg.E):
        oneh[e, e * 128:(e + 1) * 128] = 1.0
    base = dict(ident=ident, m_bd8=m_bd8, m_od8=m_od8, m_od16=m_od16,
                m_od32=m_od32, submask=submask, onehot=oneh)
    cc = np.zeros((128, 4), dtype=np.float32)
    cc[:, 0] = 1e-5
    cc[:, 2] = 1.0
    base["cconst"] = cc
    base["onesrow"] = np.ones((1, 128), dtype=np.float32)
    base["zer128"] = np.zeros((128, 128), dtype=np.float32)
    base["gammaB"] = np.ascontiguousarray(np.broadcast_to(
        np.asarray(inputs["gamma"], np.float32)[None, :], (128, D)))
    base["betaB"] = np.ascontiguousarray(np.broadcast_to(
        np.asarray(inputs["beta"], np.float32)[None, :], (128, D)))
    base["bg1C"] = np.ascontiguousarray(
        np.asarray(inputs["bg1"], np.float32).reshape(cfg.G // 128, 128).T)
    for nm in ["Wq", "Wk", "Wv", "Wg", "Wo", "wg1", "wg2", "w_in", "w_out",
               "bq", "bk", "bv", "bg", "bo", "gamma", "beta", "bg1"]:
        base[nm] = np.ascontiguousarray(np.asarray(inputs[nm], dtype=np.float32))
    maps = []
    npch = max(cfg.T_pre // cfg.CH, 1)
    for c in range(cfg.n_cores):
        t0 = c * cfg.T_own
        m = dict(base)
        m["x_own"] = np.ascontiguousarray(x[t0:t0 + cfg.T_own])
        if cfg.T_pre:
            xp = np.zeros((cfg.T_pre, cfg.D), dtype=np.float32)
            npad = cfg.T_pre - t0
            if t0 > 0:
                xp[npad:] = x[:t0]
            m["x_pre"] = xp
            pm = np.zeros(cfg.T_pre, dtype=np.float32)
            pm[npad:] = 1.0
            m["pmask"] = np.ascontiguousarray(pm.reshape(npch, cfg.CH).T)
        else:
            m["pmask"] = np.zeros((128, npch), dtype=np.float32)
        maps.append(m)
    return maps


def run(cfg, inputs, runner=None):
    maps = make_in_maps(cfg, inputs)
    nc = build_program(cfg)
    if runner is None:
        res = run_bass_kernel_spmd(nc, maps, core_ids=list(range(cfg.n_cores)))
        results = res.results
    else:
        results = runner(nc, maps)
    import os as _os2
    if _os2.environ.get("K_DBG", ""):
        np.save("/root/problem/_dbg0.npy", np.asarray(results[0]["dbg"]))
        if cfg.n_cores > 1:
            np.save("/root/problem/_dbg1.npy", np.asarray(results[1]["dbg"]))
    parts = [np.asarray(results[c]["y"]).T for c in range(cfg.n_cores)]
    out = np.concatenate(parts, axis=0).astype(np.float32)
    return out.reshape(1, cfg.n_cores * cfg.T_own, cfg.D)


def kernel(L0_attn_x, L0_attn_Wq, L0_attn_bq, L0_attn_Wk, L0_attn_bk,
           L0_attn_Wv, L0_attn_bv, L0_attn_Wg, L0_attn_bg, L0_attn_Wo,
           L0_attn_bo, L0_attn_gamma, L0_attn_beta, L0_ffn_wg1, L0_ffn_bg1,
           L0_ffn_wg2, L0_ffn_w_in, L0_ffn_w_out):
    cfg = CFG()
    inputs = dict(x=L0_attn_x, Wq=L0_attn_Wq, bq=L0_attn_bq, Wk=L0_attn_Wk,
                  bk=L0_attn_bk, Wv=L0_attn_Wv, bv=L0_attn_bv, Wg=L0_attn_Wg,
                  bg=L0_attn_bg, Wo=L0_attn_Wo, bo=L0_attn_bo,
                  gamma=L0_attn_gamma, beta=L0_attn_beta, wg1=L0_ffn_wg1,
                  bg1=L0_ffn_bg1, wg2=L0_ffn_wg2, w_in=L0_ffn_w_in,
                  w_out=L0_ffn_w_out)
    return run(cfg, inputs)
